# version 01
# baseline (speedup 1.0000x reference)
import math
import numpy as np
import ml_dtypes
import concourse.bass as bass
import concourse.mybir as mybir
from concourse.bass_utils import run_bass_kernel_spmd

F32 = mybir.dt.float32
BF16 = mybir.dt.bfloat16
I32 = mybir.dt.int32
AF = mybir.ActivationFunctionType
ALU = mybir.AluOpType
AX = mybir.AxisListType.X

D = 4096
NH = 32
DN = 128
DPE = 64
DV = 128
QL = 1536
KVL = 512
KVA = KVL + DPE
TOWN = 1024
TB = 4096
SCALE = 1.0 / math.sqrt(DN + DPE)
EPS_RMS = 1e-6
EPS_LN = 1e-5
NCORE = 8

BF = ml_dtypes.bfloat16

DEV_TAPS = ()
MAX_PHASE = 99
SERIALIZE = False
_TAPS = {
    "xq_fm": ([128, 32 * 1024], BF16),
    "dbg_xn": ([128, 4096], F32),
    "dbg_mv": ([128, 2], F32),
    "dbg_rln": ([128, 1], F32),
    "dbg_gammaB": ([128, 4096], F32),
    "dbg_xt": ([128, 4096], F32),
    "dbg_xln": ([128, 4096], F32),
    "dbg_sf": ([128, 1], F32),
    "dbg_xqb": ([128, 4096], BF16),
    "xh": ([128, 32 * 1024], BF16),
    "outpre": ([128, 8 * 4096], BF16),
    "ffn": ([128, 32 * 1024], F32),
    "xfm": ([128, 32 * 1024], BF16),
    "qlat": ([128, 12 * 1024], BF16),
    "kvlat": ([128, 4 * 4096], BF16),
    "kpe": ([64, 4096], BF16),
    "qn_h0": ([128, 1024], BF16),
    "qpe_h0": ([64, 1024], BF16),
    "kn_h0": ([128, 4096], BF16),
    "vtm_h0": ([128, 4096], BF16),
    "negm_h0": ([1, 1024], BF16),
    "pt_h0": ([128, 32 * 256], BF16),
    "lrow_h0": ([1, 1024], F32),
    "attnh_h0": ([128, 1024], BF16),
    "attn_all": ([4096, 1024], BF16),
}


def _hadamard(n):
    h = np.array([[1.0]], dtype=np.float32)
    while h.shape[0] < n:
        h = np.block([[h, h], [h, -h]])
    return h


class Ctr:
    __slots__ = ("h", "n", "name", "is_lane")

    def __init__(self, h, name):
        self.h = h
        self.n = 0
        self.name = name
        self.is_lane = False


class DmaQ:

    def __init__(self, lanes):
        self.lanes = lanes
        self.idx = 0

    def next(self):
        K = len(self.lanes)
        i = self.idx
        self.idx += 1
        lane = self.lanes[i % K]
        pre = (lane, 16 * (i // K)) if i >= K else None
        lane.n = 16 * (i // K + 1)
        return lane, pre, (lane, lane.n)

    def final_tokens(self):
        return [(ln, ln.n) for ln in self.lanes]


class Prog:
    ENGS = ("sync", "vector", "scalar", "tensor", "gpsimd")

    def __init__(self, nc):
        self.nc = nc
        self.q = {e: [] for e in self.ENGS}
        self.state = {e: {} for e in self.ENGS}
        self.owner = {}
        self.cur_phase = 0
        self.max_phase = 99
        self.serialize = False
        self.all_ctrs = []
        self.tap_gate = []

    def ctr(self, stack, name, owner):
        h = stack.enter_context(self.nc.semaphore(name))
        c = Ctr(h, name)
        self.owner[id(c)] = owner
        return c

    def emit(self, eng, fn, waits=(), inc=None, amt=1, fwaits=()):
        if self.cur_phase > self.max_phase:
            return None
        if self.tap_gate:
            waits = list(waits) + self.tap_gate
        if self.serialize:
            waits = list(waits) + [(c, c.n) for c in self.all_ctrs]
        wl = []
        st = self.state[eng]
        for tok in fwaits:
            if tok is None:
                continue
            c, v = tok
            if v > 0:
                wl.append((c.h, v))
        for tok in waits:
            if tok is None:
                continue
            c, v = tok
            if v <= 0:
                continue
            if (self.owner.get(id(c)) == eng and not c.is_lane
                    and (c.n - v) > 12):
                continue
            if st.get(id(c), 0) >= v:
                continue
            st[id(c)] = v
            wl.append((c.h, v))
        tok = None
        if inc is not None:
            inc.n += amt
            tok = (inc, inc.n)
        self.q[eng].append((wl, fn, ((inc.h if inc is not None else None), amt)))
        return tok

    def dma(self, eng, out, in_, waits=(), inc=None):
        def fn(e, out=out, in_=in_):
            return e.dma_start(out=out, in_=in_)
        if isinstance(inc, DmaQ):
            if self.cur_phase > self.max_phase:
                return None
            lane, pre, tok = inc.next()
            w = list(waits)
            if pre is not None:
                w.append(pre)
            self.emit(eng, fn, w, lane, 16, fwaits=())
            lane.n = tok[1]
            return tok
        return self.emit(eng, fn, waits, inc, 16)

    def finalize(self, block, final_waits):
        def runner(e, name):
            for wl, fn, (semh, amt) in self.q[name]:
                for h, v in wl:
                    e.wait_ge(h, v)
                ins = fn(e)
                if semh is not None and ins is not None:
                    ins.then_inc(semh, amt)
            if name == "sync":
                for c, v in final_waits:
                    e.wait_ge(c.h, v)
        block.sync(lambda e: runner(e, "sync"))
        block.vector(lambda e: runner(e, "vector"))
        block.scalar(lambda e: runner(e, "scalar"))
        block.tensor(lambda e: runner(e, "tensor"))
        block.gpsimd(lambda e: runner(e, "gpsimd"))


def build(taps=(), max_phase=99):
    from contextlib import ExitStack
    nc = bass.Bass("TRN2", target_bir_lowering=False, debug=False)

    def din(name, shape, dt=F32):
        return nc.dram_tensor(name, list(shape), dt, kind="ExternalInput").ap()

    x_own = din("x_own", [TOWN, D])
    wqa_t = din("wqa_t", [3, 128, 32 * 512])
    wqb_t = din("wqb_t", [NH, 128, 12 * 192])
    wkva_t = din("wkva_t", [128, 32 * KVA])
    wkvb_t = din("wkvb_t", [NH, 128, 4 * 256])
    wo_t = din("wo_t", [8, 32, 128, 512])
    wf_t = din("wf_t", [8, 32, 128, 512])
    x_cov = din("x_cov", [TB, D])
    qnw = din("qnw", [128, 12])
    kvnw = din("kvnw", [128, 4])
    gamma = din("gamma", [1, D])
    beta = din("beta", [1, D])
    h128 = din("h128", [128, 128], BF16)
    h32s = din("h32s", [128, 32], BF16)
    ident_bf = din("ident_bf", [128, 128], BF16)
    ident_f32 = din("ident_f32", [128, 128], F32)
    strip = din("strip", [128, 512], BF16)
    cos_own = din("cos_own", [32, TOWN])
    sin_own = din("sin_own", [32, TOWN])
    cos_cov = din("cos_cov", [32, TB])
    sin_cov = din("sin_cov", [32, TB])

    out = nc.dram_tensor("out", [TOWN, D], F32, kind="ExternalOutput").ap()

    bounce = nc.dram_tensor("bounce", [128, 32 * 1024], BF16, kind="Internal").ap()
    ffn_scr = nc.dram_tensor("ffn_scr", [128, 32 * 1024], F32, kind="Internal").ap()
    attn_scr = nc.dram_tensor("attn_scr", [NH * 128, TOWN], BF16, kind="Internal").ap()
    rowbc = nc.dram_tensor("rowbc", [8, 4096], F32, kind="Internal").ap()

    tap_t = {}
    for name in taps:
        shape, dt = _TAPS[name]
        tap_t[name] = nc.dram_tensor("tap_" + name, list(shape), dt,
                                     kind="ExternalOutput").ap()

    st = ExitStack()
    with st:
        P = Prog(nc)
        P.max_phase = max_phase
        import sys as _sys
        P.serialize = bool(globals().get("SERIALIZE")) or SERIALIZE
        T = P.ctr(st, "T", "tensor")
        V = P.ctr(st, "V", "vector")
        SC = P.ctr(st, "SC", "scalar")
        G = P.ctr(st, "G", "gpsimd")
        sy_lanes = [P.ctr(st, "SYL%d" % i, "sync") for i in range(8)]
        gq_lanes = [P.ctr(st, "GQL%d" % i, "gpsimd") for i in range(4)]
        for _ln in sy_lanes + gq_lanes:
            _ln.is_lane = True
        syq = DmaQ(sy_lanes)
        gq = DmaQ(gq_lanes)
        SY = [syq] * 6
        ALL = [T, V, SC, G] + sy_lanes + gq_lanes
        P.all_ctrs = ALL

        def sb(name, shape, dt, stack=None, side=None):
            return (stack or st).enter_context(
                nc.sbuf_tensor(name, list(shape), dt, side=side))

        def ps(name, shape, dt, stack=None):
            return (stack or st).enter_context(nc.psum_tensor(name, list(shape), dt))

        def join():
            return [(c, c.n) for c in ALL]

        def tap(name, src_ap, waits):
            if name in tap_t:
                tok = P.dma("sync", tap_t[name], src_ap, waits, SY[5])
                if tok is not None:
                    P.tap_gate.append(tok)

        c_h128 = sb("c_h128", [128, 128], BF16)
        c_h32s = sb("c_h32s", [128, 32], BF16)
        c_idbf = sb("c_idbf", [128, 128], BF16)
        c_idf = sb("c_idf", [128, 128], F32)
        c_strip = sb("c_strip", [128, 512], BF16)
        c_qnw = sb("c_qnw", [128, 12], F32)
        c_kvnw = sb("c_kvnw", [128, 4], F32)
        ones_r = sb("ones_r", [1, 128], BF16)
        ones_c = sb("ones_c", [128, 1], BF16)
        onesf = sb("onesf", [128, 1], F32)
        one1 = sb("one1", [1, 1], F32)
        wsq_col = sb("wsq_col", [128, 12], F32)
        wabs = sb("wabs", [128, 258], F32)
        thr_c = sb("thr_c", [128, 1], F32)
        nthr_c = sb("nthr_c", [128, 1], F32)
        wsc_c = sb("wsc_c", [128, 1], F32)
        rowtmp = sb("rowtmp", [1, 4096], F32)
        coltmp = sb("coltmp", [128, 2], F32)
        cst = sb("cst", [128, 3], F32)

        ld0 = []
        for dst, src in ((c_h128, h128), (c_h32s, h32s), (c_idbf, ident_bf),
                         (c_idf, ident_f32), (c_strip, strip), (c_qnw, qnw),
                         (c_kvnw, kvnw)):
            ld0.append(P.dma("sync", dst[:, :], src, (), SY[0]))
        m1 = P.emit("vector", lambda e: e.memset(ones_r[:, :], 1.0), (), V)
        m2 = P.emit("vector", lambda e: e.memset(ones_c[:, :], 1.0), (), V)
        m3 = P.emit("vector", lambda e: e.memset(onesf[:, :], 1.0), (), V)
        m4 = P.emit("vector", lambda e: e.memset(one1[:, :], 1.0), (), V)
        m5a = P.emit("vector", lambda e: e.memset(cst[:, 0:1], EPS_LN), (), V)
        m5b = P.emit("vector", lambda e: e.memset(cst[:, 1:2], EPS_RMS), (), V)
        m5c = P.emit("vector", lambda e: e.memset(cst[:, 2:3], 1e-30), (), V)
        consts_tok0 = [m5a, m5b, m5c]
        consts_tok = ld0 + [m1, m2, m3, m4]

        def transposes(src_tile, n_c, dst_write, tok_src, pbank, ident,
                       extra_src_toks=()):
            last = None
            for c0 in range(0, n_c, 4):
                nb = min(4, n_c - c0)
                tm = None
                for k in range(nb):
                    c = c0 + k
                    w = [tok_src] + list(extra_src_toks)
                    if last is not None:
                        w.append(last)
                    tm = P.emit("tensor", lambda e, c=c, k=k: e.transpose(
                        pbank[:, k * 128:(k + 1) * 128],
                        src_tile[:, c * 128:(c + 1) * 128], ident[:, :]),
                        w, T if k == nb - 1 else None)
                last = dst_write(c0, nb, pbank, tm)
            return last

        def row_to_col(row_ap, psc_out, dst_col, waits):
            tm = P.emit("tensor", lambda e: e.matmul(
                psc_out, row_ap, one1[:, :], start=True, stop=True),
                list(waits), T)
            return P.emit("vector", lambda e: e.tensor_copy(out=dst_col, in_=psc_out),
                          [tm], V)

        P.cur_phase = 1
        sXQ = ExitStack()
        xq_fm = sb("xq_fm", [128, 32 * 1024], BF16, sXQ, side="right")
        pA = ExitStack()
        with pA:
            xt = [sb("xtA%d" % i, [128, D], F32, pA) for i in range(2)]
            gammaB = sb("gammaB", [128, D], F32, pA)
            betaB = sb("betaB", [128, D], F32, pA)
            bn6 = sb("bn6", [128, 48], F32, pA)
            mv = sb("mv", [128, 2], F32, pA)
            rln_c = sb("rln_c", [128, 1], F32, pA)
            nmu_c = sb("nmu_c", [128, 1], F32, pA)
            sf_c = sb("sf_c", [128, 1], F32, pA)
            s1 = sb("s1A", [128, D], F32, pA)
            xqi = sb("xqi", [128, D // 2], I32, pA)
            xqb = sb("xqb", [128, D], BF16, pA)
            ptrA = ps("ptrA", [128, 512], BF16, pA)

            gb = P.dma("sync", gammaB[:, :], gamma.partition_broadcast(128), (), SY[0])
            bb = P.dma("sync", betaB[:, :], beta.partition_broadcast(128), (), SY[0])

            ev = None
            tg_prev = None
            for i in range(8):
                b = i % 2
                tl = P.dma("sync", xt[b][:, :], x_own[i * 128:(i + 1) * 128, :],
                           [ev] if i >= 2 else [], SY[0])
                t1 = None
                for kk in range(8):
                    t1 = P.emit("vector", lambda e, b=b, kk=kk: e.bn_stats(
                        bn6[:, kk * 6:(kk + 1) * 6],
                        xt[b][:, kk * 512:(kk + 1) * 512]), [tl, tg_prev], V)
                t2 = P.emit("vector", lambda e: e.bn_aggr(mv[:, :], bn6[:, :]), [t1], V)
                t3 = P.emit("scalar", lambda e: e.activation(
                    rln_c[:, :], mv[:, 1:2], AF.Ln, bias=cst[:, 0:1], scale=1.0), [t2], SC)
                t4 = P.emit("scalar", lambda e: e.activation(
                    rln_c[:, :], rln_c[:, :], AF.Exp, bias=0.0, scale=-0.5), [t3], SC)
                t5 = P.emit("vector", lambda e: e.tensor_scalar(
                    out=nmu_c[:, :], in0=mv[:, 0:1], scalar1=-1.0, scalar2=None,
                    op0=ALU.mult), [t2], V)
                t7 = P.emit("vector", lambda e, b=b: e.tensor_scalar(
                    out=s1[:, :], in0=xt[b][:, :], scalar1=nmu_c[:, :],
                    scalar2=rln_c[:, :], op0=ALU.add, op1=ALU.mult), [t4], V,
                    fwaits=[t5])
                if i == 0:
                    tap("dbg_xn", s1[:, :], [t7])
                    tap("dbg_mv", mv[:, :], [t2])
                    tap("dbg_rln", rln_c[:, :], [t4])
                    tap("dbg_gammaB", gammaB[:, :], [gb])
                    tap("dbg_xt", xt[0][:, :], [tl])
                t8 = P.emit("gpsimd", lambda e: e.tensor_tensor(
                    out=s1[:, :], in0=s1[:, :], in1=gammaB[:, :], op=ALU.mult),
                    [t7, gb], G)
                t9 = P.emit("gpsimd", lambda e: e.tensor_tensor(
                    out=s1[:, :], in0=s1[:, :], in1=betaB[:, :], op=ALU.add),
                    [t8, bb], G)
                ta = P.emit("vector", lambda e: e.reduce_max(
                    sf_c[:, :], s1[:, :], axis=AX, apply_absolute_value=True), [t9], V)
                tb = P.emit("vector", lambda e: e.tensor_scalar(
                    out=coltmp[:, 0:1], in0=sf_c[:, :], scalar1=1e-6, scalar2=1e6,
                    op0=ALU.max, op1=ALU.min), [ta], V)
                tc = P.emit("vector", lambda e: e.reciprocal(
                    coltmp[:, 1:2], coltmp[:, 0:1]), [tb], V)
                td = P.emit("vector", lambda e: e.tensor_scalar(
                    out=coltmp[:, 0:1], in0=coltmp[:, 1:2], scalar1=7.0, scalar2=1e6,
                    op0=ALU.mult, op1=ALU.min), [tc], V)
                te = P.emit("vector", lambda e: e.tensor_scalar(
                    out=sf_c[:, :], in0=coltmp[:, 0:1], scalar1=1e-6, scalar2=None,
                    op0=ALU.max), [td], V)
                tg = None
                for hh in range(2):
                    tf = P.emit("vector", lambda e, hh=hh: e.tensor_scalar(
                        out=xqi[:, :], in0=s1[:, hh * 2048:(hh + 1) * 2048],
                        scalar1=sf_c[:, :], scalar2=None,
                        op0=ALU.mult), [tg], V, fwaits=[te])
                    tg = P.emit("vector", lambda e, hh=hh: e.tensor_copy(
                        out=xqb[:, hh * 2048:(hh + 1) * 2048], in_=xqi[:, :]),
                        [tf], V)
                tg_prev = tg
                if i == 0:
                    tap("dbg_xln", s1[:, :], [t9])
                    tap("dbg_sf", sf_c[:, :], [te])
                    tap("dbg_xqb", xqb[:, :], [tg])

                def wrA(c0, nb, pbank, tokT, i=i):
                    def f(e, c0=c0, nb=nb, i=i):
                        dst = xq_fm[:, :].rearrange("p (c t) -> p c t", c=32)
                        return e.tensor_copy(
                            out=dst[:, c0:c0 + nb, i * 128:(i + 1) * 128],
                            in_=pbank[:, 0:nb * 128])
                    return P.emit("vector", f, [tokT], V)
                ev = transposes(xqb, 32, wrA, tg, ptrA, c_idbf,
                                extra_src_toks=consts_tok)
            xqfm_done = ev
            tap("xq_fm", xq_fm[:, :], [xqfm_done])

        P.cur_phase = 2
        pW = ExitStack()
        with pW:
            wt = [sb("wp1_%d" % i, [128, 2048], F32, pW) for i in range(2)]
            wp1dump = sb("wp1dump", [128, 512], F32, pW)
            vts = {}
            jwW = join()
            for idx in range(64):
                b = idx % 2
                osp, g4 = divmod(idx, 8)
                srcv = wf_t[osp, g4 * 4:(g4 + 1) * 4, :, :].rearrange("g p o -> p g o")
                w = [vts[idx - 2]] if idx >= 2 else list(jwW)
                tl = P.dma("sync", wt[b][:, :].rearrange("p (g o) -> p g o", g=4),
                           srcv, w, SY[1])
                vt2 = vts.get(idx - 1)
                for kq in range(4):
                    vt2 = P.emit("scalar", lambda e, b=b, idx=idx, kq=kq: e.activation(
                        out=wp1dump[:, :], in_=wt[b][:, kq * 512:(kq + 1) * 512],
                        func=AF.Abs,
                        accum_out=wabs[:, idx * 4 + kq:idx * 4 + kq + 1]),
                        [tl, vt2], SC)
                vts[idx] = vt2
            tr = P.emit("vector", lambda e: e.reduce_sum(
                wabs[:, 256:257], wabs[:, 0:256], axis=AX),
                [vts[i] for i in range(64)], V)
            psW = ps("psW", [1, 8], F32, pW)
            tm = P.emit("tensor", lambda e: e.matmul(
                psW[0:1, 0:1], wabs[:, 256:257], onesf[:, :], start=True, stop=True),
                [tr, m3], T)
            tv = P.emit("vector", lambda e: e.tensor_scalar(
                out=rowtmp[0:1, 0:1], in0=psW[0:1, 0:1],
                scalar1=0.5 / float(D * D), scalar2=None, op0=ALU.mult), [tm], V)
            tv2 = P.emit("vector", lambda e: e.tensor_scalar(
                out=rowtmp[0:1, 1:2], in0=rowtmp[0:1, 0:1], scalar1=-1.0,
                scalar2=None, op0=ALU.mult), [tv], V)
            tv3 = P.emit("vector", lambda e: e.tensor_scalar(
                out=rowtmp[0:1, 2:3], in0=rowtmp[0:1, 0:1], scalar1=2.0,
                scalar2=None, op0=ALU.mult), [tv], V)
            td1 = P.dma("sync", rowbc[0:1, 0:3], rowtmp[0:1, 0:3], [tv2, tv3], SY[1])
            td2 = P.dma("sync", thr_c[:, :], rowbc[0:1, 0:1].partition_broadcast(128),
                        [td1], SY[1])
            td3 = P.dma("sync", nthr_c[:, :], rowbc[0:1, 1:2].partition_broadcast(128),
                        [td1], SY[1])
            td4 = P.dma("sync", wsc_c[:, :], rowbc[0:1, 2:3].partition_broadcast(128),
                        [td1], SY[1])
            wp1_done = [td2, td3, td4]

        P.cur_phase = 2.2
        pF = ExitStack()
        with pF:
            y1 = sb("y1", [128, 32 * 1024], BF16, pF)
            psA = [ps("psFA%d" % i, [128, 512], F32, pF) for i in range(2)]
            evh = {}
            jwF = join()
            import os as _os
            _ffa = int(_os.environ.get("FFA_C", "32"))
            for c in range(_ffa):
                for th in range(2):
                    k = (c * 2 + th) % 2
                    w = [xqfm_done] + (jwF if (c == 0 and th == 0) else [])
                    if (c * 2 + th) >= 2:
                        w.append(evh[c * 2 + th - 2])
                    tm = P.emit("tensor", lambda e, c=c, th=th, k=k: e.matmul(
                        psA[k][:, :], c_h128[:, :],
                        xq_fm[:, c * 1024 + th * 512:c * 1024 + th * 512 + 512],
                        start=True, stop=True), w, T)
                    evh[c * 2 + th] = P.emit(
                        "vector", lambda e, c=c, th=th, k=k: e.tensor_copy(
                            out=y1[:, c * 1024 + th * 512:c * 1024 + th * 512 + 512],
                            in_=psA[k][:, :]), [tm], V)
            y1_done = evh[_ffa * 2 - 1]
            P.cur_phase = 2.22
            tb1 = P.dma("sync", bounce[:, :], y1[:, :], [y1_done], SY[2])
        sXQ.close()

        P.cur_phase = 2.3
        pF2 = ExitStack()
        with pF2:
            xh = sb("xh", [128, 32 * 1024], BF16, pF2)
            pY1T = ExitStack()
            y1t = sb("y1t", [128, 32 * 1024], BF16, pY1T)
            tb2 = P.dma("sync", y1t[:, :].rearrange("p (g t) -> p g t", g=32),
                        bounce.rearrange("(g j) (c t) -> (j c) g t", j=4, c=32),
                        [tb1], SY[2])
            psB = [ps("psFB%d" % i, [128, 512], F32, pY1T) for i in range(2)]
            evh = {}
            for g in range(32):
                for tsp in range(2):
                    gi = g * 2 + tsp
                    k = gi % 2
                    tm = None
                    for j in range(4):
                        w = [tb2]
                        if gi >= 2:
                            w.append(evh[gi - 2])
                        tm = P.emit("tensor", lambda e, g=g, tsp=tsp, j=j, k=k: e.matmul(
                            psB[k][32 * j:32 * j + 32, 0:512],
                            c_h32s[32 * j:32 * j + 32, :],
                            y1t[32 * j:32 * j + 32,
                                g * 1024 + tsp * 512:g * 1024 + tsp * 512 + 512],
                            start=True, stop=True, tile_position=(32 * j, 32 * j)),
                            w, T if j == 3 else None)
                    evh[gi] = P.emit("vector", lambda e, g=g, tsp=tsp, k=k: e.tensor_copy(
                        out=xh[:, g * 1024 + tsp * 512:g * 1024 + tsp * 512 + 512],
                        in_=psB[k][:, :]), [tm], V)
            xh_done = evh[63]
            tap("xh", xh[:, :], [xh_done])
            pY1T.close()

            P.cur_phase = 2.4
            pOPF = ExitStack()
            opfm = sb("opfm", [128, 32 * 1024], BF16, pOPF, side="right")
            pMAIN = ExitStack()
            outsp = sb("outsp", [128, 8 * 512], BF16, pMAIN)
            wfr = [sb("wfr%d" % i, [128, 512], F32, pMAIN) for i in range(3)]
            wqg = [sb("wqg%d" % i, [128, 512], BF16, pMAIN) for i in range(2)]
            q01v = sb("q01v", [128, 512], BF16, pMAIN)
            q02v = sb("q02v", [128, 512], BF16, pMAIN)
            q01g = sb("q01g", [128, 512], BF16, pMAIN)
            q02g = sb("q02g", [128, 512], BF16, pMAIN)
            psM = [ps("psM%d" % i, [128, 512], F32, pMAIN) for i in range(8)]
            qt = {}
            mm_last = {}
            tr_done = [None] * 8
            op_done = [None] * 8
            for osp in range(8):
                for g in range(32):
                    idx = osp * 32 + g
                    rb = idx % 3
                    qb = idx % 2
                    w = list(wp1_done)
                    if idx < 3:
                        w.append(xh_done)
                    if idx >= 3:
                        w.append(qt[idx - 3])
                    tl = P.dma("sync", wfr[rb][:, :], wf_t[osp, g, :, :], w, SY[2])
                    qe, qc = (("vector", V) if idx % 2 == 0 else ("gpsimd", G))
                    qa = q01v if idx % 2 == 0 else q01g
                    qbuf = q02v if idx % 2 == 0 else q02g
                    wq_war = [mm_last[idx - 2]] if idx >= 2 else []
                    g1 = P.emit(qe, lambda e, rb=rb, qa=qa: e.tensor_scalar(
                        out=qa[:, :], in0=wfr[rb][:, :], scalar1=thr_c[:, :],
                        scalar2=None, op0=ALU.is_gt), [tl] + wq_war, qc)
                    g2 = P.emit(qe, lambda e, rb=rb, qbuf=qbuf: e.tensor_scalar(
                        out=qbuf[:, :], in0=wfr[rb][:, :], scalar1=nthr_c[:, :],
                        scalar2=None, op0=ALU.is_lt), [g1], qc)
                    qt[idx] = P.emit(qe, lambda e, qb=qb, qa=qa, qbuf=qbuf: e.tensor_tensor(
                        out=wqg[qb][:, :], in0=qa[:, :], in1=qbuf[:, :],
                        op=ALU.subtract), [g2], qc)
                    tm = None
                    for tt in range(8):
                        w = [qt[idx], xh_done]
                        if g == 0:
                            if tr_done[tt] is not None:
                                w.append(tr_done[tt])
                        tm = P.emit("tensor", lambda e, g=g, tt=tt, qb=qb: e.matmul(
                            psM[tt][:, :],
                            xh[:, g * 1024 + tt * 128:g * 1024 + tt * 128 + 128],
                            wqg[qb][:, :], start=(g == 0), stop=(g == 31)),
                            w, T if (g == 31 or tt == 7) else None)
                    mm_last[idx] = tm
                for tt in range(8):
                    ev = P.emit("vector", lambda e, tt=tt: e.tensor_scalar(
                        out=outsp[:, tt * 512:(tt + 1) * 512],
                        in0=psM[tt][:, :], scalar1=wsc_c[:, :],
                        scalar2=None, op0=ALU.mult),
                        [mm_last[osp * 32 + 31]] + ([op_done[tt]] if op_done[tt] else []),
                        V)
                    tm2 = None
                    for cj in range(4):
                        tm2 = P.emit("tensor", lambda e, tt=tt, cj=cj: e.matmul(
                            psM[tt][:, cj * 128:(cj + 1) * 128],
                            outsp[:, tt * 512 + cj * 128:tt * 512 + cj * 128 + 128],
                            c_idbf[:, :], start=True, stop=True),
                            [ev], T if cj == 3 else None)
                    tr_done[tt] = tm2
                    def fop(e, tt=tt, osp=osp):
                        dst = opfm[:, :].rearrange("p (c t) -> p c t", c=32)
                        return e.tensor_copy(
                            out=dst[:, osp * 4:osp * 4 + 4, tt * 128:(tt + 1) * 128],
                            in_=psM[tt][:, :])
                    op_done[tt] = P.emit("vector", fop, [tm2], V)
            opfm_done = [t for t in op_done]
            tap("outpre", opfm[:, :], opfm_done)
            pMAIN.close()

        P.cur_phase = 2.5
        pF3 = ExitStack()
        with pF3:
            y2 = sb("y2", [128, 32 * 1024], BF16, pF3)
            psA2 = [ps("psA2%d" % i, [128, 512], F32, pF3) for i in range(2)]
            evh = {}
            for c in range(32):
                for th in range(2):
                    gi = c * 2 + th
                    k = gi % 2
                    w = list(opfm_done)
                    if gi >= 2:
                        w.append(evh[gi - 2])
                    tm = P.emit("tensor", lambda e, c=c, th=th, k=k: e.matmul(
                        psA2[k][:, :], c_h128[:, :],
                        opfm[:, c * 1024 + th * 512:c * 1024 + th * 512 + 512],
                        start=True, stop=True), w, T)
                    evh[gi] = P.emit("vector", lambda e, c=c, th=th, k=k: e.tensor_copy(
                        out=y2[:, c * 1024 + th * 512:c * 1024 + th * 512 + 512],
                        in_=psA2[k][:, :]), [tm], V)
            tb3 = P.dma("sync", bounce[:, :], y2[:, :], [evh[63]], SY[2])
        pOPF.close()
        P.cur_phase = 2.6
        pF4 = ExitStack()
        with pF4:
            y2t = sb("y2t", [128, 32 * 1024], BF16, pF4)
            tb4 = P.dma("sync", y2t[:, :].rearrange("p (g t) -> p g t", g=32),
                        bounce.rearrange("(g j) (c t) -> (j c) g t", j=4, c=32),
                        [tb3], SY[2])
            psB2 = [ps("psB2%d" % i, [128, 512], F32, pF4) for i in range(2)]
            sbf = [sb("sbf%d" % i, [128, 512], F32, pF4) for i in range(2)]
            dstw = ffn_scr.rearrange("(gg j) (cc t) -> (j cc) gg t", j=4, cc=32)
            dvh = {}
            for g in range(32):
                for tsp in range(2):
                    gi = g * 2 + tsp
                    k = gi % 2
                    tm = None
                    for j in range(4):
                        w = [tb4]
                        if gi >= 2:
                            w.append(dvh[gi - 2])
                        tm = P.emit("tensor", lambda e, g=g, tsp=tsp, j=j, k=k: e.matmul(
                            psB2[k][32 * j:32 * j + 32, 0:512],
                            c_h32s[32 * j:32 * j + 32, :],
                            y2t[32 * j:32 * j + 32,
                                g * 1024 + tsp * 512:g * 1024 + tsp * 512 + 512],
                            start=True, stop=True, tile_position=(32 * j, 32 * j)),
                            w, T if j == 3 else None)
                    ev = P.emit("vector", lambda e, k=k: e.tensor_copy(
                        out=sbf[k][:, :], in_=psB2[k][:, :]), [tm], V)
                    dvh[gi] = P.dma("sync", dstw[:, g, tsp * 512:tsp * 512 + 512],
                                    sbf[k][:, :], [ev], SY[2])
            ffn_done = list(dvh.values())
            tap("ffn", ffn_scr, ffn_done)

        P.cur_phase = 3
        pM1 = ExitStack()
        qlat = sb("qlat", [128, 12 * TOWN], BF16, pM1)
        pQ = ExitStack()
        with pQ:
            murln_row = sb("murln_row", [1, TOWN], F32, pQ)
            rln_row = sb("rln_row", [1, TOWN], F32, pQ)
            xfm = sb("xfm", [128, 32 * 1024], BF16, pQ)
            xt2 = [sb("xtB%d" % i, [128, D], F32, pQ) for i in range(2)]
            bn6b = sb("bn6b", [128, 48], F32, pQ)
            mvb = sb("mvb", [128, 2], F32, pQ)
            rlnb_c = sb("rlnb_c", [128, 1], F32, pQ)
            nmub_c = sb("nmub_c", [128, 1], F32, pQ)
            mnrb_c = sb("mnrb_c", [128, 1], F32, pQ)
            prowB = ps("prowB", [1, 512], F32, pQ)
            prowB2 = ps("prowB2", [1, 512], F32, pQ)
            ptrB = ps("ptrB", [128, 512], F32, pQ)
            jw = join()
            ev = None
            rtok = None
            he1 = he2 = None
            t6hist = {}
            for i in range(8):
                b = i % 2
                tl = P.dma("sync", xt2[b][:, :], x_own[i * 128:(i + 1) * 128, :],
                           (jw if i < 2 else [ev, t6hist[i - 2]]), SY[3])
                t1 = None
                for kk in range(8):
                    t1 = P.emit("vector", lambda e, b=b, kk=kk: e.bn_stats(
                        bn6b[:, kk * 6:(kk + 1) * 6],
                        xt2[b][:, kk * 512:(kk + 1) * 512]), [tl], V)
                t2 = P.emit("vector", lambda e: e.bn_aggr(mvb[:, :], bn6b[:, :]),
                            [t1], V)
                t3 = P.emit("scalar", lambda e: e.activation(
                    rlnb_c[:, :], mvb[:, 1:2], AF.Ln, bias=cst[:, 0:1], scale=1.0), [t2], SC)
                t4 = P.emit("scalar", lambda e: e.activation(
                    rlnb_c[:, :], rlnb_c[:, :], AF.Exp, bias=0.0, scale=-0.5), [t3], SC)
                t5 = P.emit("vector", lambda e: e.tensor_scalar(
                    out=nmub_c[:, :], in0=mvb[:, 0:1], scalar1=-1.0, scalar2=None,
                    op0=ALU.mult), [t2], V)
                t6 = P.emit("vector", lambda e: e.tensor_scalar(
                    out=mnrb_c[:, :], in0=nmub_c[:, :], scalar1=rlnb_c[:, :],
                    scalar2=None, op0=ALU.mult), [t4, t5], V)
                t6hist[i] = t6
                half, col = divmod(i, 4)
                hw = [t6] + ([rtok] if rtok else [])
                if col == 0 and half == 1:
                    hw += [he1, he2]
                r1 = P.emit("tensor", lambda e, col=col: e.matmul(
                    prowB[0:1, col * 128:(col + 1) * 128], mnrb_c[:, :], c_idf[:, :],
                    start=True, stop=True), hw, T)
                rtok = P.emit("tensor", lambda e, col=col: e.matmul(
                    prowB2[0:1, col * 128:(col + 1) * 128], rlnb_c[:, :], c_idf[:, :],
                    start=True, stop=True), [t4, r1], T)
                if col == 3:
                    he1 = P.emit("vector", lambda e, half=half: e.tensor_copy(
                        out=murln_row[0:1, half * 512:(half + 1) * 512],
                        in_=prowB[0:1, :]), [rtok], V)
                    he2 = P.emit("vector", lambda e, half=half: e.tensor_copy(
                        out=rln_row[0:1, half * 512:(half + 1) * 512],
                        in_=prowB2[0:1, :]), [rtok], V)

                def wrB(c0, nb, pbank, tokT, i=i):
                    def f(e, c0=c0, nb=nb, i=i):
                        dst = xfm[:, :].rearrange("p (c t) -> p c t", c=32)
                        return e.activation(
                            out=dst[:, c0:c0 + nb, i * 128:(i + 1) * 128],
                            in_=pbank[:, 0:nb * 128], func=AF.Copy)
                    return P.emit("scalar", f, [tokT], SC)
                ev = transposes(xt2[b], 32, wrB, tl, ptrB, c_idf)
            xfm_done = ev
            tap("xfm", xfm[:, :], [xfm_done])
            d1 = P.dma("sync", rowbc[1:2, 0:1024], murln_row[0:1, :], [he1], SY[3])
            d2 = P.dma("sync", rowbc[2:3, 0:1024], rln_row[0:1, :], [he2], SY[3])
            B1q = sb("B1q", [128, 1024], F32, pQ)
            rlnB = sb("rlnB", [128, 1024], F32, pQ)
            d3 = P.dma("sync", B1q[:, :], rowbc[1:2, 0:1024].partition_broadcast(128),
                       [d1], SY[3])
            d4 = P.dma("sync", rlnB[:, :], rowbc[2:3, 0:1024].partition_broadcast(128),
                       [d2], SY[3])

            wqr = [sb("wqr0", [128, 32 * 512], BF16, pQ)]
            psQ = [ps("psQ%d" % i, [128, 512], F32, pQ) for i in range(2)]
            psR = ps("psRq", [1, 512], F32, pQ)
            psC = ps("psCq", [128, 1], F32, pQ)
            qraw = sb("qraw", [128, 512], F32, pQ)
            qc_last = None
            psq_free = [None, None]
            wq_free = None
            ws_done = {}
            for MB in range(3):
                wb = 0
                tl = P.dma("gpsimd", wqr[wb][:, :], wqa_t[MB, :, :],
                           [wq_free] if wq_free else join(), gq)
                for Mi in range(4):
                    M = MB * 4 + Mi
                    tm = None
                    for g in range(32):
                        w = [tl, m2]
                        if M > 0:
                            w.append(ws_done[M - 1])
                        tm = P.emit("tensor", lambda e, Mi=Mi, g=g, wb=wb: e.matmul(
                            psR[0:1, 0:128], ones_c[:, :],
                            wqr[wb][:, g * 512 + Mi * 128:g * 512 + Mi * 128 + 128],
                            start=(g == 0), stop=(g == 31)), w,
                            T if g == 31 else None)
                    er = P.emit("vector", lambda e: e.tensor_copy(
                        out=rowtmp[0:1, 0:128], in_=psR[0:1, 0:128]), [tm], V)
                    ws_done[M] = row_to_col(rowtmp[0:1, 0:128], psC[:, 0:1],
                                            wsq_col[:, M:M + 1], [er, m4])
                for Mi in range(4):
                    M = MB * 4 + Mi
                    for th in range(2):
                        k = (M * 2 + th) % 2
                        tm = None
                        for g in range(32):
                            w = [tl, xfm_done]
                            if g == 0 and psq_free[k] is not None:
                                w.append(psq_free[k])
                            if g == 0:
                                w.append(ws_done[11] if M == 11 else ws_done[M])
                            tm = P.emit("tensor",
                                        lambda e, Mi=Mi, g=g, th=th, k=k, wb=wb: e.matmul(
                                            psQ[k][:, :],
                                            wqr[wb][:, g * 512 + Mi * 128:
                                                    g * 512 + Mi * 128 + 128],
                                            xfm[:, g * 1024 + th * 512:
                                                g * 1024 + th * 512 + 512],
                                            start=(g == 0), stop=(g == 31)),
                                        w, T if g == 31 else None)
                        c1 = P.emit("vector", lambda e, k=k, th=th: e.scalar_tensor_tensor(
                            out=qraw[:, :], in0=psQ[k][:, :], scalar=1.0,
                            in1=rlnB[:, th * 512:th * 512 + 512],
                            op0=ALU.bypass, op1=ALU.mult), [tm, d4, qc_last], V)
                        psq_free[k] = c1
                        qc_last = P.emit(
                            "vector", lambda e, M=M, th=th: e.scalar_tensor_tensor(
                                out=qlat[:, M * 1024 + th * 512:M * 1024 + th * 512 + 512],
                                in0=B1q[:, th * 512:th * 512 + 512],
                                scalar=wsq_col[:, M:M + 1], in1=qraw[:, :],
                                op0=ALU.mult, op1=ALU.add),
                            [c1, d3], V, fwaits=[ws_done[M]])
                wq_free = qc_last
            sqs = sb("sqs", [128, 512], BF16, pQ)
            psS2 = ps("psS2", [1, 512], F32, pQ)
            rms_ev = {}
            for th in range(2):
                tm = None
                sq_tok = None
                for M in range(12):
                    w = [qc_last, tm]
                    if M == 0 and th == 1:
                        w.append(rms_ev[0])
                    sq_tok = P.emit("vector", lambda e, M=M, th=th: e.tensor_tensor(
                        out=sqs[:, :],
                        in0=qlat[:, M * 1024 + th * 512:M * 1024 + th * 512 + 512],
                        in1=qlat[:, M * 1024 + th * 512:M * 1024 + th * 512 + 512],
                        op=ALU.mult), w, V)
                    tm = P.emit("tensor", lambda e, M=M, th=th: e.matmul(
                        psS2[0:1, :], ones_c[:, :], sqs[:, :],
                        start=(M == 0), stop=(M == 11)),
                        [sq_tok] + ([rms_ev[0]] if (M == 0 and th == 1) else []), T)
                rms_ev[th] = P.emit("vector", lambda e, th=th: e.tensor_copy(
                    out=rowtmp[0:1, th * 512:th * 512 + 512], in_=psS2[0:1, :]),
                    [tm], V)
            s3 = P.emit("scalar", lambda e: e.activation(
                rowtmp[0:1, 1024:2048], rowtmp[0:1, 0:1024], AF.Ln,
                bias=cst[0:1, 1:2], scale=1.0 / QL), [rms_ev[0], rms_ev[1]], SC)
            s4 = P.emit("scalar", lambda e: e.activation(
                rowtmp[0:1, 1024:2048], rowtmp[0:1, 1024:2048], AF.Exp,
                bias=0.0, scale=-0.5), [s3], SC)
            d5 = P.dma("sync", rowbc[3:4, 0:1024], rowtmp[0:1, 1024:2048], [s4], SY[3])
            rmsqB = sb("rmsqB", [128, 1024], F32, pQ)
            d6 = P.dma("sync", rmsqB[:, :], rowbc[3:4, 0:1024].partition_broadcast(128),
                       [d5], SY[3])
            ql_done = None
            for M in range(12):
                ql_done = P.emit("vector", lambda e, M=M: e.scalar_tensor_tensor(
                    out=qlat[:, M * 1024:(M + 1) * 1024],
                    in0=qlat[:, M * 1024:(M + 1) * 1024],
                    scalar=c_qnw[:, M:M + 1], in1=rmsqB[:, :],
                    op0=ALU.mult, op1=ALU.mult), [d6, qc_last], V)
            qlat_done = ql_done
            tap("qlat", qlat[:, :], [qlat_done])

        P.cur_phase = 4
        pM2 = ExitStack()
        kvlat = sb("kvlat", [128, 4 * TB], BF16, pM2)
        kpe = sb("kpe", [64, TB], BF16, pM2)
        pC = ExitStack()
        with pC:
            wkva = sb("wkva", [128, 32 * KVA], BF16, pC)
            xt3 = [sb("xtC0", [128, D], F32, pC)]
            xfsp = sb("xfsp", [128, 32 * 512], BF16, pC)
            bn6c = sb("bn6c", [128, 48], F32, pC)
            mvc = sb("mvc", [128, 2], F32, pC)
            rlnc_c = sb("rlnc_c", [128, 1], F32, pC)
            nmuc_c = sb("nmuc_c", [128, 1], F32, pC)
            mnrc_c = sb("mnrc_c", [128, 1], F32, pC)
            kvc = sb("kvc", [128, 6 * 512], F32, pC)
            kvr = sb("kvr", [128, 512], F32, pC)
            sqc = sb("sqc", [128, 512], F32, pC)
            mB = sb("mB_c", [128, 512], F32, pC)
            rB = sb("rB_c", [128, 512], F32, pC)
            rkB = sb("rkB_c", [128, 512], F32, pC)
            ropes = sb("ropes", [32, 512], F32, pC)
            ropes2 = sb("ropes2", [32, 512], F32, pC)
            cosc = sb("cosc", [32, 512], F32, pC)
            sinc = sb("sinc", [32, 512], F32, pC)
            prowC = ps("prowC", [1, 512], F32, pC)
            prowC2 = ps("prowC2", [1, 512], F32, pC)
            ptrC = ps("ptrC", [128, 512], F32, pC)
            psK = [ps("psK%d" % i, [128, 512], F32, pC) for i in range(2)]
            psRC = ps("psRC", [1, 512], F32, pC)
            psCC = ps("psCC", [128, 1], F32, pC)
            wskv_col = sb("wskv_col", [128, 6], F32, pC)

            jw = join()
            lt = P.dma("gpsimd", wkva[:, :], wkva_t, jw, gq)
            ws_kv_done = {}
            prev = None
            _kvoff = [0, 128, 256, 384, 512, 544]
            _kvwid = [128, 128, 128, 128, 32, 32]
            for M in range(6):
                wid = _kvwid[M]
                off = _kvoff[M]
                tm = None
                for g in range(32):
                    w = [lt, m2] + ([prev] if prev else [])
                    tm = P.emit("tensor", lambda e, off=off, g=g, wid=wid: e.matmul(
                        psRC[0:1, 0:wid], ones_c[:, :],
                        wkva[:, g * KVA + off:g * KVA + off + wid],
                        start=(g == 0), stop=(g == 31)), w, T if g == 31 else None)
                er = P.emit("vector", lambda e, wid=wid: e.tensor_copy(
                    out=rowtmp[0:1, 0:wid], in_=psRC[0:1, 0:wid]), [tm], V)
                ws_kv_done[M] = row_to_col(rowtmp[0:1, 0:wid], psCC[0:wid, 0:1],
                                           wskv_col[0:wid, M:M + 1], [er, m4])
                prev = ws_kv_done[M]

            kv_done = None
            t6c = {}
            db1_prev = db4_prev = None
            ev_sp = None
            for sp in range(8):
                rtokc = None
                lc1 = P.dma("sync", cosc[:, :], cos_cov[:, sp * 512:(sp + 1) * 512],
                            jw + ([kv_done] if kv_done else []), SY[4])
                lc2 = P.dma("sync", sinc[:, :], sin_cov[:, sp * 512:(sp + 1) * 512],
                            jw + ([kv_done] if kv_done else []), SY[4])
                for q in range(4):
                    i = sp * 4 + q
                    b = 0
                    w = [kv_done] if (sp > 0 and q < 2) else []
                    if i >= 1:
                        w = w + ([ev_sp] if ev_sp else []) + [t6c[i - 1]]
                    tl = P.dma("sync", xt3[b][:, :], x_cov[i * 128:(i + 1) * 128, :],
                               (jw + w) if i < 2 else w, SY[4])
                    t1 = None
                    for kk in range(8):
                        t1 = P.emit("vector", lambda e, b=b, kk=kk: e.bn_stats(
                            bn6c[:, kk * 6:(kk + 1) * 6],
                            xt3[b][:, kk * 512:(kk + 1) * 512]), [tl], V)
                    t2 = P.emit("vector", lambda e: e.bn_aggr(mvc[:, :], bn6c[:, :]),
                                [t1], V)
                    t3 = P.emit("scalar", lambda e: e.activation(
                        rlnc_c[:, :], mvc[:, 1:2], AF.Ln, bias=cst[:, 0:1], scale=1.0),
                        [t2], SC)
                    t4 = P.emit("scalar", lambda e: e.activation(
                        rlnc_c[:, :], rlnc_c[:, :], AF.Exp, bias=0.0, scale=-0.5),
                        [t3], SC)
                    t5 = P.emit("vector", lambda e: e.tensor_scalar(
                        out=nmuc_c[:, :], in0=mvc[:, 0:1], scalar1=-1.0, scalar2=None,
                        op0=ALU.mult), [t2], V)
                    t6 = P.emit("vector", lambda e: e.tensor_scalar(
                        out=mnrc_c[:, :], in0=nmuc_c[:, :], scalar1=rlnc_c[:, :],
                        scalar2=None, op0=ALU.mult), [t4, t5], V)
                    t6c[i] = t6
                    r1 = P.emit("tensor", lambda e, q=q: e.matmul(
                        prowC[0:1, q * 128:(q + 1) * 128], mnrc_c[:, :], c_idf[:, :],
                        start=True, stop=True), [t6] + ([rtokc] if rtokc else []), T)
                    rtokc = P.emit("tensor", lambda e, q=q: e.matmul(
                        prowC2[0:1, q * 128:(q + 1) * 128], rlnc_c[:, :], c_idf[:, :],
                        start=True, stop=True), [t4, r1], T)

                    def wrC(c0, nb, pbank, tokT, q=q):
                        def f(e, c0=c0, nb=nb, q=q):
                            dst = xfsp[:, :].rearrange("p (c t) -> p c t", c=32)
                            return e.activation(
                                out=dst[:, c0:c0 + nb, q * 128:(q + 1) * 128],
                                in_=pbank[:, 0:nb * 128], func=AF.Copy)
                        return P.emit("scalar", f, [tokT], SC)
                    ev_sp = transposes(xt3[b], 32, wrC, tl, ptrC, c_idf)
                e1 = P.emit("vector", lambda e: e.tensor_copy(
                    out=rowtmp[0:1, 1024:1536], in_=prowC[0:1, 0:512]),
                    [rtokc] + ([db1_prev] if db1_prev else []), V)
                e2 = P.emit("vector", lambda e: e.tensor_copy(
                    out=rowtmp[0:1, 1536:2048], in_=prowC2[0:1, 0:512]), [rtokc], V)
                db1 = P.dma("sync", rowbc[4:5, 0:1024], rowtmp[0:1, 1024:2048],
                            [e1, e2], SY[4])
                db1_prev = db1
                db2 = P.dma("sync", mB[:, :], rowbc[4:5, 0:512].partition_broadcast(128),
                            [db1], SY[4])
                db3 = P.dma("sync", rB[:, :], rowbc[4:5, 512:1024].partition_broadcast(128),
                            [db1], SY[4])
                cor = {}
                for M in range(6):
                    wid = _kvwid[M]
                    off = _kvoff[M]
                    k = M % 2
                    tm = None
                    for g in range(32):
                        w = [lt, ev_sp]
                        if g == 0 and M >= 2:
                            w.append(cor[M - 2])
                        tm = P.emit("tensor", lambda e, off=off, g=g, wid=wid, k=k: e.matmul(
                            psK[k][0:wid, :],
                            wkva[:, g * KVA + off:g * KVA + off + wid],
                            xfsp[:, g * 512:(g + 1) * 512],
                            start=(g == 0), stop=(g == 31)), w,
                            T if g == 31 else None)
                    c1 = P.emit("vector", lambda e, M=M, wid=wid, k=k: e.scalar_tensor_tensor(
                        out=kvr[0:wid, :], in0=psK[k][0:wid, :], scalar=1.0,
                        in1=rB[0:wid, :], op0=ALU.bypass, op1=ALU.mult),
                        [tm, db3] + ([cor[M - 1]] if M >= 1 else []), V)
                    cor[M] = P.emit("vector", lambda e, M=M, wid=wid: e.scalar_tensor_tensor(
                        out=kvc[0:wid, M * 512:(M + 1) * 512], in0=mB[0:wid, :],
                        scalar=wskv_col[0:wid, M:M + 1], in1=kvr[0:wid, :],
                        op0=ALU.mult, op1=ALU.add), [c1, db2], V,
                        fwaits=[ws_kv_done[M]])
                tm = None
                sq_tok = None
                for M in range(4):
                    sq_tok = P.emit("vector", lambda e, M=M: e.tensor_tensor(
                        out=sqc[:, :], in0=kvc[:, M * 512:(M + 1) * 512],
                        in1=kvc[:, M * 512:(M + 1) * 512], op=ALU.mult),
                        [cor[M], tm], V)
                    tm = P.emit("tensor", lambda e, M=M: e.matmul(
                        psRC[0:1, :], onesf[:, :], sqc[:, :],
                        start=(M == 0), stop=(M == 3)), [sq_tok, m3], T)
                er = P.emit("vector", lambda e: e.tensor_copy(
                    out=rowtmp[0:1, 2048:2560], in_=psRC[0:1, :]), [tm], V)
                s5 = P.emit("scalar", lambda e: e.activation(
                    rowtmp[0:1, 2560:3072], rowtmp[0:1, 2048:2560], AF.Ln,
                    bias=cst[0:1, 1:2], scale=1.0 / KVL),
                    [er] + ([db4_prev] if db4_prev else []), SC)
                s6 = P.emit("scalar", lambda e: e.activation(
                    rowtmp[0:1, 2560:3072], rowtmp[0:1, 2560:3072], AF.Exp,
                    bias=0.0, scale=-0.5), [s5], SC)
                db4 = P.dma("sync", rowbc[5:6, 0:512], rowtmp[0:1, 2560:3072],
                            [s6], SY[4])
                db4_prev = db4
                db5 = P.dma("sync", rkB[:, :], rowbc[5:6, 0:512].partition_broadcast(128),
                            [db4], SY[4])
                kv_ev = {}
                for M in range(4):
                    kv_ev[M] = P.emit("vector", lambda e, M=M, sp=sp: e.scalar_tensor_tensor(
                        out=kvlat[:, M * TB + sp * 512:M * TB + sp * 512 + 512],
                        in0=kvc[:, M * 512:(M + 1) * 512],
                        scalar=c_kvnw[:, M:M + 1], in1=rkB[:, :],
                        op0=ALU.mult, op1=ALU.mult), [cor[M], db5], V)
                ro1 = P.emit("vector", lambda e: e.tensor_tensor(
                    out=ropes[:, :], in0=kvc[0:32, 4 * 512:5 * 512],
                    in1=cosc[:, :], op=ALU.mult), [cor[4], cor[5], lc1], V)
                ro2 = P.emit("vector", lambda e: e.tensor_tensor(
                    out=ropes2[:, :], in0=kvc[0:32, 5 * 512:6 * 512],
                    in1=sinc[:, :], op=ALU.mult), [cor[5], lc2], V)
                ro3 = P.emit("vector", lambda e, sp=sp: e.tensor_tensor(
                    out=kpe[0:32, sp * 512:(sp + 1) * 512], in0=ropes[:, :],
                    in1=ropes2[:, :], op=ALU.subtract), [ro1, ro2], V)
                ro4 = P.emit("vector", lambda e: e.tensor_tensor(
                    out=ropes[:, :], in0=kvc[0:32, 4 * 512:5 * 512],
                    in1=sinc[:, :], op=ALU.mult), [ro3, cor[4]], V)
                ro5 = P.emit("vector", lambda e: e.tensor_tensor(
                    out=ropes2[:, :], in0=kvc[0:32, 5 * 512:6 * 512],
                    in1=cosc[:, :], op=ALU.mult), [ro3, cor[5]], V)
                ro6 = P.emit("vector", lambda e, sp=sp: e.tensor_tensor(
                    out=kpe[32:64, sp * 512:(sp + 1) * 512], in0=ropes[:, :],
                    in1=ropes2[:, :], op=ALU.add), [ro4, ro5], V)
                kv_done = ro6
                kv_sp_done = [kv_ev[M] for M in range(4)] + [ro6]
            kvlat_done = kv_sp_done
            tap("kvlat", kvlat[:, :], kvlat_done)
            tap("kpe", kpe[:, :], kvlat_done)

        P.cur_phase = 5
        pT = ExitStack()
        with pT:
            coso = sb("coso", [32, TOWN], F32, pT)
            sino = sb("sino", [32, TOWN], F32, pT)
            wqbr = [sb("wqbr%d" % i, [128, 12 * 192], BF16, pT) for i in range(2)]
            wkbr = [sb("wkbr%d" % i, [128, 4 * 256], BF16, pT) for i in range(2)]
            qn = sb("qn", [128, TOWN], BF16, pT)
            qpe = sb("qpe", [64, TOWN], BF16, pT)
            kn = sb("kn", [128, TB], BF16, pT)
            vfm = sb("vfm", [128, TB], BF16, pT)
            vtm = sb("vtm", [128, TB], BF16, pT)
            ptb = sb("ptb", [128, 32 * 256], BF16, pT)
            kpe2 = sb("kpe2", [64, TB], BF16, pT)
            attnh = sb("attnh", [128, TOWN], BF16, pT)
            q2s = sb("q2s", [128, TOWN], BF16, pT)
            k2s = sb("k2s", [128, TB], BF16, pT)
            negm = sb("negm", [1, TOWN], BF16, pT)
            linvB = sb("linvB", [128, TOWN], BF16, pT)
            rop1 = sb("rop1", [32, 512], F32, pT)
            rop2 = sb("rop2", [32, 512], F32, pT)
            rop3 = sb("rop3", [32, 512], F32, pT)
            psq = [ps("psq%d" % i, [128, 512], F32, pT) for i in range(2)]
            psq2 = ps("psq2", [32, 512], F32, pT)
            pss = [ps("pss%d" % i, [128, 256], F32, pT) for i in range(2)]
            psr = ps("psr", [1, 512], F32, pT)
            pstr = ps("pstr", [128, 512], BF16, pT)
            psav = psq[0]

            jw = join()
            lo1 = P.dma("sync", coso[:, :], cos_own, jw, SY[4])
            lo2 = P.dma("sync", sino[:, :], sin_own, jw, SY[4])
            prev_head = {}
            for h in range(NH):
                hb = h % 2
                wl1 = P.dma("gpsimd", wqbr[hb][:, :], wqb_t[h, :, :],
                            jw if h < 2 else [prev_head[h - 2]["qdone"]], gq)
                wl2 = P.dma("gpsimd", wkbr[hb][:, :], wkvb_t[h, :, :],
                            jw if h < 2 else [prev_head[h - 2]["kvdone"]], gq)

                qev = {}
                for th in range(2):
                    k = th
                    tm = None
                    for g in range(12):
                        w = [wl1, qlat_done]
                        if h > 0 and g == 0:
                            w.append(prev_head[h - 1]["s_done"])
                        tm = P.emit("tensor", lambda e, g=g, th=th, k=k, hb=hb: e.matmul(
                            psq[k][:, :],
                            wqbr[hb][:, g * 192:g * 192 + 128],
                            qlat[:, g * 1024 + th * 512:g * 1024 + th * 512 + 512],
                            start=(g == 0), stop=(g == 11)), w,
                            T if g == 11 else None)
                    qev[th] = P.emit("vector", lambda e, th=th, k=k: e.tensor_scalar(
                        out=qn[:, th * 512:th * 512 + 512], in0=psq[k][:, :],
                        scalar1=SCALE, scalar2=None, op0=ALU.mult), [tm], V)
                pe_ev = {}
                for th in range(2):
                    k = th
                    tm1 = None
                    for g in range(12):
                        w = [wl1, qlat_done, qev[th]]
                        tm1 = P.emit("tensor", lambda e, g=g, th=th, k=k, hb=hb: e.matmul(
                            psq[k][0:32, 0:512],
                            wqbr[hb][:, g * 192 + 128:g * 192 + 160],
                            qlat[:, g * 1024 + th * 512:g * 1024 + th * 512 + 512],
                            start=(g == 0), stop=(g == 11)), w,
                            T if g == 11 else None)
                    tm2 = None
                    for g in range(12):
                        w2_ = [tm1]
                        if g == 0 and th == 1:
                            w2_.append(pe_ev[0])
                        tm2 = P.emit("tensor", lambda e, g=g, th=th, k=k, hb=hb: e.matmul(
                            psq2[0:32, 0:512],
                            wqbr[hb][:, g * 192 + 160:g * 192 + 192],
                            qlat[:, g * 1024 + th * 512:g * 1024 + th * 512 + 512],
                            start=(g == 0), stop=(g == 11)), w2_,
                            T if g == 11 else None)
                    p1 = P.emit("vector", lambda e, th=th, k=k: e.tensor_tensor(
                        out=rop1[:, :], in0=psq[k][0:32, 0:512],
                        in1=coso[:, th * 512:th * 512 + 512], op=ALU.mult),
                        [tm1, tm2, lo1], V)
                    p2 = P.emit("vector", lambda e, th=th, k=k: e.tensor_tensor(
                        out=rop2[:, :], in0=psq2[0:32, 0:512],
                        in1=sino[:, th * 512:th * 512 + 512], op=ALU.mult),
                        [tm2, lo2], V)
                    p3 = P.emit("vector", lambda e: e.tensor_tensor(
                        out=rop3[:, :], in0=rop1[:, :], in1=rop2[:, :],
                        op=ALU.subtract), [p1, p2], V)
                    p4 = P.emit("vector", lambda e, th=th: e.tensor_scalar(
                        out=qpe[0:32, th * 512:th * 512 + 512], in0=rop3[:, :],
                        scalar1=SCALE, scalar2=None, op0=ALU.mult), [p3], V)
                    p5 = P.emit("vector", lambda e, th=th, k=k: e.tensor_tensor(
                        out=rop1[:, :], in0=psq[k][0:32, 0:512],
                        in1=sino[:, th * 512:th * 512 + 512], op=ALU.mult), [p3], V)
                    p6 = P.emit("vector", lambda e, th=th, k=k: e.tensor_tensor(
                        out=rop2[:, :], in0=psq2[0:32, 0:512],
                        in1=coso[:, th * 512:th * 512 + 512], op=ALU.mult), [p3], V)
                    p7 = P.emit("vector", lambda e: e.tensor_tensor(
                        out=rop3[:, :], in0=rop1[:, :], in1=rop2[:, :],
                        op=ALU.add), [p5, p6], V)
                    pe_ev[th] = P.emit("vector", lambda e, th=th: e.tensor_scalar(
                        out=qpe[32:64, th * 512:th * 512 + 512], in0=rop3[:, :],
                        scalar1=SCALE, scalar2=None, op0=ALU.mult), [p4, p7], V)
                q_done = [qev[0], qev[1], pe_ev[0], pe_ev[1]]

                kv_ev = []
                for which in range(2):
                    for sp2 in range(8):
                        k = sp2 % 2
                        tm = None
                        for g in range(4):
                            w = [wl2] + list(kvlat_done)
                            if which == 0 and sp2 == 0 and g == 0:
                                w += [pe_ev[0], pe_ev[1]]
                            if h > 0 and sp2 == 0 and g == 0:
                                w.append(prev_head[h - 1]["s_done"])
                            if len(kv_ev) >= 2 and g == 0:
                                w.append(kv_ev[-2])
                            tm = P.emit("tensor",
                                        lambda e, g=g, sp2=sp2, which=which, k=k,
                                        hb=hb: e.matmul(
                                            psq[k][:, :],
                                            wkbr[hb][:, g * 256 + which * 128:
                                                     g * 256 + which * 128 + 128],
                                            kvlat[:, g * TB + sp2 * 512:
                                                  g * TB + sp2 * 512 + 512],
                                            start=(g == 0), stop=(g == 3)), w,
                                        T if g == 3 else None)
                        dst = kn if which == 0 else vfm
                        kv_ev.append(P.emit(
                            "vector", lambda e, sp2=sp2, k=k, dst=dst: e.tensor_copy(
                                out=dst[:, sp2 * 512:sp2 * 512 + 512],
                                in_=psq[k][:, :]), [tm], V))
                kv_done_h = kv_ev[-1]
                def wrV(c0, nb, pbank, tokT):
                    def f(e, c0=c0, nb=nb):
                        return e.tensor_copy(
                            out=vtm[:, c0 * 128:c0 * 128 + nb * 128],
                            in_=pbank[:, 0:nb * 128])
                    return P.emit("vector", f, [tokT], V)
                vt_done = transposes(vfm, 32, wrV, kv_done_h, pstr, c_idbf,
                                     extra_src_toks=([prev_head[h - 1]["vt"]]
                                                     if h > 0 else []))

                prev_nr1 = prev_head[h - 1]["nr1"] if h > 0 else None
                q2a = P.emit("vector", lambda e: e.tensor_tensor(
                    out=q2s[:, :], in0=qn[:, :], in1=qn[:, :], op=ALU.mult),
                    q_done + ([prev_nr1] if prev_nr1 else []), V)
                qpe2 = P.emit("vector", lambda e: e.tensor_tensor(
                    out=k2s[0:64, 0:TOWN], in0=qpe[:, :], in1=qpe[:, :],
                    op=ALU.mult), q_done, V)
                q2r = None
                for th in range(2):
                    tmq = P.emit("tensor", lambda e, th=th: e.matmul(
                        psr[0:1, 0:512], ones_c[:, :],
                        q2s[:, th * 512:th * 512 + 512],
                        start=True, stop=False), [q2a, m2, q2r], None)
                    tmq = P.emit("tensor", lambda e, th=th: e.matmul(
                        psr[0:1, 0:512], ones_c[0:64, :],
                        k2s[0:64, th * 512:th * 512 + 512],
                        start=False, stop=True), [qpe2], T)
                    q2r = P.emit("vector", lambda e, th=th: e.tensor_copy(
                        out=rowtmp[0:1, th * 512:th * 512 + 512], in_=psr[0:1, 0:512]),
                        [tmq] + ([prev_nr1] if prev_nr1 else []), V)
                k2a = P.emit("vector", lambda e: e.tensor_tensor(
                    out=k2s[:, :], in0=kn[:, :], in1=kn[:, :], op=ALU.mult),
                    [kv_ev[7], q2r, qpe2], V)
                kp2 = None
                for sp2 in range(8):
                    kp2 = P.emit("vector", lambda e, sp2=sp2: e.tensor_tensor(
                        out=kpe2[:, sp2 * 512:(sp2 + 1) * 512],
                        in0=kpe[:, sp2 * 512:(sp2 + 1) * 512],
                        in1=kpe[:, sp2 * 512:(sp2 + 1) * 512], op=ALU.mult),
                        [kp2] if kp2 else list(kvlat_done), V)
                tmk = None
                for sp2 in range(8):
                    tmk = P.emit("tensor", lambda e, sp2=sp2: e.matmul(
                        psr[0:1, 0:512], ones_c[:, :],
                        k2s[:, sp2 * 512:sp2 * 512 + 512],
                        start=True, stop=False), [k2a, q2r, tmk], None)
                    tmk = P.emit("tensor", lambda e, sp2=sp2: e.matmul(
                        psr[0:1, 0:512], ones_c[0:64, :],
                        kpe2[:, sp2 * 512:sp2 * 512 + 512],
                        start=False, stop=True), [kp2], T)
                    tmk = P.emit("vector", lambda e, sp2=sp2: e.reduce_max(
                        rowtmp[0:1, 2048 + sp2:2048 + sp2 + 1], psr[0:1, 0:512],
                        axis=AX), [tmk], V)
                mk1 = P.emit("vector", lambda e: e.reduce_max(
                    rowtmp[0:1, 2056:2057], rowtmp[0:1, 2048:2056], axis=AX),
                    [tmk], V)
                mk2 = P.emit("scalar", lambda e: e.activation(
                    rowtmp[0:1, 2057:2058], rowtmp[0:1, 2056:2057], AF.Ln,
                    bias=cst[0:1, 2:3], scale=1.0), [mk1], SC)
                mk3 = P.emit("scalar", lambda e: e.activation(
                    rowtmp[0:1, 2057:2058], rowtmp[0:1, 2057:2058], AF.Exp,
                    bias=0.0, scale=0.5), [mk2], SC)
                mk4 = P.emit("vector", lambda e: e.tensor_scalar(
                    out=rowtmp[0:1, 2058:2059], in0=rowtmp[0:1, 2057:2058],
                    scalar1=-1.02, scalar2=None, op0=ALU.mult), [mk3], V)
                nr1 = P.emit("scalar", lambda e: e.activation(
                    rowtmp[0:1, 1024:2048], rowtmp[0:1, 0:1024], AF.Ln,
                    bias=cst[0:1, 2:3], scale=1.0), [q2r, mk4], SC)
                nr2 = P.emit("scalar", lambda e: e.activation(
                    rowtmp[0:1, 1024:2048], rowtmp[0:1, 1024:2048], AF.Exp,
                    bias=0.0, scale=0.5), [nr1], SC)
                nr3 = P.emit("vector", lambda e: e.tensor_scalar(
                    out=negm[0:1, :], in0=rowtmp[0:1, 1024:2048],
                    scalar1=rowtmp[0:1, 2058:2059], scalar2=None, op0=ALU.mult),
                    [nr2], V, fwaits=[mk4])
                if h == 0:
                    tap("negm_h0", negm[0:1, :], [nr3])

                s_done = None
                avev = {}
                lrow_ev = {}
                prev_out = prev_head[h - 1]["out"] if h > 0 else None
                prev_dl1 = prev_head[h - 1]["dl1"] if h > 0 else None
                for g5 in range(4):
                    KC = 8 * (g5 + 1)
                    av_tok = None
                    l_tok = None
                    exp_done = {}
                    for kc in range(KC):
                        kb = kc % 2
                        w = q_done + [kv_ev[kc // 4], nr3, kv_done_h]
                        if kc >= 2:
                            w.append(exp_done[kc - 2])
                        P.emit("tensor", lambda e, kc=kc, g5=g5, kb=kb: e.matmul(
                            pss[kb][:, :], kn[:, kc * 128:(kc + 1) * 128],
                            qn[:, g5 * 256:g5 * 256 + 256],
                            start=True, stop=False), w, None)
                        P.emit("tensor", lambda e, kc=kc, g5=g5, kb=kb: e.matmul(
                            pss[kb][:, :], kpe[:, kc * 128:(kc + 1) * 128],
                            qpe[:, g5 * 256:g5 * 256 + 256],
                            start=False, stop=False), [], None)
                        tm = P.emit("tensor", lambda e, kc=kc, g5=g5, kb=kb: e.matmul(
                            pss[kb][:, :], ones_r[:, :],
                            negm[0:1, g5 * 256:g5 * 256 + 256],
                            start=False, stop=True), [m1], T)
                        ex = P.emit("scalar", lambda e, kc=kc, kb=kb: e.activation(
                            out=ptb[:, kc * 256:(kc + 1) * 256], in_=pss[kb][:, :],
                            func=AF.Exp, bias=0.0, scale=1.0), [tm], SC)
                        mtoks = []
                        for dq in range(2):
                            i_t = 2 * g5 + dq
                            if 4 * i_t <= kc < 4 * i_t + 4:
                                O = 128 * kc - 512 * i_t
                                mtoks.append(P.emit(
                                    "vector",
                                    lambda e, kc=kc, dq=dq, O=O: e.scalar_tensor_tensor(
                                        out=ptb[:, kc * 256 + dq * 128:
                                                kc * 256 + dq * 128 + 128],
                                        in0=ptb[:, kc * 256 + dq * 128:
                                                kc * 256 + dq * 128 + 128],
                                        scalar=1.0,
                                        in1=c_strip[:, 384 - O:384 - O + 128],
                                        op0=ALU.bypass, op1=ALU.mult), [ex], V))
                            elif kc >= 4 * i_t + 4:
                                mtoks.append(P.emit(
                                    "vector", lambda e, kc=kc, dq=dq: e.memset(
                                        ptb[:, kc * 256 + dq * 128:
                                            kc * 256 + dq * 128 + 128], 0.0),
                                    [ex], V))
                        exp_done[kc] = mtoks[-1] if mtoks else ex
                        lw = [exp_done[kc], m2]
                        if kc == 0:
                            if g5 > 0:
                                lw.append(lrow_ev[g5 - 1])
                            elif prev_dl1 is not None:
                                lw.append(prev_dl1)
                        l_tok = P.emit("tensor", lambda e, kc=kc, KC=KC: e.matmul(
                            psr[0:1, 0:256], ones_c[:, :],
                            ptb[:, kc * 256:(kc + 1) * 256],
                            start=(kc == 0), stop=(kc == KC - 1)), lw,
                            T if kc == KC - 1 else None)
                        aw = [exp_done[kc], vt_done]
                        if kc == 0:
                            if g5 > 0:
                                aw.append(avev[g5 - 1])
                            elif prev_out is not None:
                                aw.append(prev_out)
                        av_tok = P.emit("tensor", lambda e, kc=kc, KC=KC: e.matmul(
                            psav[:, 0:256], vtm[:, kc * 128:(kc + 1) * 128],
                            ptb[:, kc * 256:(kc + 1) * 256],
                            start=(kc == 0), stop=(kc == KC - 1)), aw,
                            T if kc == KC - 1 else None)
                    avw = [av_tok]
                    if g5 == 0 and prev_out is not None:
                        avw.append(prev_out)
                    avev[g5] = P.emit("vector", lambda e, g5=g5: e.tensor_copy(
                        out=attnh[:, g5 * 256:(g5 + 1) * 256], in_=psav[:, 0:256]),
                        avw, V)
                    lvw = [l_tok]
                    if g5 == 0 and prev_dl1 is not None:
                        lvw.append(prev_dl1)
                    lrow_ev[g5] = P.emit("vector", lambda e, g5=g5: e.tensor_copy(
                        out=rowtmp[0:1, 3072 + g5 * 256:3072 + (g5 + 1) * 256],
                        in_=psr[0:1, 0:256]), lvw, V)
                    s_done = avev[g5]
                rc = P.emit("vector", lambda e: e.reciprocal(
                    rowtmp[0:1, 3072:3072 + 1024],
                    rowtmp[0:1, 3072:3072 + 1024]),
                    [lrow_ev[g] for g in range(4)], V)
                dl1 = P.dma("sync", rowbc[6:7, 0:1024], rowtmp[0:1, 3072:4096],
                            [rc], SY[4])
                dl2 = P.dma("gpsimd", linvB[:, :],
                            rowbc[6:7, 0:1024].partition_broadcast(128), [dl1], gq)
                nm = P.emit("vector", lambda e: e.tensor_tensor(
                    out=attnh[:, :], in0=attnh[:, :], in1=linvB[:, :],
                    op=ALU.mult), [dl2] + [avev[g] for g in range(4)], V)
                da = P.dma("sync", attn_scr[h * 128:(h + 1) * 128, :], attnh[:, :],
                           [nm], SY[4])
                if h == 0:
                    tap("qn_h0", qn[:, :], q_done)
                    tap("qpe_h0", qpe[:, :], q_done)
                    tap("kn_h0", kn[:, :], [kv_ev[7]])
                    tap("vtm_h0", vtm[:, :], [vt_done])
                    tap("pt_h0", ptb[:, :], [s_done])
                    tap("lrow_h0", rowtmp[0:1, 3072:4096],
                        [lrow_ev[g] for g in range(4)])
                    tap("attnh_h0", attnh[:, :], [nm])
                prev_head[h] = {"qdone": qev[1], "kvdone": kv_done_h,
                                "s_done": s_done, "out": da, "dl1": dl1,
                                "nr1": nr1, "vt": vt_done}
            attn_done = [prev_head[hh]["out"] for hh in range(NH)]
            tap("attn_all", attn_scr, attn_done)

        P.cur_phase = 6
        pZ = ExitStack()
        with pZ:
            attn_all = sb("attn_all", [128, 32 * 1024], BF16, pZ)
            wor = [sb("wor%d" % i, [128, 512], F32, pZ) for i in range(3)]
            wob = [sb("wob%d" % i, [128, 512], BF16, pZ) for i in range(2)]
            xsl = [sb("xsl%d" % i, [128, 512], F32, pZ) for i in range(2)]
            fsl = [sb("fsl%d" % i, [128, 512], F32, pZ) for i in range(2)]
            oub = [sb("oub%d" % i, [128, 512], F32, pZ) for i in range(2)]
            psO = [ps("psO%d" % i, [128, 512], F32, pZ) for i in range(8)]
            jw = join()
            srcv = attn_scr.rearrange("(k p) t -> p k t", p=128)
            dstv = attn_all[:, :].rearrange("p (k t) -> p k t", k=32)
            la_toks = []
            for kq in range(4):
                la_toks.append(P.dma("sync", dstv[:, kq * 8:(kq + 1) * 8, :],
                                     srcv[:, kq * 8:(kq + 1) * 8, :],
                                     attn_done + jw, SY[0]))
            cast_prev = None
            mm_done2 = {}
            ev_done = {}
            out_toks = []
            for osp in range(8):
                tmz = {}
                for K in range(32):
                    idx = osp * 32 + K
                    rb = idx % 3
                    qb = idx % 2
                    w = jw if idx == 0 else []
                    if idx >= 3:
                        w = [mm_done2[idx - 3]]
                    tl = P.dma("sync", wor[rb][:, :], wo_t[osp, K, :, :], w, SY[0])
                    cw = [tl]
                    if idx >= 2:
                        cw.append(mm_done2[idx - 2])
                    cast_prev = P.emit("scalar", lambda e, rb=rb, qb=qb: e.activation(
                        out=wob[qb][:, :], in_=wor[rb][:, :], func=AF.Copy),
                        cw, SC)
                    tm = None
                    for tt in range(8):
                        w2 = [cast_prev] + la_toks
                        if K == 0 and ev_done.get((osp - 1, tt)) is not None:
                            w2.append(ev_done[(osp - 1, tt)])
                        tm = P.emit("tensor", lambda e, K=K, tt=tt, qb=qb: e.matmul(
                            psO[tt][:, :],
                            attn_all[:, K * 1024 + tt * 128:K * 1024 + tt * 128 + 128],
                            wob[qb][:, :], start=(K == 0), stop=(K == 31)),
                            w2, T if (K == 31 or tt == 7) else None)
                        if K == 31:
                            tmz[tt] = tm
                    mm_done2[idx] = tm
                for tt in range(8):
                    b = tt % 2
                    xw = [out_toks[-2]] if len(out_toks) >= 2 else []
                    tx = P.dma("sync", xsl[b][:, :],
                               x_own[tt * 128:(tt + 1) * 128,
                                     osp * 512:(osp + 1) * 512], xw, SY[0])
                    e1 = P.emit("vector", lambda e, tt=tt, b=b: e.scalar_tensor_tensor(
                        out=oub[b][:, :], in0=psO[tt][:, :], scalar=1.0,
                        in1=xsl[b][:, :], op0=ALU.bypass, op1=ALU.add),
                        [tmz[tt], tx], V)
                    ev_done[(osp, tt)] = e1
                    tf = P.dma("sync",
                               fsl[b][:, :].rearrange("p (c t) -> p c t", c=4),
                               ffn_scr[:, :].rearrange("p (c t) -> p c t", c=32)
                               [:, 4 * osp:4 * osp + 4, tt * 128:(tt + 1) * 128],
                               ffn_done + xw, SY[0])
                    tm2 = None
                    for cj in range(4):
                        tm2 = P.emit("tensor", lambda e, cj=cj, tt=tt, b=b: e.transpose(
                            psO[tt][:, cj * 128:(cj + 1) * 128],
                            fsl[b][:, cj * 128:(cj + 1) * 128], c_idf[:, :]),
                            [tf, e1], T if cj == 3 else None)
                    e2 = P.emit("vector", lambda e, tt=tt, b=b: e.scalar_tensor_tensor(
                        out=oub[b][:, :], in0=oub[b][:, :], scalar=1.0,
                        in1=psO[tt][:, :], op0=ALU.bypass, op1=ALU.add),
                        [tm2], V)
                    ot = P.dma("sync", out[tt * 128:(tt + 1) * 128,
                                          osp * 512:(osp + 1) * 512],
                               oub[b][:, :], [e2], SY[0])
                    out_toks.append(ot)

        pM2.close()
        pM1.close()
        final_waits = [(c, c.n) for c in ALL]
        with nc.Block() as block:
            P.finalize(block, final_waits)
    return nc


def _prep_inputs(inputs):
    x = np.asarray(inputs["L0_attn_x"])
    wq_a = np.asarray(inputs["L0_attn_wq_a"])
    q_norm_w = np.asarray(inputs["L0_attn_q_norm_w"])
    wq_b = np.asarray(inputs["L0_attn_wq_b"])
    wkv_a = np.asarray(inputs["L0_attn_wkv_a"])
    kv_norm_w = np.asarray(inputs["L0_attn_kv_norm_w"])
    wkv_b = np.asarray(inputs["L0_attn_wkv_b"])
    wo = np.asarray(inputs["L0_attn_wo"])
    gamma = np.asarray(inputs["L0_ffn_gamma"])
    beta = np.asarray(inputs["L0_ffn_beta"])
    wf = np.asarray(inputs["L0_ffn_weight"])

    pe = np.concatenate([np.arange(0, DPE, 2), np.arange(1, DPE, 2)])

    wqaT = np.ascontiguousarray(wq_a.T)
    wqa_t = np.empty((3, 128, 32 * 512), np.float32)
    for MB in range(3):
        blk = wqaT[:, MB * 512:(MB + 1) * 512].reshape(32, 128, 512)
        wqa_t[MB] = blk.transpose(1, 0, 2).reshape(128, 32 * 512)

    wqbT = wq_b.T.reshape(QL, NH, DN + DPE)
    perm = np.concatenate([np.arange(DN), DN + pe])
    wqbT = wqbT[:, :, perm]
    wqb_t = np.empty((NH, 128, 12 * 192), np.float32)
    for h in range(NH):
        blk = wqbT[:, h, :].reshape(12, 128, 192)
        wqb_t[h] = blk.transpose(1, 0, 2).reshape(128, 12 * 192)

    wkvaT = wkv_a.T
    perm2 = np.concatenate([np.arange(KVL), KVL + pe])
    wkvaT = wkvaT[:, perm2]
    wkva_t = np.ascontiguousarray(
        wkvaT.reshape(32, 128, KVA).transpose(1, 0, 2).reshape(128, 32 * KVA))

    wkvbT = wkv_b.T.reshape(KVL, NH, DN + DV)
    wkvb_t = np.empty((NH, 128, 4 * 256), np.float32)
    for h in range(NH):
        blk = wkvbT[:, h, :].reshape(4, 128, 256)
        wkvb_t[h] = blk.transpose(1, 0, 2).reshape(128, 4 * 256)

    woT = np.ascontiguousarray(wo.T)
    wo_t = np.ascontiguousarray(
        woT.reshape(32, 128, 8, 512).transpose(2, 0, 1, 3))

    Pv = np.arange(128)
    pi = np.zeros(4096, np.int64)
    for g in range(32):
        pi[g * 128:(g + 1) * 128] = (Pv % 32) * 128 + 4 * g + Pv // 32
    wfT = np.ascontiguousarray(wf.T[pi, :])
    wf_t = np.ascontiguousarray(
        wfT.reshape(32, 128, 8, 512).transpose(2, 0, 1, 3))

    qnw_t = np.ascontiguousarray(q_norm_w.reshape(12, 128).T)
    kvnw_t = np.ascontiguousarray(kv_norm_w.reshape(4, 128).T)

    h128c = _hadamard(128).astype(BF)
    h32 = _hadamard(32).astype(BF)
    h32s = np.ascontiguousarray(np.concatenate([h32] * 4, axis=0))
    idbf = np.eye(128, dtype=np.float32).astype(BF)
    idf = np.eye(128, dtype=np.float32)

    freqs = 1.0 / 10000.0 ** (np.arange(0, DPE, 2, dtype=np.float32) / DPE)
    ang = np.outer(freqs, np.arange(TB, dtype=np.float32))
    cosc = np.ascontiguousarray(np.cos(ang).astype(np.float32))
    sinc = np.ascontiguousarray(np.sin(ang).astype(np.float32))

    in_maps = []
    own_rows_all = []
    for core in range(NCORE):
        b, a = divmod(core, 4)
        own = (128 * (4 * np.arange(8)[:, None] + a) +
               np.arange(128)[None, :]).reshape(-1)
        own_rows_all.append((b, own))
        u = np.arange(512)[None, :]
        p = np.arange(128)[:, None]
        stripc = (u >= p - 128 * a + 384).astype(np.float32).astype(BF)
        m = {
            "x_own": np.ascontiguousarray(x[b][own, :]),
            "x_cov": np.ascontiguousarray(x[b]),
            "wqa_t": wqa_t, "wqb_t": wqb_t, "wkva_t": wkva_t,
            "wkvb_t": wkvb_t, "wo_t": wo_t, "wf_t": wf_t,
            "qnw": qnw_t, "kvnw": kvnw_t,
            "gamma": np.ascontiguousarray(gamma.reshape(1, D)),
            "beta": np.ascontiguousarray(beta.reshape(1, D)),
            "h128": h128c, "h32s": h32s, "ident_bf": idbf, "ident_f32": idf,
            "strip": stripc,
            "cos_own": np.ascontiguousarray(cosc[:, own]),
            "sin_own": np.ascontiguousarray(sinc[:, own]),
            "cos_cov": cosc, "sin_cov": sinc,
        }
        in_maps.append(m)
    return in_maps, own_rows_all


def kernel(**inputs):
    in_maps, own_rows_all = _prep_inputs(inputs)
    nc = build(DEV_TAPS, MAX_PHASE)
    res = run_bass_kernel_spmd(nc, in_maps, core_ids=list(range(NCORE)))
    outp = np.zeros((2, TB, D), np.float32)
    for core in range(NCORE):
        b, own = own_rows_all[core]
        outp[b][own, :] = np.asarray(res.results[core]["out"])
    if DEV_TAPS:
        kernel.last_results = res.results
    return outp


# version 02
# speedup vs baseline: 1.3355x; 1.3355x over previous
import math
import numpy as np
import ml_dtypes
import concourse.bass as bass
import concourse.mybir as mybir
from concourse.bass_utils import run_bass_kernel_spmd

F32 = mybir.dt.float32
BF16 = mybir.dt.bfloat16
I32 = mybir.dt.int32
AF = mybir.ActivationFunctionType
ALU = mybir.AluOpType
AX = mybir.AxisListType.X

D = 4096
NH = 32
DN = 128
DPE = 64
DV = 128
QL = 1536
KVL = 512
KVA = KVL + DPE
TOWN = 1024
TB = 4096
SCALE = 1.0 / math.sqrt(DN + DPE)
EPS_RMS = 1e-6
EPS_LN = 1e-5
NCORE = 8

BF = ml_dtypes.bfloat16

DEV_TAPS = ()
MAX_PHASE = 99
SERIALIZE = False
_TAPS = {
    "xq_fm": ([128, 32 * 1024], BF16),
    "dbg_xn": ([128, 4096], F32),
    "dbg_mv": ([128, 2], F32),
    "dbg_rln": ([128, 1], F32),
    "dbg_gammaB": ([128, 4096], F32),
    "dbg_xt": ([128, 4096], F32),
    "dbg_xln": ([128, 4096], F32),
    "dbg_sf": ([128, 1], F32),
    "dbg_xqb": ([128, 4096], BF16),
    "xh": ([128, 32 * 1024], BF16),
    "outpre": ([128, 8 * 4096], BF16),
    "ffn": ([128, 32 * 1024], F32),
    "xfm": ([128, 32 * 1024], BF16),
    "qlat": ([128, 12 * 1024], BF16),
    "kvlat": ([128, 4 * 4096], BF16),
    "kpe": ([64, 4096], BF16),
    "qn_h0": ([128, 1024], BF16),
    "qpe_h0": ([64, 1024], BF16),
    "kn_h0": ([128, 4096], BF16),
    "vtm_h0": ([128, 4096], BF16),
    "negm_h0": ([1, 1024], BF16),
    "pt_h0": ([128, 32 * 256], BF16),
    "lrow_h0": ([1, 1024], F32),
    "attnh_h0": ([128, 1024], BF16),
    "attn_all": ([4096, 1024], BF16),
}


def _hadamard(n):
    h = np.array([[1.0]], dtype=np.float32)
    while h.shape[0] < n:
        h = np.block([[h, h], [h, -h]])
    return h


class Ctr:
    __slots__ = ("h", "n", "name", "is_lane")

    def __init__(self, h, name):
        self.h = h
        self.n = 0
        self.name = name
        self.is_lane = False


class DmaQ:

    def __init__(self, lanes):
        self.lanes = lanes
        self.idx = 0

    def next(self):
        K = len(self.lanes)
        i = self.idx
        self.idx += 1
        lane = self.lanes[i % K]
        pre = (lane, 16 * (i // K)) if i >= K else None
        lane.n = 16 * (i // K + 1)
        return lane, pre, (lane, lane.n)

    def final_tokens(self):
        return [(ln, ln.n) for ln in self.lanes]


class Prog:
    ENGS = ("sync", "vector", "scalar", "tensor", "gpsimd")

    def __init__(self, nc):
        self.nc = nc
        self.q = {e: [] for e in self.ENGS}
        self.state = {e: {} for e in self.ENGS}
        self.owner = {}
        self.cur_phase = 0
        self.max_phase = 99
        self.serialize = False
        self.all_ctrs = []
        self.tap_gate = []

    def ctr(self, stack, name, owner):
        h = stack.enter_context(self.nc.semaphore(name))
        c = Ctr(h, name)
        self.owner[id(c)] = owner
        return c

    def emit(self, eng, fn, waits=(), inc=None, amt=1, fwaits=()):
        if self.cur_phase > self.max_phase:
            return None
        if self.tap_gate:
            waits = list(waits) + self.tap_gate
        if self.serialize:
            waits = list(waits) + [(c, c.n) for c in self.all_ctrs]
        wl = []
        st = self.state[eng]
        for tok in fwaits:
            if tok is None:
                continue
            c, v = tok
            if v > 0:
                wl.append((c.h, v))
        for tok in waits:
            if tok is None:
                continue
            c, v = tok
            if v <= 0:
                continue
            if (self.owner.get(id(c)) == eng and not c.is_lane
                    and (c.n - v) > 12):
                continue
            if st.get(id(c), 0) >= v:
                continue
            st[id(c)] = v
            wl.append((c.h, v))
        tok = None
        if inc is not None:
            inc.n += amt
            tok = (inc, inc.n)
        self.q[eng].append((wl, fn, ((inc.h if inc is not None else None), amt)))
        return tok

    def dma(self, eng, out, in_, waits=(), inc=None):
        def fn(e, out=out, in_=in_):
            return e.dma_start(out=out, in_=in_)
        if isinstance(inc, DmaQ):
            if self.cur_phase > self.max_phase:
                return None
            lane, pre, tok = inc.next()
            w = list(waits)
            if pre is not None:
                w.append(pre)
            self.emit(eng, fn, w, lane, 16, fwaits=())
            lane.n = tok[1]
            return tok
        return self.emit(eng, fn, waits, inc, 16)

    def finalize(self, block, final_waits):
        def runner(e, name):
            for wl, fn, (semh, amt) in self.q[name]:
                for h, v in wl:
                    e.wait_ge(h, v)
                ins = fn(e)
                if semh is not None and ins is not None:
                    ins.then_inc(semh, amt)
            if name == "sync":
                for c, v in final_waits:
                    e.wait_ge(c.h, v)
        block.sync(lambda e: runner(e, "sync"))
        block.vector(lambda e: runner(e, "vector"))
        block.scalar(lambda e: runner(e, "scalar"))
        block.tensor(lambda e: runner(e, "tensor"))
        block.gpsimd(lambda e: runner(e, "gpsimd"))


def build(taps=(), max_phase=99):
    from contextlib import ExitStack
    nc = bass.Bass("TRN2", target_bir_lowering=False, debug=False)

    def din(name, shape, dt=F32):
        return nc.dram_tensor(name, list(shape), dt, kind="ExternalInput").ap()

    x_own = din("x_own", [TOWN, D])
    wqa_t = din("wqa_t", [3, 128, 32 * 512])
    wqb_t = din("wqb_t", [NH, 128, 12 * 192])
    wkva_t = din("wkva_t", [128, 32 * KVA])
    wkvb_t = din("wkvb_t", [NH, 128, 4 * 256])
    wo_t = din("wo_t", [8, 32, 128, 512])
    wf_t = din("wf_t", [8, 32, 128, 512])
    x_cov = din("x_cov", [TB, D])
    qnw = din("qnw", [128, 12])
    kvnw = din("kvnw", [128, 4])
    gamma = din("gamma", [1, D])
    beta = din("beta", [1, D])
    h128 = din("h128", [128, 128], BF16)
    h32s = din("h32s", [128, 32], BF16)
    ident_bf = din("ident_bf", [128, 128], BF16)
    ident_f32 = din("ident_f32", [128, 128], F32)
    strip = din("strip", [128, 512], BF16)
    cos_own = din("cos_own", [32, TOWN])
    sin_own = din("sin_own", [32, TOWN])
    cos_cov = din("cos_cov", [32, TB])
    sin_cov = din("sin_cov", [32, TB])

    out = nc.dram_tensor("out", [TOWN, D], F32, kind="ExternalOutput").ap()

    bounce = nc.dram_tensor("bounce", [128, 32 * 1024], BF16, kind="Internal").ap()
    ffn_scr = nc.dram_tensor("ffn_scr", [128, 32 * 1024], F32, kind="Internal").ap()
    attn_scr = nc.dram_tensor("attn_scr", [NH * 128, TOWN], BF16, kind="Internal").ap()
    rowbc = nc.dram_tensor("rowbc", [8, 4096], F32, kind="Internal").ap()

    tap_t = {}
    for name in taps:
        shape, dt = _TAPS[name]
        tap_t[name] = nc.dram_tensor("tap_" + name, list(shape), dt,
                                     kind="ExternalOutput").ap()

    st = ExitStack()
    with st:
        P = Prog(nc)
        P.max_phase = max_phase
        import sys as _sys
        P.serialize = bool(globals().get("SERIALIZE")) or SERIALIZE
        T = P.ctr(st, "T", "tensor")
        V = P.ctr(st, "V", "vector")
        SC = P.ctr(st, "SC", "scalar")
        G = P.ctr(st, "G", "gpsimd")
        sy_lanes = [P.ctr(st, "SYL%d" % i, "sync") for i in range(8)]
        gq_lanes = [P.ctr(st, "GQL%d" % i, "gpsimd") for i in range(4)]
        for _ln in sy_lanes + gq_lanes:
            _ln.is_lane = True
        syq = DmaQ(sy_lanes)
        gq = DmaQ(gq_lanes)
        SY = [syq] * 6
        ALL = [T, V, SC, G] + sy_lanes + gq_lanes
        P.all_ctrs = ALL

        def sb(name, shape, dt, stack=None, side=None):
            return (stack or st).enter_context(
                nc.sbuf_tensor(name, list(shape), dt, side=side))

        def ps(name, shape, dt, stack=None):
            return (stack or st).enter_context(nc.psum_tensor(name, list(shape), dt))

        def join():
            return [(c, c.n) for c in ALL]

        def tap(name, src_ap, waits):
            if name in tap_t:
                tok = P.dma("sync", tap_t[name], src_ap, waits, SY[5])
                if tok is not None:
                    P.tap_gate.append(tok)

        c_h128 = sb("c_h128", [128, 128], BF16)
        c_h32s = sb("c_h32s", [128, 32], BF16)
        c_idbf = sb("c_idbf", [128, 128], BF16)
        c_idf = sb("c_idf", [128, 128], F32)
        c_strip = sb("c_strip", [128, 512], BF16)
        c_qnw = sb("c_qnw", [128, 12], F32)
        c_kvnw = sb("c_kvnw", [128, 4], F32)
        ones_r = sb("ones_r", [1, 128], BF16)
        ones_c = sb("ones_c", [128, 1], BF16)
        onesf = sb("onesf", [128, 1], F32)
        one1 = sb("one1", [1, 1], F32)
        wsq_col = sb("wsq_col", [128, 12], F32)
        wabs = sb("wabs", [128, 258], F32)
        thr_c = sb("thr_c", [128, 1], F32)
        nthr_c = sb("nthr_c", [128, 1], F32)
        wsc_c = sb("wsc_c", [128, 1], F32)
        i2t_c = sb("i2t_c", [128, 1], F32)
        rowtmp = sb("rowtmp", [1, 4096], F32)
        coltmp = sb("coltmp", [128, 2], F32)
        cst = sb("cst", [128, 3], F32)

        ld0 = []
        for dst, src in ((c_h128, h128), (c_h32s, h32s), (c_idbf, ident_bf),
                         (c_idf, ident_f32), (c_strip, strip), (c_qnw, qnw),
                         (c_kvnw, kvnw)):
            ld0.append(P.dma("sync", dst[:, :], src, (), SY[0]))
        m1 = P.emit("vector", lambda e: e.memset(ones_r[:, :], 1.0), (), V)
        m2 = P.emit("vector", lambda e: e.memset(ones_c[:, :], 1.0), (), V)
        m3 = P.emit("vector", lambda e: e.memset(onesf[:, :], 1.0), (), V)
        m4 = P.emit("vector", lambda e: e.memset(one1[:, :], 1.0), (), V)
        m5a = P.emit("vector", lambda e: e.memset(cst[:, 0:1], EPS_LN), (), V)
        m5b = P.emit("vector", lambda e: e.memset(cst[:, 1:2], EPS_RMS), (), V)
        m5c = P.emit("vector", lambda e: e.memset(cst[:, 2:3], 1e-30), (), V)
        consts_tok0 = [m5a, m5b, m5c]
        consts_tok = ld0 + [m1, m2, m3, m4]

        def transposes(src_tile, n_c, dst_write, tok_src, pbank, ident,
                       extra_src_toks=()):
            last = None
            for c0 in range(0, n_c, 4):
                nb = min(4, n_c - c0)
                tm = None
                for k in range(nb):
                    c = c0 + k
                    w = [tok_src] + list(extra_src_toks)
                    if last is not None:
                        w.append(last)
                    tm = P.emit("tensor", lambda e, c=c, k=k: e.transpose(
                        pbank[:, k * 128:(k + 1) * 128],
                        src_tile[:, c * 128:(c + 1) * 128], ident[:, :]),
                        w, T if k == nb - 1 else None)
                last = dst_write(c0, nb, pbank, tm)
            return last

        def row_to_col(row_ap, psc_out, dst_col, waits):
            tm = P.emit("tensor", lambda e: e.matmul(
                psc_out, row_ap, one1[:, :], start=True, stop=True),
                list(waits), T)
            return P.emit("vector", lambda e: e.tensor_copy(out=dst_col, in_=psc_out),
                          [tm], V)

        P.cur_phase = 1
        sXQ = ExitStack()
        xq_fm = sb("xq_fm", [128, 32 * 1024], BF16, sXQ, side="right")
        pA = ExitStack()
        with pA:
            xt = [sb("xtA%d" % i, [128, D], F32, pA) for i in range(2)]
            gammaB = sb("gammaB", [128, D], F32, pA)
            betaB = sb("betaB", [128, D], F32, pA)
            bn6 = sb("bn6", [128, 48], F32, pA)
            mv = sb("mv", [128, 2], F32, pA)
            rln_c = sb("rln_c", [128, 1], F32, pA)
            nmu_c = sb("nmu_c", [128, 1], F32, pA)
            sf_c = sb("sf_c", [128, 1], F32, pA)
            s1 = sb("s1A", [128, D], F32, pA)
            xqi = sb("xqi", [128, D // 2], I32, pA)
            xqb = sb("xqb", [128, D], BF16, pA)
            ptrA = ps("ptrA", [128, 512], BF16, pA)

            gb = P.dma("sync", gammaB[:, :], gamma.partition_broadcast(128), (), SY[0])
            bb = P.dma("sync", betaB[:, :], beta.partition_broadcast(128), (), SY[0])

            ev = None
            tg_prev = None
            for i in range(8):
                b = i % 2
                tl = P.dma("sync", xt[b][:, :], x_own[i * 128:(i + 1) * 128, :],
                           [ev] if i >= 2 else [], SY[0])
                t1 = None
                for kk in range(8):
                    t1 = P.emit("vector", lambda e, b=b, kk=kk: e.bn_stats(
                        bn6[:, kk * 6:(kk + 1) * 6],
                        xt[b][:, kk * 512:(kk + 1) * 512]), [tl, tg_prev], V)
                t2 = P.emit("vector", lambda e: e.bn_aggr(mv[:, :], bn6[:, :]), [t1], V)
                t3 = P.emit("scalar", lambda e: e.activation(
                    rln_c[:, :], mv[:, 1:2], AF.Ln, bias=cst[:, 0:1], scale=1.0), [t2], SC)
                t4 = P.emit("scalar", lambda e: e.activation(
                    rln_c[:, :], rln_c[:, :], AF.Exp, bias=0.0, scale=-0.5), [t3], SC)
                t5 = P.emit("vector", lambda e: e.tensor_scalar(
                    out=nmu_c[:, :], in0=mv[:, 0:1], scalar1=-1.0, scalar2=None,
                    op0=ALU.mult), [t2], V)
                t7 = P.emit("vector", lambda e, b=b: e.tensor_scalar(
                    out=s1[:, :], in0=xt[b][:, :], scalar1=nmu_c[:, :],
                    scalar2=rln_c[:, :], op0=ALU.add, op1=ALU.mult), [t4], V,
                    fwaits=[t5])
                if i == 0:
                    tap("dbg_xn", s1[:, :], [t7])
                    tap("dbg_mv", mv[:, :], [t2])
                    tap("dbg_rln", rln_c[:, :], [t4])
                    tap("dbg_gammaB", gammaB[:, :], [gb])
                    tap("dbg_xt", xt[0][:, :], [tl])
                t8 = P.emit("gpsimd", lambda e: e.tensor_tensor(
                    out=s1[:, :], in0=s1[:, :], in1=gammaB[:, :], op=ALU.mult),
                    [t7, gb], G)
                t9 = P.emit("gpsimd", lambda e: e.tensor_tensor(
                    out=s1[:, :], in0=s1[:, :], in1=betaB[:, :], op=ALU.add),
                    [t8, bb], G)
                ta = P.emit("vector", lambda e: e.reduce_max(
                    sf_c[:, :], s1[:, :], axis=AX, apply_absolute_value=True), [t9], V)
                tb = P.emit("vector", lambda e: e.tensor_scalar(
                    out=coltmp[:, 0:1], in0=sf_c[:, :], scalar1=1e-6, scalar2=1e6,
                    op0=ALU.max, op1=ALU.min), [ta], V)
                tc = P.emit("vector", lambda e: e.reciprocal(
                    coltmp[:, 1:2], coltmp[:, 0:1]), [tb], V)
                td = P.emit("vector", lambda e: e.tensor_scalar(
                    out=coltmp[:, 0:1], in0=coltmp[:, 1:2], scalar1=7.0, scalar2=1e6,
                    op0=ALU.mult, op1=ALU.min), [tc], V)
                te = P.emit("vector", lambda e: e.tensor_scalar(
                    out=sf_c[:, :], in0=coltmp[:, 0:1], scalar1=1e-6, scalar2=None,
                    op0=ALU.max), [td], V)
                tg = None
                for hh in range(2):
                    tf = P.emit("vector", lambda e, hh=hh: e.tensor_scalar(
                        out=xqi[:, :], in0=s1[:, hh * 2048:(hh + 1) * 2048],
                        scalar1=sf_c[:, :], scalar2=None,
                        op0=ALU.mult), [tg], V, fwaits=[te])
                    tg = P.emit("vector", lambda e, hh=hh: e.tensor_copy(
                        out=xqb[:, hh * 2048:(hh + 1) * 2048], in_=xqi[:, :]),
                        [tf], V)
                tg_prev = tg
                if i == 0:
                    tap("dbg_xln", s1[:, :], [t9])
                    tap("dbg_sf", sf_c[:, :], [te])
                    tap("dbg_xqb", xqb[:, :], [tg])

                def wrA(c0, nb, pbank, tokT, i=i):
                    def f(e, c0=c0, nb=nb, i=i):
                        dst = xq_fm[:, :].rearrange("p (c t) -> p c t", c=32)
                        return e.tensor_copy(
                            out=dst[:, c0:c0 + nb, i * 128:(i + 1) * 128],
                            in_=pbank[:, 0:nb * 128])
                    return P.emit("vector", f, [tokT], V)
                ev = transposes(xqb, 32, wrA, tg, ptrA, c_idbf,
                                extra_src_toks=consts_tok)
            xqfm_done = ev
            tap("xq_fm", xq_fm[:, :], [xqfm_done])

        P.cur_phase = 2
        pW = ExitStack()
        with pW:
            wt = [sb("wp1_%d" % i, [128, 2048], F32, pW) for i in range(2)]
            wp1dump = sb("wp1dump", [128, 512], F32, pW)
            vts = {}
            jwW = join()
            for idx in range(64):
                b = idx % 2
                osp, g4 = divmod(idx, 8)
                srcv = wf_t[osp, g4 * 4:(g4 + 1) * 4, :, :].rearrange("g p o -> p g o")
                w = [vts[idx - 2]] if idx >= 2 else list(jwW)
                tl = P.dma("sync", wt[b][:, :].rearrange("p (g o) -> p g o", g=4),
                           srcv, w, SY[1])
                vt2 = vts.get(idx - 1)
                for kq in range(4):
                    vt2 = P.emit("scalar", lambda e, b=b, idx=idx, kq=kq: e.activation(
                        out=wp1dump[:, :], in_=wt[b][:, kq * 512:(kq + 1) * 512],
                        func=AF.Abs,
                        accum_out=wabs[:, idx * 4 + kq:idx * 4 + kq + 1]),
                        [tl, vt2], SC)
                vts[idx] = vt2
            tr = P.emit("vector", lambda e: e.reduce_sum(
                wabs[:, 256:257], wabs[:, 0:256], axis=AX),
                [vts[i] for i in range(64)], V)
            psW = ps("psW", [1, 8], F32, pW)
            tm = P.emit("tensor", lambda e: e.matmul(
                psW[0:1, 0:1], wabs[:, 256:257], onesf[:, :], start=True, stop=True),
                [tr, m3], T)
            tv = P.emit("vector", lambda e: e.tensor_scalar(
                out=rowtmp[0:1, 0:1], in0=psW[0:1, 0:1],
                scalar1=0.5 / float(D * D), scalar2=None, op0=ALU.mult), [tm], V)
            tv2 = P.emit("vector", lambda e: e.tensor_scalar(
                out=rowtmp[0:1, 1:2], in0=rowtmp[0:1, 0:1], scalar1=-1.0,
                scalar2=None, op0=ALU.mult), [tv], V)
            tv3 = P.emit("vector", lambda e: e.tensor_scalar(
                out=rowtmp[0:1, 2:3], in0=rowtmp[0:1, 0:1], scalar1=2.0,
                scalar2=None, op0=ALU.mult), [tv], V)
            tv4 = P.emit("vector", lambda e: e.reciprocal(
                rowtmp[0:1, 3:4], rowtmp[0:1, 2:3]), [tv3], V)
            td1 = P.dma("sync", rowbc[0:1, 0:4], rowtmp[0:1, 0:4], [tv2, tv3, tv4],
                        SY[1])
            td2 = P.dma("sync", thr_c[:, :], rowbc[0:1, 0:1].partition_broadcast(128),
                        [td1], SY[1])
            td3 = P.dma("sync", nthr_c[:, :], rowbc[0:1, 1:2].partition_broadcast(128),
                        [td1], SY[1])
            td4 = P.dma("sync", wsc_c[:, :], rowbc[0:1, 2:3].partition_broadcast(128),
                        [td1], SY[1])
            td5 = P.dma("sync", i2t_c[:, :], rowbc[0:1, 3:4].partition_broadcast(128),
                        [td1], SY[1])
            wp1_done = [td2, td3, td4, td5]

        P.cur_phase = 2.2
        pF = ExitStack()
        with pF:
            y1 = sb("y1", [128, 32 * 1024], BF16, pF)
            psA = [ps("psFA%d" % i, [128, 512], F32, pF) for i in range(2)]
            evh = {}
            jwF = join()
            import os as _os
            _ffa = int(_os.environ.get("FFA_C", "32"))
            for c in range(_ffa):
                for th in range(2):
                    k = (c * 2 + th) % 2
                    w = [xqfm_done] + (jwF if (c == 0 and th == 0) else [])
                    if (c * 2 + th) >= 2:
                        w.append(evh[c * 2 + th - 2])
                    tm = P.emit("tensor", lambda e, c=c, th=th, k=k: e.matmul(
                        psA[k][:, :], c_h128[:, :],
                        xq_fm[:, c * 1024 + th * 512:c * 1024 + th * 512 + 512],
                        start=True, stop=True), w, T)
                    evh[c * 2 + th] = P.emit(
                        "vector", lambda e, c=c, th=th, k=k: e.tensor_copy(
                            out=y1[:, c * 1024 + th * 512:c * 1024 + th * 512 + 512],
                            in_=psA[k][:, :]), [tm], V)
            y1_done = evh[_ffa * 2 - 1]
            P.cur_phase = 2.22
            tb1 = P.dma("sync", bounce[:, :], y1[:, :], [y1_done], SY[2])
        sXQ.close()

        P.cur_phase = 2.3
        pF2 = ExitStack()
        with pF2:
            xh = sb("xh", [128, 32 * 1024], BF16, pF2)
            pY1T = ExitStack()
            y1t = sb("y1t", [128, 32 * 1024], BF16, pY1T)
            tb2 = P.dma("sync", y1t[:, :].rearrange("p (g t) -> p g t", g=32),
                        bounce.rearrange("(g j) (c t) -> (j c) g t", j=4, c=32),
                        [tb1], SY[2])
            psB = [ps("psFB%d" % i, [128, 512], F32, pY1T) for i in range(2)]
            evh = {}
            for g in range(32):
                for tsp in range(2):
                    gi = g * 2 + tsp
                    k = gi % 2
                    tm = None
                    for j in range(4):
                        w = [tb2]
                        if gi >= 2:
                            w.append(evh[gi - 2])
                        tm = P.emit("tensor", lambda e, g=g, tsp=tsp, j=j, k=k: e.matmul(
                            psB[k][32 * j:32 * j + 32, 0:512],
                            c_h32s[32 * j:32 * j + 32, :],
                            y1t[32 * j:32 * j + 32,
                                g * 1024 + tsp * 512:g * 1024 + tsp * 512 + 512],
                            start=True, stop=True, tile_position=(32 * j, 32 * j)),
                            w, T if j == 3 else None)
                    evh[gi] = P.emit("vector", lambda e, g=g, tsp=tsp, k=k: e.tensor_copy(
                        out=xh[:, g * 1024 + tsp * 512:g * 1024 + tsp * 512 + 512],
                        in_=psB[k][:, :]), [tm], V)
            xh_done = evh[63]
            tap("xh", xh[:, :], [xh_done])
            pY1T.close()

            P.cur_phase = 2.4
            pOPF = ExitStack()
            opfm = sb("opfm", [128, 32 * 1024], BF16, pOPF, side="right")
            pMAIN = ExitStack()
            outsp = sb("outsp", [128, 8 * 512], BF16, pMAIN)
            wfr = [sb("wfr%d" % i, [128, 512], F32, pMAIN) for i in range(3)]
            wqg = [sb("wqg%d" % i, [128, 512], BF16, pMAIN) for i in range(2)]
            qi32v = sb("qi32v", [128, 512], I32, pMAIN)
            qi32g = sb("qi32g", [128, 512], I32, pMAIN)
            psM = [ps("psM%d" % i, [128, 512], F32, pMAIN) for i in range(8)]
            qt = {}
            mm_last = {}
            tr_done = [None] * 8
            op_done = [None] * 8
            for osp in range(8):
                for g in range(32):
                    idx = osp * 32 + g
                    rb = idx % 3
                    qb = idx % 2
                    w = list(wp1_done)
                    if idx < 3:
                        w.append(xh_done)
                    if idx >= 3:
                        w.append(qt[idx - 3])
                    tl = P.dma("sync", wfr[rb][:, :], wf_t[osp, g, :, :], w, SY[2])
                    qe, qc = (("vector", V) if idx % 2 == 0 else ("gpsimd", G))
                    qa = qi32v if idx % 2 == 0 else qi32g
                    wq_war = [mm_last[idx - 2]] if idx >= 2 else []
                    g1 = P.emit(qe, lambda e, rb=rb, qa=qa: e.tensor_scalar(
                        out=qa[:, :], in0=wfr[rb][:, :], scalar1=i2t_c[:, :],
                        scalar2=None, op0=ALU.mult), [tl] + wq_war, qc)
                    qt[idx] = P.emit(qe, lambda e, qb=qb, qa=qa: e.tensor_scalar(
                        out=wqg[qb][:, :], in0=qa[:, :], scalar1=1.0, scalar2=-1.0,
                        op0=ALU.min, op1=ALU.max), [g1], qc)
                    tm = None
                    for tt in range(8):
                        w = [qt[idx], xh_done]
                        if g == 0:
                            if tr_done[tt] is not None:
                                w.append(tr_done[tt])
                        tm = P.emit("tensor", lambda e, g=g, tt=tt, qb=qb: e.matmul(
                            psM[tt][:, :],
                            xh[:, g * 1024 + tt * 128:g * 1024 + tt * 128 + 128],
                            wqg[qb][:, :], start=(g == 0), stop=(g == 31)),
                            w, T if (g == 31 or tt == 7) else None)
                    mm_last[idx] = tm
                for tt in range(8):
                    ev = P.emit("vector", lambda e, tt=tt: e.tensor_scalar(
                        out=outsp[:, tt * 512:(tt + 1) * 512],
                        in0=psM[tt][:, :], scalar1=wsc_c[:, :],
                        scalar2=None, op0=ALU.mult),
                        [mm_last[osp * 32 + 31]] + ([op_done[tt]] if op_done[tt] else []),
                        V)
                    tm2 = None
                    for cj in range(4):
                        tm2 = P.emit("tensor", lambda e, tt=tt, cj=cj: e.matmul(
                            psM[tt][:, cj * 128:(cj + 1) * 128],
                            outsp[:, tt * 512 + cj * 128:tt * 512 + cj * 128 + 128],
                            c_idbf[:, :], start=True, stop=True),
                            [ev], T if cj == 3 else None)
                    tr_done[tt] = tm2
                    def fop(e, tt=tt, osp=osp):
                        dst = opfm[:, :].rearrange("p (c t) -> p c t", c=32)
                        return e.tensor_copy(
                            out=dst[:, osp * 4:osp * 4 + 4, tt * 128:(tt + 1) * 128],
                            in_=psM[tt][:, :])
                    op_done[tt] = P.emit("vector", fop, [tm2], V)
            opfm_done = [t for t in op_done]
            tap("outpre", opfm[:, :], opfm_done)
            pMAIN.close()

        P.cur_phase = 2.5
        pF3 = ExitStack()
        with pF3:
            y2 = sb("y2", [128, 32 * 1024], BF16, pF3)
            psA2 = [ps("psA2%d" % i, [128, 512], F32, pF3) for i in range(2)]
            evh = {}
            for c in range(32):
                for th in range(2):
                    gi = c * 2 + th
                    k = gi % 2
                    w = list(opfm_done)
                    if gi >= 2:
                        w.append(evh[gi - 2])
                    tm = P.emit("tensor", lambda e, c=c, th=th, k=k: e.matmul(
                        psA2[k][:, :], c_h128[:, :],
                        opfm[:, c * 1024 + th * 512:c * 1024 + th * 512 + 512],
                        start=True, stop=True), w, T)
                    evh[gi] = P.emit("vector", lambda e, c=c, th=th, k=k: e.tensor_copy(
                        out=y2[:, c * 1024 + th * 512:c * 1024 + th * 512 + 512],
                        in_=psA2[k][:, :]), [tm], V)
            tb3 = P.dma("sync", bounce[:, :], y2[:, :], [evh[63]], SY[2])
        pOPF.close()
        P.cur_phase = 2.6
        pF4 = ExitStack()
        with pF4:
            y2t = sb("y2t", [128, 32 * 1024], BF16, pF4)
            tb4 = P.dma("sync", y2t[:, :].rearrange("p (g t) -> p g t", g=32),
                        bounce.rearrange("(g j) (c t) -> (j c) g t", j=4, c=32),
                        [tb3], SY[2])
            psB2 = [ps("psB2%d" % i, [128, 512], F32, pF4) for i in range(2)]
            sbf = [sb("sbf%d" % i, [128, 512], F32, pF4) for i in range(2)]
            dstw = ffn_scr.rearrange("(gg j) (cc t) -> (j cc) gg t", j=4, cc=32)
            dvh = {}
            for g in range(32):
                for tsp in range(2):
                    gi = g * 2 + tsp
                    k = gi % 2
                    tm = None
                    for j in range(4):
                        w = [tb4]
                        if gi >= 2:
                            w.append(dvh[gi - 2])
                        tm = P.emit("tensor", lambda e, g=g, tsp=tsp, j=j, k=k: e.matmul(
                            psB2[k][32 * j:32 * j + 32, 0:512],
                            c_h32s[32 * j:32 * j + 32, :],
                            y2t[32 * j:32 * j + 32,
                                g * 1024 + tsp * 512:g * 1024 + tsp * 512 + 512],
                            start=True, stop=True, tile_position=(32 * j, 32 * j)),
                            w, T if j == 3 else None)
                    ev = P.emit("vector", lambda e, k=k: e.tensor_copy(
                        out=sbf[k][:, :], in_=psB2[k][:, :]), [tm], V)
                    dvh[gi] = P.dma("sync", dstw[:, g, tsp * 512:tsp * 512 + 512],
                                    sbf[k][:, :], [ev], SY[2])
            ffn_done = list(dvh.values())
            tap("ffn", ffn_scr, ffn_done)

        P.cur_phase = 3
        pM1 = ExitStack()
        qlat = sb("qlat", [128, 12 * TOWN], BF16, pM1)
        pQ = ExitStack()
        with pQ:
            murln_row = sb("murln_row", [1, TOWN], F32, pQ)
            rln_row = sb("rln_row", [1, TOWN], F32, pQ)
            xfm = sb("xfm", [128, 32 * 1024], BF16, pQ)
            xt2 = [sb("xtB%d" % i, [128, D], F32, pQ) for i in range(2)]
            bn6b = sb("bn6b", [128, 48], F32, pQ)
            mvb = sb("mvb", [128, 2], F32, pQ)
            rlnb_c = sb("rlnb_c", [128, 1], F32, pQ)
            nmub_c = sb("nmub_c", [128, 1], F32, pQ)
            mnrb_c = sb("mnrb_c", [128, 1], F32, pQ)
            prowB = ps("prowB", [1, 512], F32, pQ)
            prowB2 = ps("prowB2", [1, 512], F32, pQ)
            ptrB = ps("ptrB", [128, 512], F32, pQ)
            jw = join()
            ev = None
            rtok = None
            he1 = he2 = None
            t6hist = {}
            for i in range(8):
                b = i % 2
                tl = P.dma("sync", xt2[b][:, :], x_own[i * 128:(i + 1) * 128, :],
                           (jw if i < 2 else [ev, t6hist[i - 2]]), SY[3])
                t1 = None
                for kk in range(8):
                    t1 = P.emit("vector", lambda e, b=b, kk=kk: e.bn_stats(
                        bn6b[:, kk * 6:(kk + 1) * 6],
                        xt2[b][:, kk * 512:(kk + 1) * 512]), [tl], V)
                t2 = P.emit("vector", lambda e: e.bn_aggr(mvb[:, :], bn6b[:, :]),
                            [t1], V)
                t3 = P.emit("scalar", lambda e: e.activation(
                    rlnb_c[:, :], mvb[:, 1:2], AF.Ln, bias=cst[:, 0:1], scale=1.0), [t2], SC)
                t4 = P.emit("scalar", lambda e: e.activation(
                    rlnb_c[:, :], rlnb_c[:, :], AF.Exp, bias=0.0, scale=-0.5), [t3], SC)
                t5 = P.emit("vector", lambda e: e.tensor_scalar(
                    out=nmub_c[:, :], in0=mvb[:, 0:1], scalar1=-1.0, scalar2=None,
                    op0=ALU.mult), [t2], V)
                t6 = P.emit("vector", lambda e: e.tensor_scalar(
                    out=mnrb_c[:, :], in0=nmub_c[:, :], scalar1=rlnb_c[:, :],
                    scalar2=None, op0=ALU.mult), [t4, t5], V)
                t6hist[i] = t6
                half, col = divmod(i, 4)
                hw = [t6] + ([rtok] if rtok else [])
                if col == 0 and half == 1:
                    hw += [he1, he2]
                r1 = P.emit("tensor", lambda e, col=col: e.matmul(
                    prowB[0:1, col * 128:(col + 1) * 128], mnrb_c[:, :], c_idf[:, :],
                    start=True, stop=True), hw, T)
                rtok = P.emit("tensor", lambda e, col=col: e.matmul(
                    prowB2[0:1, col * 128:(col + 1) * 128], rlnb_c[:, :], c_idf[:, :],
                    start=True, stop=True), [t4, r1], T)
                if col == 3:
                    he1 = P.emit("vector", lambda e, half=half: e.tensor_copy(
                        out=murln_row[0:1, half * 512:(half + 1) * 512],
                        in_=prowB[0:1, :]), [rtok], V)
                    he2 = P.emit("vector", lambda e, half=half: e.tensor_copy(
                        out=rln_row[0:1, half * 512:(half + 1) * 512],
                        in_=prowB2[0:1, :]), [rtok], V)

                def wrB(c0, nb, pbank, tokT, i=i):
                    def f(e, c0=c0, nb=nb, i=i):
                        dst = xfm[:, :].rearrange("p (c t) -> p c t", c=32)
                        return e.activation(
                            out=dst[:, c0:c0 + nb, i * 128:(i + 1) * 128],
                            in_=pbank[:, 0:nb * 128], func=AF.Copy)
                    return P.emit("scalar", f, [tokT], SC)
                ev = transposes(xt2[b], 32, wrB, tl, ptrB, c_idf)
            xfm_done = ev
            tap("xfm", xfm[:, :], [xfm_done])
            d1 = P.dma("sync", rowbc[1:2, 0:1024], murln_row[0:1, :], [he1], SY[3])
            d2 = P.dma("sync", rowbc[2:3, 0:1024], rln_row[0:1, :], [he2], SY[3])
            B1q = sb("B1q", [128, 1024], F32, pQ)
            rlnB = sb("rlnB", [128, 1024], F32, pQ)
            d3 = P.dma("sync", B1q[:, :], rowbc[1:2, 0:1024].partition_broadcast(128),
                       [d1], SY[3])
            d4 = P.dma("sync", rlnB[:, :], rowbc[2:3, 0:1024].partition_broadcast(128),
                       [d2], SY[3])

            wqr = [sb("wqr0", [128, 32 * 512], BF16, pQ)]
            psQ = [ps("psQ%d" % i, [128, 512], F32, pQ) for i in range(2)]
            psR = ps("psRq", [1, 512], F32, pQ)
            psC = ps("psCq", [128, 1], F32, pQ)
            qraw = sb("qraw", [128, 512], F32, pQ)
            qc_last = None
            psq_free = [None, None]
            wq_free = None
            ws_done = {}
            for MB in range(3):
                wb = 0
                tl = P.dma("gpsimd", wqr[wb][:, :], wqa_t[MB, :, :],
                           [wq_free] if wq_free else join(), gq)
                for Mi in range(4):
                    M = MB * 4 + Mi
                    tm = None
                    for g in range(32):
                        w = [tl, m2]
                        if M > 0:
                            w.append(ws_done[M - 1])
                        tm = P.emit("tensor", lambda e, Mi=Mi, g=g, wb=wb: e.matmul(
                            psR[0:1, 0:128], ones_c[:, :],
                            wqr[wb][:, g * 512 + Mi * 128:g * 512 + Mi * 128 + 128],
                            start=(g == 0), stop=(g == 31)), w,
                            T if g == 31 else None)
                    er = P.emit("vector", lambda e: e.tensor_copy(
                        out=rowtmp[0:1, 0:128], in_=psR[0:1, 0:128]), [tm], V)
                    ws_done[M] = row_to_col(rowtmp[0:1, 0:128], psC[:, 0:1],
                                            wsq_col[:, M:M + 1], [er, m4])
                for Mi in range(4):
                    M = MB * 4 + Mi
                    for th in range(2):
                        k = (M * 2 + th) % 2
                        tm = None
                        for g in range(32):
                            w = [tl, xfm_done]
                            if g == 0 and psq_free[k] is not None:
                                w.append(psq_free[k])
                            if g == 0:
                                w.append(ws_done[11] if M == 11 else ws_done[M])
                            tm = P.emit("tensor",
                                        lambda e, Mi=Mi, g=g, th=th, k=k, wb=wb: e.matmul(
                                            psQ[k][:, :],
                                            wqr[wb][:, g * 512 + Mi * 128:
                                                    g * 512 + Mi * 128 + 128],
                                            xfm[:, g * 1024 + th * 512:
                                                g * 1024 + th * 512 + 512],
                                            start=(g == 0), stop=(g == 31)),
                                        w, T if g == 31 else None)
                        c1 = P.emit("vector", lambda e, k=k, th=th: e.scalar_tensor_tensor(
                            out=qraw[:, :], in0=psQ[k][:, :], scalar=1.0,
                            in1=rlnB[:, th * 512:th * 512 + 512],
                            op0=ALU.bypass, op1=ALU.mult), [tm, d4, qc_last], V)
                        psq_free[k] = c1
                        qc_last = P.emit(
                            "vector", lambda e, M=M, th=th: e.scalar_tensor_tensor(
                                out=qlat[:, M * 1024 + th * 512:M * 1024 + th * 512 + 512],
                                in0=B1q[:, th * 512:th * 512 + 512],
                                scalar=wsq_col[:, M:M + 1], in1=qraw[:, :],
                                op0=ALU.mult, op1=ALU.add),
                            [c1, d3], V, fwaits=[ws_done[M]])
                wq_free = qc_last
            sqs = sb("sqs", [128, 512], BF16, pQ)
            psS2 = ps("psS2", [1, 512], F32, pQ)
            rms_ev = {}
            for th in range(2):
                tm = None
                sq_tok = None
                for M in range(12):
                    w = [qc_last, tm]
                    if M == 0 and th == 1:
                        w.append(rms_ev[0])
                    sq_tok = P.emit("vector", lambda e, M=M, th=th: e.tensor_tensor(
                        out=sqs[:, :],
                        in0=qlat[:, M * 1024 + th * 512:M * 1024 + th * 512 + 512],
                        in1=qlat[:, M * 1024 + th * 512:M * 1024 + th * 512 + 512],
                        op=ALU.mult), w, V)
                    tm = P.emit("tensor", lambda e, M=M, th=th: e.matmul(
                        psS2[0:1, :], ones_c[:, :], sqs[:, :],
                        start=(M == 0), stop=(M == 11)),
                        [sq_tok] + ([rms_ev[0]] if (M == 0 and th == 1) else []), T)
                rms_ev[th] = P.emit("vector", lambda e, th=th: e.tensor_copy(
                    out=rowtmp[0:1, th * 512:th * 512 + 512], in_=psS2[0:1, :]),
                    [tm], V)
            s3 = P.emit("scalar", lambda e: e.activation(
                rowtmp[0:1, 1024:2048], rowtmp[0:1, 0:1024], AF.Ln,
                bias=cst[0:1, 1:2], scale=1.0 / QL), [rms_ev[0], rms_ev[1]], SC)
            s4 = P.emit("scalar", lambda e: e.activation(
                rowtmp[0:1, 1024:2048], rowtmp[0:1, 1024:2048], AF.Exp,
                bias=0.0, scale=-0.5), [s3], SC)
            d5 = P.dma("sync", rowbc[3:4, 0:1024], rowtmp[0:1, 1024:2048], [s4], SY[3])
            rmsqB = sb("rmsqB", [128, 1024], F32, pQ)
            d6 = P.dma("sync", rmsqB[:, :], rowbc[3:4, 0:1024].partition_broadcast(128),
                       [d5], SY[3])
            ql_done = None
            for M in range(12):
                ql_done = P.emit("vector", lambda e, M=M: e.scalar_tensor_tensor(
                    out=qlat[:, M * 1024:(M + 1) * 1024],
                    in0=qlat[:, M * 1024:(M + 1) * 1024],
                    scalar=c_qnw[:, M:M + 1], in1=rmsqB[:, :],
                    op0=ALU.mult, op1=ALU.mult), [d6, qc_last], V)
            qlat_done = ql_done
            tap("qlat", qlat[:, :], [qlat_done])

        P.cur_phase = 4
        pM2 = ExitStack()
        kvlat = sb("kvlat", [128, 4 * TB], BF16, pM2)
        kpe = sb("kpe", [65, TB], BF16, pM2)
        pC = ExitStack()
        with pC:
            wkva = sb("wkva", [128, 32 * KVA], BF16, pC)
            xt3 = [sb("xtC0", [128, D], F32, pC)]
            xfsp = sb("xfsp", [128, 32 * 512], BF16, pC)
            bn6c = sb("bn6c", [128, 48], F32, pC)
            mvc = sb("mvc", [128, 2], F32, pC)
            rlnc_c = sb("rlnc_c", [128, 1], F32, pC)
            nmuc_c = sb("nmuc_c", [128, 1], F32, pC)
            mnrc_c = sb("mnrc_c", [128, 1], F32, pC)
            kvc = sb("kvc", [128, 6 * 512], F32, pC)
            kvr = sb("kvr", [128, 512], F32, pC)
            sqc = sb("sqc", [128, 512], F32, pC)
            mB = sb("mB_c", [128, 512], F32, pC)
            rB = sb("rB_c", [128, 512], F32, pC)
            rkB = sb("rkB_c", [128, 512], F32, pC)
            ropes = sb("ropes", [32, 512], F32, pC)
            ropes2 = sb("ropes2", [32, 512], F32, pC)
            cosc = sb("cosc", [32, 512], F32, pC)
            sinc = sb("sinc", [32, 512], F32, pC)
            prowC = ps("prowC", [1, 512], F32, pC)
            prowC2 = ps("prowC2", [1, 512], F32, pC)
            ptrC = ps("ptrC", [128, 512], F32, pC)
            psK = [ps("psK%d" % i, [128, 512], F32, pC) for i in range(2)]
            psRC = ps("psRC", [1, 512], F32, pC)
            psCC = ps("psCC", [128, 1], F32, pC)
            wskv_col = sb("wskv_col", [128, 6], F32, pC)

            jw = join()
            lt = P.dma("gpsimd", wkva[:, :], wkva_t, jw, gq)
            ws_kv_done = {}
            prev = None
            _kvoff = [0, 128, 256, 384, 512, 544]
            _kvwid = [128, 128, 128, 128, 32, 32]
            for M in range(6):
                wid = _kvwid[M]
                off = _kvoff[M]
                tm = None
                for g in range(32):
                    w = [lt, m2] + ([prev] if prev else [])
                    tm = P.emit("tensor", lambda e, off=off, g=g, wid=wid: e.matmul(
                        psRC[0:1, 0:wid], ones_c[:, :],
                        wkva[:, g * KVA + off:g * KVA + off + wid],
                        start=(g == 0), stop=(g == 31)), w, T if g == 31 else None)
                er = P.emit("vector", lambda e, wid=wid: e.tensor_copy(
                    out=rowtmp[0:1, 0:wid], in_=psRC[0:1, 0:wid]), [tm], V)
                ws_kv_done[M] = row_to_col(rowtmp[0:1, 0:wid], psCC[0:wid, 0:1],
                                           wskv_col[0:wid, M:M + 1], [er, m4])
                prev = ws_kv_done[M]

            kv_done = None
            t6c = {}
            db1_prev = db4_prev = None
            ev_sp = None
            for sp in range(8):
                rtokc = None
                lc1 = P.dma("sync", cosc[:, :], cos_cov[:, sp * 512:(sp + 1) * 512],
                            jw + ([kv_done] if kv_done else []), SY[4])
                lc2 = P.dma("sync", sinc[:, :], sin_cov[:, sp * 512:(sp + 1) * 512],
                            jw + ([kv_done] if kv_done else []), SY[4])
                for q in range(4):
                    i = sp * 4 + q
                    b = 0
                    w = [kv_done] if (sp > 0 and q < 2) else []
                    if i >= 1:
                        w = w + ([ev_sp] if ev_sp else []) + [t6c[i - 1]]
                    tl = P.dma("sync", xt3[b][:, :], x_cov[i * 128:(i + 1) * 128, :],
                               (jw + w) if i < 2 else w, SY[4])
                    t1 = None
                    for kk in range(8):
                        t1 = P.emit("vector", lambda e, b=b, kk=kk: e.bn_stats(
                            bn6c[:, kk * 6:(kk + 1) * 6],
                            xt3[b][:, kk * 512:(kk + 1) * 512]), [tl], V)
                    t2 = P.emit("vector", lambda e: e.bn_aggr(mvc[:, :], bn6c[:, :]),
                                [t1], V)
                    t3 = P.emit("scalar", lambda e: e.activation(
                        rlnc_c[:, :], mvc[:, 1:2], AF.Ln, bias=cst[:, 0:1], scale=1.0),
                        [t2], SC)
                    t4 = P.emit("scalar", lambda e: e.activation(
                        rlnc_c[:, :], rlnc_c[:, :], AF.Exp, bias=0.0, scale=-0.5),
                        [t3], SC)
                    t5 = P.emit("vector", lambda e: e.tensor_scalar(
                        out=nmuc_c[:, :], in0=mvc[:, 0:1], scalar1=-1.0, scalar2=None,
                        op0=ALU.mult), [t2], V)
                    t6 = P.emit("vector", lambda e: e.tensor_scalar(
                        out=mnrc_c[:, :], in0=nmuc_c[:, :], scalar1=rlnc_c[:, :],
                        scalar2=None, op0=ALU.mult), [t4, t5], V)
                    t6c[i] = t6
                    r1 = P.emit("tensor", lambda e, q=q: e.matmul(
                        prowC[0:1, q * 128:(q + 1) * 128], mnrc_c[:, :], c_idf[:, :],
                        start=True, stop=True), [t6] + ([rtokc] if rtokc else []), T)
                    rtokc = P.emit("tensor", lambda e, q=q: e.matmul(
                        prowC2[0:1, q * 128:(q + 1) * 128], rlnc_c[:, :], c_idf[:, :],
                        start=True, stop=True), [t4, r1], T)

                    def wrC(c0, nb, pbank, tokT, q=q):
                        def f(e, c0=c0, nb=nb, q=q):
                            dst = xfsp[:, :].rearrange("p (c t) -> p c t", c=32)
                            return e.activation(
                                out=dst[:, c0:c0 + nb, q * 128:(q + 1) * 128],
                                in_=pbank[:, 0:nb * 128], func=AF.Copy)
                        return P.emit("scalar", f, [tokT], SC)
                    ev_sp = transposes(xt3[b], 32, wrC, tl, ptrC, c_idf)
                e1 = P.emit("vector", lambda e: e.tensor_copy(
                    out=rowtmp[0:1, 1024:1536], in_=prowC[0:1, 0:512]),
                    [rtokc] + ([db1_prev] if db1_prev else []), V)
                e2 = P.emit("vector", lambda e: e.tensor_copy(
                    out=rowtmp[0:1, 1536:2048], in_=prowC2[0:1, 0:512]), [rtokc], V)
                db1 = P.dma("sync", rowbc[4:5, 0:1024], rowtmp[0:1, 1024:2048],
                            [e1, e2], SY[4])
                db1_prev = db1
                db2 = P.dma("sync", mB[:, :], rowbc[4:5, 0:512].partition_broadcast(128),
                            [db1], SY[4])
                db3 = P.dma("sync", rB[:, :], rowbc[4:5, 512:1024].partition_broadcast(128),
                            [db1], SY[4])
                cor = {}
                for M in range(6):
                    wid = _kvwid[M]
                    off = _kvoff[M]
                    k = M % 2
                    tm = None
                    for g in range(32):
                        w = [lt, ev_sp]
                        if g == 0 and M >= 2:
                            w.append(cor[M - 2])
                        tm = P.emit("tensor", lambda e, off=off, g=g, wid=wid, k=k: e.matmul(
                            psK[k][0:wid, :],
                            wkva[:, g * KVA + off:g * KVA + off + wid],
                            xfsp[:, g * 512:(g + 1) * 512],
                            start=(g == 0), stop=(g == 31)), w,
                            T if g == 31 else None)
                    c1 = P.emit("vector", lambda e, M=M, wid=wid, k=k: e.scalar_tensor_tensor(
                        out=kvr[0:wid, :], in0=psK[k][0:wid, :], scalar=1.0,
                        in1=rB[0:wid, :], op0=ALU.bypass, op1=ALU.mult),
                        [tm, db3] + ([cor[M - 1]] if M >= 1 else []), V)
                    cor[M] = P.emit("vector", lambda e, M=M, wid=wid: e.scalar_tensor_tensor(
                        out=kvc[0:wid, M * 512:(M + 1) * 512], in0=mB[0:wid, :],
                        scalar=wskv_col[0:wid, M:M + 1], in1=kvr[0:wid, :],
                        op0=ALU.mult, op1=ALU.add), [c1, db2], V,
                        fwaits=[ws_kv_done[M]])
                tm = None
                sq_tok = None
                for M in range(4):
                    sq_tok = P.emit("vector", lambda e, M=M: e.tensor_tensor(
                        out=sqc[:, :], in0=kvc[:, M * 512:(M + 1) * 512],
                        in1=kvc[:, M * 512:(M + 1) * 512], op=ALU.mult),
                        [cor[M], tm], V)
                    tm = P.emit("tensor", lambda e, M=M: e.matmul(
                        psRC[0:1, :], onesf[:, :], sqc[:, :],
                        start=(M == 0), stop=(M == 3)), [sq_tok, m3], T)
                er = P.emit("vector", lambda e: e.tensor_copy(
                    out=rowtmp[0:1, 2048:2560], in_=psRC[0:1, :]), [tm], V)
                s5 = P.emit("scalar", lambda e: e.activation(
                    rowtmp[0:1, 2560:3072], rowtmp[0:1, 2048:2560], AF.Ln,
                    bias=cst[0:1, 1:2], scale=1.0 / KVL),
                    [er] + ([db4_prev] if db4_prev else []), SC)
                s6 = P.emit("scalar", lambda e: e.activation(
                    rowtmp[0:1, 2560:3072], rowtmp[0:1, 2560:3072], AF.Exp,
                    bias=0.0, scale=-0.5), [s5], SC)
                db4 = P.dma("sync", rowbc[5:6, 0:512], rowtmp[0:1, 2560:3072],
                            [s6], SY[4])
                db4_prev = db4
                db5 = P.dma("sync", rkB[:, :], rowbc[5:6, 0:512].partition_broadcast(128),
                            [db4], SY[4])
                kv_ev = {}
                for M in range(4):
                    kv_ev[M] = P.emit("vector", lambda e, M=M, sp=sp: e.scalar_tensor_tensor(
                        out=kvlat[:, M * TB + sp * 512:M * TB + sp * 512 + 512],
                        in0=kvc[:, M * 512:(M + 1) * 512],
                        scalar=c_kvnw[:, M:M + 1], in1=rkB[:, :],
                        op0=ALU.mult, op1=ALU.mult), [cor[M], db5], V)
                ro1 = P.emit("vector", lambda e: e.tensor_tensor(
                    out=ropes[:, :], in0=kvc[0:32, 4 * 512:5 * 512],
                    in1=cosc[:, :], op=ALU.mult), [cor[4], cor[5], lc1], V)
                ro2 = P.emit("vector", lambda e: e.tensor_tensor(
                    out=ropes2[:, :], in0=kvc[0:32, 5 * 512:6 * 512],
                    in1=sinc[:, :], op=ALU.mult), [cor[5], lc2], V)
                ro3 = P.emit("vector", lambda e, sp=sp: e.tensor_tensor(
                    out=kpe[0:32, sp * 512:(sp + 1) * 512], in0=ropes[:, :],
                    in1=ropes2[:, :], op=ALU.subtract), [ro1, ro2], V)
                ro4 = P.emit("vector", lambda e: e.tensor_tensor(
                    out=ropes[:, :], in0=kvc[0:32, 4 * 512:5 * 512],
                    in1=sinc[:, :], op=ALU.mult), [ro3, cor[4]], V)
                ro5 = P.emit("vector", lambda e: e.tensor_tensor(
                    out=ropes2[:, :], in0=kvc[0:32, 5 * 512:6 * 512],
                    in1=cosc[:, :], op=ALU.mult), [ro3, cor[5]], V)
                ro6 = P.emit("vector", lambda e, sp=sp: e.tensor_tensor(
                    out=kpe[32:64, sp * 512:(sp + 1) * 512], in0=ropes[:, :],
                    in1=ropes2[:, :], op=ALU.add), [ro4, ro5], V)
                kv_done = ro6
                kv_sp_done = [kv_ev[M] for M in range(4)] + [ro6]
            kvlat_done = kv_sp_done
            tap("kvlat", kvlat[:, :], kvlat_done)
            tap("kpe", kpe[0:64, :], kvlat_done)

        P.cur_phase = 5
        pT = ExitStack()
        with pT:
            coso = sb("coso", [32, TOWN], F32, pT)
            sino = sb("sino", [32, TOWN], F32, pT)
            wqbr = [sb("wqbr%d" % i, [128, 12 * 192], BF16, pT) for i in range(2)]
            wkbr = [sb("wkbr%d" % i, [128, 4 * 256], BF16, pT) for i in range(2)]
            qn = sb("qn", [128, TOWN], BF16, pT)
            qpe = sb("qpe", [65, TOWN], BF16, pT)
            kn = sb("kn", [128, TB], BF16, pT)
            vfm = sb("vfm", [128, TB], BF16, pT)
            vtm = sb("vtm", [128, TB], BF16, pT)
            ptb = sb("ptb", [128, 32 * 512], BF16, pT)
            kpe2 = sb("kpe2", [64, TB], BF16, pT)
            attnh = sb("attnh", [128, TOWN], BF16, pT)
            q2s = sb("q2s", [128, TOWN], BF16, pT)
            k2s = sb("k2s", [128, TB], BF16, pT)
            negm = sb("negm", [1, TOWN], BF16, pT)
            linvB = sb("linvB", [128, TOWN], BF16, pT)
            rop1 = sb("rop1", [32, 512], F32, pT)
            rop2 = sb("rop2", [32, 512], F32, pT)
            rop3 = sb("rop3", [32, 512], F32, pT)
            psq = [ps("psq%d" % i, [128, 512], F32, pT) for i in range(2)]
            psq2 = ps("psq2", [32, 512], F32, pT)
            pss = [ps("pss%d" % i, [128, 512], F32, pT) for i in range(2)]
            psr = ps("psr", [1, 512], F32, pT)
            pstr = ps("pstr", [128, 512], BF16, pT)
            psav = ps("psavb", [128, 512], F32, pT)

            jw = join()
            lo1 = P.dma("sync", coso[:, :], cos_own, jw, SY[4])
            lo2 = P.dma("sync", sino[:, :], sin_own, jw, SY[4])
            mko = P.emit("vector", lambda e: e.memset(kpe[64:65, :], 1.0), jw, V)
            prev_head = {}
            for h in range(NH):
                hb = h % 2
                wl1 = P.dma("gpsimd", wqbr[hb][:, :], wqb_t[h, :, :],
                            jw if h < 2 else [prev_head[h - 2]["qdone"]], gq)
                wl2 = P.dma("gpsimd", wkbr[hb][:, :], wkvb_t[h, :, :],
                            jw if h < 2 else [prev_head[h - 2]["kvdone"]], gq)

                qev = {}
                for th in range(2):
                    k = th
                    tm = None
                    for g in range(12):
                        w = [wl1, qlat_done]
                        if h > 0 and g == 0:
                            w.append(prev_head[h - 1]["s_done"])
                        tm = P.emit("tensor", lambda e, g=g, th=th, k=k, hb=hb: e.matmul(
                            psq[k][:, :],
                            wqbr[hb][:, g * 192:g * 192 + 128],
                            qlat[:, g * 1024 + th * 512:g * 1024 + th * 512 + 512],
                            start=(g == 0), stop=(g == 11)), w,
                            T if g == 11 else None)
                    qev[th] = P.emit("vector", lambda e, th=th, k=k: e.tensor_scalar(
                        out=qn[:, th * 512:th * 512 + 512], in0=psq[k][:, :],
                        scalar1=SCALE, scalar2=None, op0=ALU.mult), [tm], V)
                pe_ev = {}
                for th in range(2):
                    k = th
                    tm1 = None
                    for g in range(12):
                        w = [wl1, qlat_done, qev[th]]
                        tm1 = P.emit("tensor", lambda e, g=g, th=th, k=k, hb=hb: e.matmul(
                            psq[k][0:32, 0:512],
                            wqbr[hb][:, g * 192 + 128:g * 192 + 160],
                            qlat[:, g * 1024 + th * 512:g * 1024 + th * 512 + 512],
                            start=(g == 0), stop=(g == 11)), w,
                            T if g == 11 else None)
                    tm2 = None
                    for g in range(12):
                        w2_ = [tm1]
                        if g == 0 and th == 1:
                            w2_.append(pe_ev[0])
                        tm2 = P.emit("tensor", lambda e, g=g, th=th, k=k, hb=hb: e.matmul(
                            psq2[0:32, 0:512],
                            wqbr[hb][:, g * 192 + 160:g * 192 + 192],
                            qlat[:, g * 1024 + th * 512:g * 1024 + th * 512 + 512],
                            start=(g == 0), stop=(g == 11)), w2_,
                            T if g == 11 else None)
                    p1 = P.emit("vector", lambda e, th=th, k=k: e.tensor_tensor(
                        out=rop1[:, :], in0=psq[k][0:32, 0:512],
                        in1=coso[:, th * 512:th * 512 + 512], op=ALU.mult),
                        [tm1, tm2, lo1], V)
                    p2 = P.emit("vector", lambda e, th=th, k=k: e.tensor_tensor(
                        out=rop2[:, :], in0=psq2[0:32, 0:512],
                        in1=sino[:, th * 512:th * 512 + 512], op=ALU.mult),
                        [tm2, lo2], V)
                    p3 = P.emit("vector", lambda e: e.tensor_tensor(
                        out=rop3[:, :], in0=rop1[:, :], in1=rop2[:, :],
                        op=ALU.subtract), [p1, p2], V)
                    p4 = P.emit("vector", lambda e, th=th: e.tensor_scalar(
                        out=qpe[0:32, th * 512:th * 512 + 512], in0=rop3[:, :],
                        scalar1=SCALE, scalar2=None, op0=ALU.mult), [p3], V)
                    p5 = P.emit("vector", lambda e, th=th, k=k: e.tensor_tensor(
                        out=rop1[:, :], in0=psq[k][0:32, 0:512],
                        in1=sino[:, th * 512:th * 512 + 512], op=ALU.mult), [p3], V)
                    p6 = P.emit("vector", lambda e, th=th, k=k: e.tensor_tensor(
                        out=rop2[:, :], in0=psq2[0:32, 0:512],
                        in1=coso[:, th * 512:th * 512 + 512], op=ALU.mult), [p3], V)
                    p7 = P.emit("vector", lambda e: e.tensor_tensor(
                        out=rop3[:, :], in0=rop1[:, :], in1=rop2[:, :],
                        op=ALU.add), [p5, p6], V)
                    pe_ev[th] = P.emit("vector", lambda e, th=th: e.tensor_scalar(
                        out=qpe[32:64, th * 512:th * 512 + 512], in0=rop3[:, :],
                        scalar1=SCALE, scalar2=None, op0=ALU.mult), [p4, p7], V)
                q_done = [qev[0], qev[1], pe_ev[0], pe_ev[1]]

                kv_ev = []
                for which in range(2):
                    for sp2 in range(8):
                        k = sp2 % 2
                        tm = None
                        for g in range(4):
                            w = [wl2] + list(kvlat_done)
                            if which == 0 and sp2 == 0 and g == 0:
                                w += [pe_ev[0], pe_ev[1]]
                            if h > 0 and sp2 == 0 and g == 0:
                                w.append(prev_head[h - 1]["s_done"])
                            if len(kv_ev) >= 2 and g == 0:
                                w.append(kv_ev[-2])
                            tm = P.emit("tensor",
                                        lambda e, g=g, sp2=sp2, which=which, k=k,
                                        hb=hb: e.matmul(
                                            psq[k][:, :],
                                            wkbr[hb][:, g * 256 + which * 128:
                                                     g * 256 + which * 128 + 128],
                                            kvlat[:, g * TB + sp2 * 512:
                                                  g * TB + sp2 * 512 + 512],
                                            start=(g == 0), stop=(g == 3)), w,
                                        T if g == 3 else None)
                        dst = kn if which == 0 else vfm
                        kv_ev.append(P.emit(
                            "vector", lambda e, sp2=sp2, k=k, dst=dst: e.tensor_copy(
                                out=dst[:, sp2 * 512:sp2 * 512 + 512],
                                in_=psq[k][:, :]), [tm], V))
                kv_done_h = kv_ev[-1]
                def wrV(c0, nb, pbank, tokT):
                    def f(e, c0=c0, nb=nb):
                        return e.tensor_copy(
                            out=vtm[:, c0 * 128:c0 * 128 + nb * 128],
                            in_=pbank[:, 0:nb * 128])
                    return P.emit("vector", f, [tokT], V)
                vt_done = transposes(vfm, 32, wrV, kv_done_h, pstr, c_idbf,
                                     extra_src_toks=([prev_head[h - 1]["vt"]]
                                                     if h > 0 else []))

                prev_nr1 = prev_head[h - 1]["nr1"] if h > 0 else None
                q2a = P.emit("vector", lambda e: e.tensor_tensor(
                    out=q2s[:, :], in0=qn[:, :], in1=qn[:, :], op=ALU.mult),
                    q_done + ([prev_nr1] if prev_nr1 else []), V)
                qpe2 = P.emit("vector", lambda e: e.tensor_tensor(
                    out=k2s[0:64, 0:TOWN], in0=qpe[0:64, :], in1=qpe[0:64, :],
                    op=ALU.mult), q_done, V)
                q2r = None
                for th in range(2):
                    tmq = P.emit("tensor", lambda e, th=th: e.matmul(
                        psr[0:1, 0:512], ones_c[:, :],
                        q2s[:, th * 512:th * 512 + 512],
                        start=True, stop=False), [q2a, m2, q2r], None)
                    tmq = P.emit("tensor", lambda e, th=th: e.matmul(
                        psr[0:1, 0:512], ones_c[0:64, :],
                        k2s[0:64, th * 512:th * 512 + 512],
                        start=False, stop=True), [qpe2], T)
                    q2r = P.emit("vector", lambda e, th=th: e.tensor_copy(
                        out=rowtmp[0:1, th * 512:th * 512 + 512], in_=psr[0:1, 0:512]),
                        [tmq] + ([prev_nr1] if prev_nr1 else []), V)
                k2a = P.emit("vector", lambda e: e.tensor_tensor(
                    out=k2s[:, :], in0=kn[:, :], in1=kn[:, :], op=ALU.mult),
                    [kv_ev[7], q2r, qpe2], V)
                kp2 = None
                for sp2 in range(8):
                    kp2 = P.emit("vector", lambda e, sp2=sp2: e.tensor_tensor(
                        out=kpe2[:, sp2 * 512:(sp2 + 1) * 512],
                        in0=kpe[0:64, sp2 * 512:(sp2 + 1) * 512],
                        in1=kpe[0:64, sp2 * 512:(sp2 + 1) * 512], op=ALU.mult),
                        [kp2] if kp2 else list(kvlat_done), V)
                tmk = None
                for sp2 in range(8):
                    tmk = P.emit("tensor", lambda e, sp2=sp2: e.matmul(
                        psr[0:1, 0:512], ones_c[:, :],
                        k2s[:, sp2 * 512:sp2 * 512 + 512],
                        start=True, stop=False), [k2a, q2r, tmk], None)
                    tmk = P.emit("tensor", lambda e, sp2=sp2: e.matmul(
                        psr[0:1, 0:512], ones_c[0:64, :],
                        kpe2[:, sp2 * 512:sp2 * 512 + 512],
                        start=False, stop=True), [kp2], T)
                    tmk = P.emit("vector", lambda e, sp2=sp2: e.reduce_max(
                        rowtmp[0:1, 2048 + sp2:2048 + sp2 + 1], psr[0:1, 0:512],
                        axis=AX), [tmk], V)
                mk1 = P.emit("vector", lambda e: e.reduce_max(
                    rowtmp[0:1, 2056:2057], rowtmp[0:1, 2048:2056], axis=AX),
                    [tmk], V)
                mk2 = P.emit("scalar", lambda e: e.activation(
                    rowtmp[0:1, 2057:2058], rowtmp[0:1, 2056:2057], AF.Ln,
                    bias=cst[0:1, 2:3], scale=1.0), [mk1], SC)
                mk3 = P.emit("scalar", lambda e: e.activation(
                    rowtmp[0:1, 2057:2058], rowtmp[0:1, 2057:2058], AF.Exp,
                    bias=0.0, scale=0.5), [mk2], SC)
                mk4 = P.emit("vector", lambda e: e.tensor_scalar(
                    out=rowtmp[0:1, 2058:2059], in0=rowtmp[0:1, 2057:2058],
                    scalar1=-1.02, scalar2=None, op0=ALU.mult), [mk3], V)
                nr1 = P.emit("scalar", lambda e: e.activation(
                    rowtmp[0:1, 1024:2048], rowtmp[0:1, 0:1024], AF.Ln,
                    bias=cst[0:1, 2:3], scale=1.0), [q2r, mk4], SC)
                nr2 = P.emit("scalar", lambda e: e.activation(
                    rowtmp[0:1, 1024:2048], rowtmp[0:1, 1024:2048], AF.Exp,
                    bias=0.0, scale=0.5), [nr1], SC)
                nr3 = P.emit("vector", lambda e: e.tensor_scalar(
                    out=negm[0:1, :], in0=rowtmp[0:1, 1024:2048],
                    scalar1=rowtmp[0:1, 2058:2059], scalar2=None, op0=ALU.mult),
                    [nr2], V, fwaits=[mk4])
                nb1 = P.dma("sync", rowbc[7:8, 0:1024].bitcast(BF16)[0:1, 0:1024],
                            negm[0:1, :], [nr3], SY[4])
                nb2 = P.dma("sync", qpe[64:65, :],
                            rowbc[7:8, 0:1024].bitcast(BF16)[0:1, 0:1024],
                            [nb1] + [pe_ev[0], pe_ev[1]], SY[4])
                if h == 0:
                    tap("negm_h0", negm[0:1, :], [nr3])

                s_done = None
                avev = {}
                lrow_ev = {}
                prev_out = prev_head[h - 1]["out"] if h > 0 else None
                prev_dl1 = prev_head[h - 1]["dl1"] if h > 0 else None
                for g5 in range(2):
                    KC = 16 * (g5 + 1)
                    av_tok = None
                    l_tok = None
                    exp_done = {}
                    for kc in range(KC):
                        kb = kc % 2
                        w = q_done + [kv_ev[kc // 4], nb2, kv_done_h]
                        if kc >= 2:
                            w.append(exp_done[kc - 2])
                        P.emit("tensor", lambda e, kc=kc, g5=g5, kb=kb: e.matmul(
                            pss[kb][:, :], kn[:, kc * 128:(kc + 1) * 128],
                            qn[:, g5 * 512:g5 * 512 + 512],
                            start=True, stop=False), w, None)
                        tm = P.emit("tensor", lambda e, kc=kc, g5=g5, kb=kb: e.matmul(
                            pss[kb][:, :], kpe[:, kc * 128:(kc + 1) * 128],
                            qpe[:, g5 * 512:g5 * 512 + 512],
                            start=False, stop=True), [mko], T)
                        ex = P.emit("scalar", lambda e, kc=kc, kb=kb: e.activation(
                            out=ptb[:, kc * 512:(kc + 1) * 512], in_=pss[kb][:, :],
                            func=AF.Exp, bias=0.0, scale=1.0), [tm], SC)
                        mtoks = []
                        for dq in range(4):
                            i_t = 4 * g5 + dq
                            if 4 * i_t <= kc < 4 * i_t + 4:
                                O = 128 * kc - 512 * i_t
                                mtoks.append(P.emit(
                                    "vector",
                                    lambda e, kc=kc, dq=dq, O=O: e.scalar_tensor_tensor(
                                        out=ptb[:, kc * 512 + dq * 128:
                                                kc * 512 + dq * 128 + 128],
                                        in0=ptb[:, kc * 512 + dq * 128:
                                                kc * 512 + dq * 128 + 128],
                                        scalar=1.0,
                                        in1=c_strip[:, 384 - O:384 - O + 128],
                                        op0=ALU.bypass, op1=ALU.mult), [ex], V))
                            elif kc >= 4 * i_t + 4:
                                mtoks.append(P.emit(
                                    "vector", lambda e, kc=kc, dq=dq: e.memset(
                                        ptb[:, kc * 512 + dq * 128:
                                            kc * 512 + dq * 128 + 128], 0.0),
                                    [ex], V))
                        exp_done[kc] = mtoks[-1] if mtoks else ex
                        lw = [exp_done[kc], m2]
                        if kc == 0:
                            if g5 > 0:
                                lw.append(lrow_ev[g5 - 1])
                            elif prev_dl1 is not None:
                                lw.append(prev_dl1)
                        l_tok = P.emit("tensor", lambda e, kc=kc, KC=KC: e.matmul(
                            psr[0:1, 0:512], ones_c[:, :],
                            ptb[:, kc * 512:(kc + 1) * 512],
                            start=(kc == 0), stop=(kc == KC - 1)), lw,
                            T if kc == KC - 1 else None)
                        aw = [exp_done[kc], vt_done]
                        if kc == 0:
                            if g5 > 0:
                                aw.append(avev[g5 - 1])
                            elif prev_out is not None:
                                aw.append(prev_out)
                        av_tok = P.emit("tensor", lambda e, kc=kc, KC=KC: e.matmul(
                            psav[:, :], vtm[:, kc * 128:(kc + 1) * 128],
                            ptb[:, kc * 512:(kc + 1) * 512],
                            start=(kc == 0), stop=(kc == KC - 1)), aw,
                            T if kc == KC - 1 else None)
                    avw = [av_tok]
                    if g5 == 0 and prev_out is not None:
                        avw.append(prev_out)
                    avev[g5] = P.emit("vector", lambda e, g5=g5: e.tensor_copy(
                        out=attnh[:, g5 * 512:(g5 + 1) * 512], in_=psav[:, :]),
                        avw, V)
                    lvw = [l_tok]
                    if g5 == 0 and prev_dl1 is not None:
                        lvw.append(prev_dl1)
                    lrow_ev[g5] = P.emit("vector", lambda e, g5=g5: e.tensor_copy(
                        out=rowtmp[0:1, 3072 + g5 * 512:3072 + (g5 + 1) * 512],
                        in_=psr[0:1, 0:512]), lvw, V)
                    s_done = avev[g5]
                rc = P.emit("vector", lambda e: e.reciprocal(
                    rowtmp[0:1, 3072:3072 + 1024],
                    rowtmp[0:1, 3072:3072 + 1024]),
                    [lrow_ev[g] for g in range(2)], V)
                dl1 = P.dma("sync", rowbc[6:7, 0:1024], rowtmp[0:1, 3072:4096],
                            [rc], SY[4])
                dl2 = P.dma("gpsimd", linvB[:, :],
                            rowbc[6:7, 0:1024].partition_broadcast(128), [dl1], gq)
                nm = P.emit("vector", lambda e: e.tensor_tensor(
                    out=attnh[:, :], in0=attnh[:, :], in1=linvB[:, :],
                    op=ALU.mult), [dl2] + [avev[g] for g in range(2)], V)
                da = P.dma("sync", attn_scr[h * 128:(h + 1) * 128, :], attnh[:, :],
                           [nm], SY[4])
                if h == 0:
                    tap("qn_h0", qn[:, :], q_done)
                    tap("qpe_h0", qpe[0:64, :], q_done)
                    tap("kn_h0", kn[:, :], [kv_ev[7]])
                    tap("vtm_h0", vtm[:, :], [vt_done])
                    tap("pt_h0", ptb[:, 0:32 * 256], [s_done])
                    tap("lrow_h0", rowtmp[0:1, 3072:4096],
                        [lrow_ev[g] for g in range(2)])
                    tap("attnh_h0", attnh[:, :], [nm])
                prev_head[h] = {"qdone": qev[1], "kvdone": kv_done_h,
                                "s_done": s_done, "out": da, "dl1": dl1,
                                "nr1": nr1, "vt": vt_done}
            attn_done = [prev_head[hh]["out"] for hh in range(NH)]
            tap("attn_all", attn_scr, attn_done)

        P.cur_phase = 6
        pZ = ExitStack()
        with pZ:
            attn_all = sb("attn_all", [128, 32 * 1024], BF16, pZ)
            wor = [sb("wor%d" % i, [128, 512], F32, pZ) for i in range(3)]
            wob = [sb("wob%d" % i, [128, 512], BF16, pZ) for i in range(2)]
            xsl = [sb("xsl%d" % i, [128, 512], F32, pZ) for i in range(2)]
            fsl = [sb("fsl%d" % i, [128, 512], F32, pZ) for i in range(2)]
            oub = [sb("oub%d" % i, [128, 512], F32, pZ) for i in range(2)]
            psO = [ps("psO%d" % i, [128, 512], F32, pZ) for i in range(8)]
            jw = join()
            srcv = attn_scr.rearrange("(k p) t -> p k t", p=128)
            dstv = attn_all[:, :].rearrange("p (k t) -> p k t", k=32)
            la_toks = []
            for kq in range(4):
                la_toks.append(P.dma("sync", dstv[:, kq * 8:(kq + 1) * 8, :],
                                     srcv[:, kq * 8:(kq + 1) * 8, :],
                                     attn_done + jw, SY[0]))
            cast_prev = None
            mm_done2 = {}
            ev_done = {}
            out_toks = []
            for osp in range(8):
                tmz = {}
                for K in range(32):
                    idx = osp * 32 + K
                    rb = idx % 3
                    qb = idx % 2
                    w = jw if idx == 0 else []
                    if idx >= 3:
                        w = [mm_done2[idx - 3]]
                    tl = P.dma("sync", wor[rb][:, :], wo_t[osp, K, :, :], w, SY[0])
                    cw = [tl]
                    if idx >= 2:
                        cw.append(mm_done2[idx - 2])
                    cast_prev = P.emit("scalar", lambda e, rb=rb, qb=qb: e.activation(
                        out=wob[qb][:, :], in_=wor[rb][:, :], func=AF.Copy),
                        cw, SC)
                    tm = None
                    for tt in range(8):
                        w2 = [cast_prev] + la_toks
                        if K == 0 and ev_done.get((osp - 1, tt)) is not None:
                            w2.append(ev_done[(osp - 1, tt)])
                        tm = P.emit("tensor", lambda e, K=K, tt=tt, qb=qb: e.matmul(
                            psO[tt][:, :],
                            attn_all[:, K * 1024 + tt * 128:K * 1024 + tt * 128 + 128],
                            wob[qb][:, :], start=(K == 0), stop=(K == 31)),
                            w2, T if (K == 31 or tt == 7) else None)
                        if K == 31:
                            tmz[tt] = tm
                    mm_done2[idx] = tm
                for tt in range(8):
                    b = tt % 2
                    xw = [out_toks[-2]] if len(out_toks) >= 2 else []
                    tx = P.dma("sync", xsl[b][:, :],
                               x_own[tt * 128:(tt + 1) * 128,
                                     osp * 512:(osp + 1) * 512], xw, SY[0])
                    e1 = P.emit("vector", lambda e, tt=tt, b=b: e.scalar_tensor_tensor(
                        out=oub[b][:, :], in0=psO[tt][:, :], scalar=1.0,
                        in1=xsl[b][:, :], op0=ALU.bypass, op1=ALU.add),
                        [tmz[tt], tx], V)
                    ev_done[(osp, tt)] = e1
                    tf = P.dma("sync",
                               fsl[b][:, :].rearrange("p (c t) -> p c t", c=4),
                               ffn_scr[:, :].rearrange("p (c t) -> p c t", c=32)
                               [:, 4 * osp:4 * osp + 4, tt * 128:(tt + 1) * 128],
                               ffn_done + xw, SY[0])
                    tm2 = None
                    for cj in range(4):
                        tm2 = P.emit("tensor", lambda e, cj=cj, tt=tt, b=b: e.transpose(
                            psO[tt][:, cj * 128:(cj + 1) * 128],
                            fsl[b][:, cj * 128:(cj + 1) * 128], c_idf[:, :]),
                            [tf, e1], T if cj == 3 else None)
                    e2 = P.emit("vector", lambda e, tt=tt, b=b: e.scalar_tensor_tensor(
                        out=oub[b][:, :], in0=oub[b][:, :], scalar=1.0,
                        in1=psO[tt][:, :], op0=ALU.bypass, op1=ALU.add),
                        [tm2], V)
                    ot = P.dma("sync", out[tt * 128:(tt + 1) * 128,
                                          osp * 512:(osp + 1) * 512],
                               oub[b][:, :], [e2], SY[0])
                    out_toks.append(ot)

        pM2.close()
        pM1.close()
        final_waits = [(c, c.n) for c in ALL]
        with nc.Block() as block:
            P.finalize(block, final_waits)
    return nc


def _prep_inputs(inputs):
    x = np.asarray(inputs["L0_attn_x"])
    wq_a = np.asarray(inputs["L0_attn_wq_a"])
    q_norm_w = np.asarray(inputs["L0_attn_q_norm_w"])
    wq_b = np.asarray(inputs["L0_attn_wq_b"])
    wkv_a = np.asarray(inputs["L0_attn_wkv_a"])
    kv_norm_w = np.asarray(inputs["L0_attn_kv_norm_w"])
    wkv_b = np.asarray(inputs["L0_attn_wkv_b"])
    wo = np.asarray(inputs["L0_attn_wo"])
    gamma = np.asarray(inputs["L0_ffn_gamma"])
    beta = np.asarray(inputs["L0_ffn_beta"])
    wf = np.asarray(inputs["L0_ffn_weight"])

    pe = np.concatenate([np.arange(0, DPE, 2), np.arange(1, DPE, 2)])

    wqaT = np.ascontiguousarray(wq_a.T)
    wqa_t = np.empty((3, 128, 32 * 512), np.float32)
    for MB in range(3):
        blk = wqaT[:, MB * 512:(MB + 1) * 512].reshape(32, 128, 512)
        wqa_t[MB] = blk.transpose(1, 0, 2).reshape(128, 32 * 512)

    wqbT = wq_b.T.reshape(QL, NH, DN + DPE)
    perm = np.concatenate([np.arange(DN), DN + pe])
    wqbT = wqbT[:, :, perm]
    wqb_t = np.empty((NH, 128, 12 * 192), np.float32)
    for h in range(NH):
        blk = wqbT[:, h, :].reshape(12, 128, 192)
        wqb_t[h] = blk.transpose(1, 0, 2).reshape(128, 12 * 192)

    wkvaT = wkv_a.T
    perm2 = np.concatenate([np.arange(KVL), KVL + pe])
    wkvaT = wkvaT[:, perm2]
    wkva_t = np.ascontiguousarray(
        wkvaT.reshape(32, 128, KVA).transpose(1, 0, 2).reshape(128, 32 * KVA))

    wkvbT = wkv_b.T.reshape(KVL, NH, DN + DV)
    wkvb_t = np.empty((NH, 128, 4 * 256), np.float32)
    for h in range(NH):
        blk = wkvbT[:, h, :].reshape(4, 128, 256)
        wkvb_t[h] = blk.transpose(1, 0, 2).reshape(128, 4 * 256)

    woT = np.ascontiguousarray(wo.T)
    wo_t = np.ascontiguousarray(
        woT.reshape(32, 128, 8, 512).transpose(2, 0, 1, 3))

    Pv = np.arange(128)
    pi = np.zeros(4096, np.int64)
    for g in range(32):
        pi[g * 128:(g + 1) * 128] = (Pv % 32) * 128 + 4 * g + Pv // 32
    wfT = np.ascontiguousarray(wf.T[pi, :])
    wf_t = np.ascontiguousarray(
        wfT.reshape(32, 128, 8, 512).transpose(2, 0, 1, 3))

    qnw_t = np.ascontiguousarray(q_norm_w.reshape(12, 128).T)
    kvnw_t = np.ascontiguousarray(kv_norm_w.reshape(4, 128).T)

    h128c = _hadamard(128).astype(BF)
    h32 = _hadamard(32).astype(BF)
    h32s = np.ascontiguousarray(np.concatenate([h32] * 4, axis=0))
    idbf = np.eye(128, dtype=np.float32).astype(BF)
    idf = np.eye(128, dtype=np.float32)

    freqs = 1.0 / 10000.0 ** (np.arange(0, DPE, 2, dtype=np.float32) / DPE)
    ang = np.outer(freqs, np.arange(TB, dtype=np.float32))
    cosc = np.ascontiguousarray(np.cos(ang).astype(np.float32))
    sinc = np.ascontiguousarray(np.sin(ang).astype(np.float32))

    in_maps = []
    own_rows_all = []
    for core in range(NCORE):
        b, a = divmod(core, 4)
        own = (128 * (4 * np.arange(8)[:, None] + a) +
               np.arange(128)[None, :]).reshape(-1)
        own_rows_all.append((b, own))
        u = np.arange(512)[None, :]
        p = np.arange(128)[:, None]
        stripc = (u >= p - 128 * a + 384).astype(np.float32).astype(BF)
        m = {
            "x_own": np.ascontiguousarray(x[b][own, :]),
            "x_cov": np.ascontiguousarray(x[b]),
            "wqa_t": wqa_t, "wqb_t": wqb_t, "wkva_t": wkva_t,
            "wkvb_t": wkvb_t, "wo_t": wo_t, "wf_t": wf_t,
            "qnw": qnw_t, "kvnw": kvnw_t,
            "gamma": np.ascontiguousarray(gamma.reshape(1, D)),
            "beta": np.ascontiguousarray(beta.reshape(1, D)),
            "h128": h128c, "h32s": h32s, "ident_bf": idbf, "ident_f32": idf,
            "strip": stripc,
            "cos_own": np.ascontiguousarray(cosc[:, own]),
            "sin_own": np.ascontiguousarray(sinc[:, own]),
            "cos_cov": cosc, "sin_cov": sinc,
        }
        in_maps.append(m)
    return in_maps, own_rows_all


def kernel(**inputs):
    in_maps, own_rows_all = _prep_inputs(inputs)
    nc = build(DEV_TAPS, MAX_PHASE)
    res = run_bass_kernel_spmd(nc, in_maps, core_ids=list(range(NCORE)))
    outp = np.zeros((2, TB, D), np.float32)
    for core in range(NCORE):
        b, own = own_rows_all[core]
        outp[b][own, :] = np.asarray(res.results[core]["out"])
    if DEV_TAPS:
        kernel.last_results = res.results
    return outp


# version 03
# speedup vs baseline: 1.7307x; 1.2959x over previous
import math
import numpy as np
import ml_dtypes
import concourse.bass as bass
import concourse.mybir as mybir
from concourse.bass_utils import run_bass_kernel_spmd

F32 = mybir.dt.float32
BF16 = mybir.dt.bfloat16
I32 = mybir.dt.int32
AF = mybir.ActivationFunctionType
ALU = mybir.AluOpType
AX = mybir.AxisListType.X

D = 4096
NH = 32
DN = 128
DPE = 64
DV = 128
QL = 1536
KVL = 512
KVA = KVL + DPE
TOWN = 1024
TB = 4096
SCALE = 1.0 / math.sqrt(DN + DPE)
EPS_RMS = 1e-6
EPS_LN = 1e-5
NCORE = 8

BF = ml_dtypes.bfloat16

DEV_TAPS = ()
MAX_PHASE = 99
SERIALIZE = False
_TAPS = {
    "xq_fm": ([128, 32 * 1024], BF16),
    "dbg_xn": ([128, 4096], F32),
    "dbg_mv": ([128, 2], F32),
    "dbg_rln": ([128, 1], F32),
    "dbg_gammaB": ([128, 4096], F32),
    "dbg_xt": ([128, 4096], F32),
    "dbg_xln": ([128, 4096], F32),
    "dbg_sf": ([128, 1], F32),
    "dbg_xqb": ([128, 4096], BF16),
    "xh": ([128, 32 * 1024], BF16),
    "outpre": ([128, 8 * 4096], BF16),
    "ffn": ([128, 32 * 1024], F32),
    "xfm": ([128, 32 * 1024], BF16),
    "qlat": ([128, 12 * 1024], BF16),
    "kvlat": ([128, 4 * 4096], BF16),
    "kpe": ([64, 4096], BF16),
    "qn_h0": ([128, 1024], BF16),
    "qpe_h0": ([64, 1024], BF16),
    "kn_h0": ([128, 4096], BF16),
    "vtm_h0": ([128, 4096], BF16),
    "negm_h0": ([1, 1024], BF16),
    "pt_h0": ([128, 32 * 256], BF16),
    "lrow_h0": ([1, 1024], F32),
    "attnh_h0": ([128, 1024], BF16),
    "attn_all": ([4096, 1024], BF16),
}


def _hadamard(n):
    h = np.array([[1.0]], dtype=np.float32)
    while h.shape[0] < n:
        h = np.block([[h, h], [h, -h]])
    return h


class Ctr:
    __slots__ = ("h", "n", "name", "is_lane")

    def __init__(self, h, name):
        self.h = h
        self.n = 0
        self.name = name
        self.is_lane = False


class DmaQ:

    def __init__(self, lanes):
        self.lanes = lanes
        self.idx = 0

    def next(self):
        K = len(self.lanes)
        i = self.idx
        self.idx += 1
        lane = self.lanes[i % K]
        pre = (lane, 16 * (i // K)) if i >= K else None
        lane.n = 16 * (i // K + 1)
        return lane, pre, (lane, lane.n)

    def final_tokens(self):
        return [(ln, ln.n) for ln in self.lanes]


class Prog:
    ENGS = ("sync", "vector", "scalar", "tensor", "gpsimd")

    def __init__(self, nc):
        self.nc = nc
        self.q = {e: [] for e in self.ENGS}
        self.state = {e: {} for e in self.ENGS}
        self.owner = {}
        self.cur_phase = 0
        self.max_phase = 99
        self.serialize = False
        self.all_ctrs = []
        self.tap_gate = []

    def ctr(self, stack, name, owner):
        h = stack.enter_context(self.nc.semaphore(name))
        c = Ctr(h, name)
        self.owner[id(c)] = owner
        return c

    def emit(self, eng, fn, waits=(), inc=None, amt=1, fwaits=()):
        if self.cur_phase > self.max_phase:
            return None
        if self.tap_gate:
            waits = list(waits) + self.tap_gate
        if self.serialize:
            waits = list(waits) + [(c, c.n) for c in self.all_ctrs]
        wl = []
        st = self.state[eng]
        for tok in fwaits:
            if tok is None:
                continue
            c, v = tok
            if v > 0:
                wl.append((c.h, v))
        for tok in waits:
            if tok is None:
                continue
            c, v = tok
            if v <= 0:
                continue
            if (self.owner.get(id(c)) == eng and not c.is_lane
                    and (c.n - v) > 12):
                continue
            if st.get(id(c), 0) >= v:
                continue
            st[id(c)] = v
            wl.append((c.h, v))
        tok = None
        if inc is not None:
            inc.n += amt
            tok = (inc, inc.n)
        self.q[eng].append((wl, fn, ((inc.h if inc is not None else None), amt)))
        return tok

    def dma(self, eng, out, in_, waits=(), inc=None):
        def fn(e, out=out, in_=in_):
            return e.dma_start(out=out, in_=in_)
        if isinstance(inc, DmaQ):
            if self.cur_phase > self.max_phase:
                return None
            lane, pre, tok = inc.next()
            w = list(waits)
            if pre is not None:
                w.append(pre)
            self.emit(eng, fn, w, lane, 16, fwaits=())
            lane.n = tok[1]
            return tok
        return self.emit(eng, fn, waits, inc, 16)

    def finalize(self, block, final_waits):
        def runner(e, name):
            for wl, fn, (semh, amt) in self.q[name]:
                for h, v in wl:
                    e.wait_ge(h, v)
                ins = fn(e)
                if semh is not None and ins is not None:
                    ins.then_inc(semh, amt)
            if name == "sync":
                for c, v in final_waits:
                    e.wait_ge(c.h, v)
        block.sync(lambda e: runner(e, "sync"))
        block.vector(lambda e: runner(e, "vector"))
        block.scalar(lambda e: runner(e, "scalar"))
        block.tensor(lambda e: runner(e, "tensor"))
        block.gpsimd(lambda e: runner(e, "gpsimd"))


def build(taps=(), max_phase=99):
    from contextlib import ExitStack
    nc = bass.Bass("TRN2", target_bir_lowering=False, debug=False)

    def din(name, shape, dt=F32):
        return nc.dram_tensor(name, list(shape), dt, kind="ExternalInput").ap()

    x_own = din("x_own", [TOWN, D])
    wqa_t = din("wqa_t", [3, 128, 32 * 512])
    wqb_t = din("wqb_t", [NH, 128, 12 * 192])
    wkva_t = din("wkva_t", [128, 32 * KVA])
    wkvb_t = din("wkvb_t", [NH, 128, 4 * 256])
    wo_t = din("wo_t", [8, 32, 128, 512])
    wf_t = din("wf_t", [8, 32, 128, 512])
    x_cov = din("x_cov", [TB, D])
    qnw = din("qnw", [128, 12])
    kvnw = din("kvnw", [128, 4])
    gamma = din("gamma", [1, D])
    beta = din("beta", [1, D])
    h128 = din("h128", [128, 128], BF16)
    h32s = din("h32s", [128, 32], BF16)
    ident_bf = din("ident_bf", [128, 128], BF16)
    ident_f32 = din("ident_f32", [128, 128], F32)
    strip = din("strip", [128, 512], BF16)
    cos_own = din("cos_own", [32, TOWN])
    sin_own = din("sin_own", [32, TOWN])
    cos_cov = din("cos_cov", [32, TB])
    sin_cov = din("sin_cov", [32, TB])

    out = nc.dram_tensor("out", [TOWN, D], F32, kind="ExternalOutput").ap()

    bounce = nc.dram_tensor("bounce", [128, 32 * 1024], BF16, kind="Internal").ap()
    ffn_scr = nc.dram_tensor("ffn_scr", [128, 32 * 1024], F32, kind="Internal").ap()
    attn_scr = nc.dram_tensor("attn_scr", [NH * 128, TOWN], BF16, kind="Internal").ap()
    rowbc = nc.dram_tensor("rowbc", [8, 4096], F32, kind="Internal").ap()

    tap_t = {}
    for name in taps:
        shape, dt = _TAPS[name]
        tap_t[name] = nc.dram_tensor("tap_" + name, list(shape), dt,
                                     kind="ExternalOutput").ap()

    st = ExitStack()
    with st:
        P = Prog(nc)
        P.max_phase = max_phase
        import sys as _sys
        P.serialize = bool(globals().get("SERIALIZE")) or SERIALIZE
        T = P.ctr(st, "T", "tensor")
        V = P.ctr(st, "V", "vector")
        SC = P.ctr(st, "SC", "scalar")
        G = P.ctr(st, "G", "gpsimd")
        sy_lanes = [P.ctr(st, "SYL%d" % i, "sync") for i in range(8)]
        gq_lanes = [P.ctr(st, "GQL%d" % i, "gpsimd") for i in range(4)]
        for _ln in sy_lanes + gq_lanes:
            _ln.is_lane = True
        syq = DmaQ(sy_lanes)
        gq = DmaQ(gq_lanes)
        SY = [syq] * 6
        ALL = [T, V, SC, G] + sy_lanes + gq_lanes
        P.all_ctrs = ALL

        def sb(name, shape, dt, stack=None, side=None):
            return (stack or st).enter_context(
                nc.sbuf_tensor(name, list(shape), dt, side=side))

        def ps(name, shape, dt, stack=None):
            return (stack or st).enter_context(nc.psum_tensor(name, list(shape), dt))

        def join():
            return [(c, c.n) for c in ALL]

        def tap(name, src_ap, waits):
            if name in tap_t:
                tok = P.dma("sync", tap_t[name], src_ap, waits, SY[5])
                if tok is not None:
                    P.tap_gate.append(tok)

        c_h128 = sb("c_h128", [128, 128], BF16)
        c_h32s = sb("c_h32s", [128, 32], BF16)
        c_idbf = sb("c_idbf", [128, 128], BF16)
        c_idf = sb("c_idf", [128, 128], F32)
        c_strip = sb("c_strip", [128, 512], BF16)
        c_qnw = sb("c_qnw", [128, 12], F32)
        c_kvnw = sb("c_kvnw", [128, 4], F32)
        ones_r = sb("ones_r", [1, 128], BF16)
        ones_c = sb("ones_c", [128, 1], BF16)
        onesf = sb("onesf", [128, 1], F32)
        one1 = sb("one1", [1, 1], F32)
        wsq_col = sb("wsq_col", [128, 12], F32)
        wabs = sb("wabs", [128, 258], F32)
        thr_c = sb("thr_c", [128, 1], F32)
        nthr_c = sb("nthr_c", [128, 1], F32)
        wsc_c = sb("wsc_c", [128, 1], F32)
        i2t_c = sb("i2t_c", [128, 1], F32)
        rowtmp = sb("rowtmp", [1, 4096], F32)
        coltmp = sb("coltmp", [128, 2], F32)
        cst = sb("cst", [128, 3], F32)

        ld0 = []
        for dst, src in ((c_h128, h128), (c_h32s, h32s), (c_idbf, ident_bf),
                         (c_idf, ident_f32), (c_strip, strip), (c_qnw, qnw),
                         (c_kvnw, kvnw)):
            ld0.append(P.dma("sync", dst[:, :], src, (), SY[0]))
        m1 = P.emit("vector", lambda e: e.memset(ones_r[:, :], 1.0), (), V)
        m2 = P.emit("vector", lambda e: e.memset(ones_c[:, :], 1.0), (), V)
        m3 = P.emit("vector", lambda e: e.memset(onesf[:, :], 1.0), (), V)
        m4 = P.emit("vector", lambda e: e.memset(one1[:, :], 1.0), (), V)
        m5a = P.emit("vector", lambda e: e.memset(cst[:, 0:1], EPS_LN), (), V)
        m5b = P.emit("vector", lambda e: e.memset(cst[:, 1:2], EPS_RMS), (), V)
        m5c = P.emit("vector", lambda e: e.memset(cst[:, 2:3], 1e-30), (), V)
        consts_tok0 = [m5a, m5b, m5c]
        consts_tok = ld0 + [m1, m2, m3, m4]

        def transposes(src_tile, n_c, dst_write, tok_src, pbank, ident,
                       extra_src_toks=()):
            last = None
            for c0 in range(0, n_c, 4):
                nb = min(4, n_c - c0)
                tm = None
                for k in range(nb):
                    c = c0 + k
                    w = [tok_src] + list(extra_src_toks)
                    if last is not None:
                        w.append(last)
                    tm = P.emit("tensor", lambda e, c=c, k=k: e.transpose(
                        pbank[:, k * 128:(k + 1) * 128],
                        src_tile[:, c * 128:(c + 1) * 128], ident[:, :]),
                        w, T if k == nb - 1 else None)
                last = dst_write(c0, nb, pbank, tm)
            return last

        def row_to_col(row_ap, psc_out, dst_col, waits):
            tm = P.emit("tensor", lambda e: e.matmul(
                psc_out, row_ap, one1[:, :], start=True, stop=True),
                list(waits), T)
            return P.emit("vector", lambda e: e.tensor_copy(out=dst_col, in_=psc_out),
                          [tm], V)

        P.cur_phase = 1
        sXQ = ExitStack()
        xq_fm = sb("xq_fm", [128, 32 * 1024], BF16, sXQ, side="right")
        pA = ExitStack()
        with pA:
            xt = [sb("xtA%d" % i, [128, D], F32, pA) for i in range(2)]
            gammaB = sb("gammaB", [128, D], F32, pA)
            betaB = sb("betaB", [128, D], F32, pA)
            bn6 = sb("bn6", [128, 48], F32, pA)
            mv = sb("mv", [128, 2], F32, pA)
            rln_c = sb("rln_c", [128, 1], F32, pA)
            nmu_c = sb("nmu_c", [128, 1], F32, pA)
            sf_c = sb("sf_c", [128, 1], F32, pA)
            s1 = sb("s1A", [128, D], F32, pA)
            xqi = sb("xqi", [128, D // 2], I32, pA)
            xqb = sb("xqb", [128, D], BF16, pA)
            ptrA = ps("ptrA", [128, 512], BF16, pA)

            gb = P.dma("sync", gammaB[:, :], gamma.partition_broadcast(128), (), SY[0])
            bb = P.dma("sync", betaB[:, :], beta.partition_broadcast(128), (), SY[0])

            ev = None
            tg_prev = None
            for i in range(8):
                b = i % 2
                tl = P.dma("sync", xt[b][:, :], x_own[i * 128:(i + 1) * 128, :],
                           [ev] if i >= 2 else [], SY[0])
                t1 = None
                for kk in range(8):
                    t1 = P.emit("vector", lambda e, b=b, kk=kk: e.bn_stats(
                        bn6[:, kk * 6:(kk + 1) * 6],
                        xt[b][:, kk * 512:(kk + 1) * 512]), [tl, tg_prev], V)
                t2 = P.emit("vector", lambda e: e.bn_aggr(mv[:, :], bn6[:, :]), [t1], V)
                t3 = P.emit("scalar", lambda e: e.activation(
                    rln_c[:, :], mv[:, 1:2], AF.Ln, bias=cst[:, 0:1], scale=1.0), [t2], SC)
                t4 = P.emit("scalar", lambda e: e.activation(
                    rln_c[:, :], rln_c[:, :], AF.Exp, bias=0.0, scale=-0.5), [t3], SC)
                t5 = P.emit("vector", lambda e: e.tensor_scalar(
                    out=nmu_c[:, :], in0=mv[:, 0:1], scalar1=-1.0, scalar2=None,
                    op0=ALU.mult), [t2], V)
                t7 = P.emit("vector", lambda e, b=b: e.tensor_scalar(
                    out=s1[:, :], in0=xt[b][:, :], scalar1=nmu_c[:, :],
                    scalar2=rln_c[:, :], op0=ALU.add, op1=ALU.mult), [t4], V,
                    fwaits=[t5])
                if i == 0:
                    tap("dbg_xn", s1[:, :], [t7])
                    tap("dbg_mv", mv[:, :], [t2])
                    tap("dbg_rln", rln_c[:, :], [t4])
                    tap("dbg_gammaB", gammaB[:, :], [gb])
                    tap("dbg_xt", xt[0][:, :], [tl])
                t8 = P.emit("gpsimd", lambda e: e.tensor_tensor(
                    out=s1[:, :], in0=s1[:, :], in1=gammaB[:, :], op=ALU.mult),
                    [t7, gb], G)
                t9 = P.emit("gpsimd", lambda e: e.tensor_tensor(
                    out=s1[:, :], in0=s1[:, :], in1=betaB[:, :], op=ALU.add),
                    [t8, bb], G)
                ta = P.emit("vector", lambda e: e.reduce_max(
                    sf_c[:, :], s1[:, :], axis=AX, apply_absolute_value=True), [t9], V)
                tb = P.emit("vector", lambda e: e.tensor_scalar(
                    out=coltmp[:, 0:1], in0=sf_c[:, :], scalar1=1e-6, scalar2=1e6,
                    op0=ALU.max, op1=ALU.min), [ta], V)
                tc = P.emit("vector", lambda e: e.reciprocal(
                    coltmp[:, 1:2], coltmp[:, 0:1]), [tb], V)
                td = P.emit("vector", lambda e: e.tensor_scalar(
                    out=coltmp[:, 0:1], in0=coltmp[:, 1:2], scalar1=7.0, scalar2=1e6,
                    op0=ALU.mult, op1=ALU.min), [tc], V)
                te = P.emit("vector", lambda e: e.tensor_scalar(
                    out=sf_c[:, :], in0=coltmp[:, 0:1], scalar1=1e-6, scalar2=None,
                    op0=ALU.max), [td], V)
                tg = None
                for hh in range(2):
                    tf = P.emit("vector", lambda e, hh=hh: e.tensor_scalar(
                        out=xqi[:, :], in0=s1[:, hh * 2048:(hh + 1) * 2048],
                        scalar1=sf_c[:, :], scalar2=None,
                        op0=ALU.mult), [tg], V, fwaits=[te])
                    tg = P.emit("vector", lambda e, hh=hh: e.tensor_copy(
                        out=xqb[:, hh * 2048:(hh + 1) * 2048], in_=xqi[:, :]),
                        [tf], V)
                tg_prev = tg
                if i == 0:
                    tap("dbg_xln", s1[:, :], [t9])
                    tap("dbg_sf", sf_c[:, :], [te])
                    tap("dbg_xqb", xqb[:, :], [tg])

                def wrA(c0, nb, pbank, tokT, i=i):
                    def f(e, c0=c0, nb=nb, i=i):
                        dst = xq_fm[:, :].rearrange("p (c t) -> p c t", c=32)
                        return e.tensor_copy(
                            out=dst[:, c0:c0 + nb, i * 128:(i + 1) * 128],
                            in_=pbank[:, 0:nb * 128])
                    return P.emit("vector", f, [tokT], V)
                ev = transposes(xqb, 32, wrA, tg, ptrA, c_idbf,
                                extra_src_toks=consts_tok)
            xqfm_done = ev
            tap("xq_fm", xq_fm[:, :], [xqfm_done])

        P.cur_phase = 2
        pW = ExitStack()
        with pW:
            wt = [sb("wp1_%d" % i, [128, 2048], F32, pW) for i in range(2)]
            wp1dump = sb("wp1dump", [128, 512], F32, pW)
            vts = {}
            jwW = join()
            for idx in range(64):
                b = idx % 2
                osp, g4 = divmod(idx, 8)
                srcv = wf_t[osp, g4 * 4:(g4 + 1) * 4, :, :].rearrange("g p o -> p g o")
                w = [vts[idx - 2]] if idx >= 2 else list(jwW)
                tl = P.dma("sync", wt[b][:, :].rearrange("p (g o) -> p g o", g=4),
                           srcv, w, SY[1])
                vt2 = vts.get(idx - 1)
                for kq in range(4):
                    vt2 = P.emit("scalar", lambda e, b=b, idx=idx, kq=kq: e.activation(
                        out=wp1dump[:, :], in_=wt[b][:, kq * 512:(kq + 1) * 512],
                        func=AF.Abs,
                        accum_out=wabs[:, idx * 4 + kq:idx * 4 + kq + 1]),
                        [tl, vt2], SC)
                vts[idx] = vt2
            tr = P.emit("vector", lambda e: e.reduce_sum(
                wabs[:, 256:257], wabs[:, 0:256], axis=AX),
                [vts[i] for i in range(64)], V)
            psW = ps("psW", [1, 8], F32, pW)
            tm = P.emit("tensor", lambda e: e.matmul(
                psW[0:1, 0:1], wabs[:, 256:257], onesf[:, :], start=True, stop=True),
                [tr, m3], T)
            tv = P.emit("vector", lambda e: e.tensor_scalar(
                out=rowtmp[0:1, 0:1], in0=psW[0:1, 0:1],
                scalar1=0.5 / float(D * D), scalar2=None, op0=ALU.mult), [tm], V)
            tv2 = P.emit("vector", lambda e: e.tensor_scalar(
                out=rowtmp[0:1, 1:2], in0=rowtmp[0:1, 0:1], scalar1=-1.0,
                scalar2=None, op0=ALU.mult), [tv], V)
            tv3 = P.emit("vector", lambda e: e.tensor_scalar(
                out=rowtmp[0:1, 2:3], in0=rowtmp[0:1, 0:1], scalar1=2.0,
                scalar2=None, op0=ALU.mult), [tv], V)
            tv4 = P.emit("vector", lambda e: e.reciprocal(
                rowtmp[0:1, 3:4], rowtmp[0:1, 2:3]), [tv3], V)
            td1 = P.dma("sync", rowbc[0:1, 0:4], rowtmp[0:1, 0:4], [tv2, tv3, tv4],
                        SY[1])
            td2 = P.dma("sync", thr_c[:, :], rowbc[0:1, 0:1].partition_broadcast(128),
                        [td1], SY[1])
            td3 = P.dma("sync", nthr_c[:, :], rowbc[0:1, 1:2].partition_broadcast(128),
                        [td1], SY[1])
            td4 = P.dma("sync", wsc_c[:, :], rowbc[0:1, 2:3].partition_broadcast(128),
                        [td1], SY[1])
            td5 = P.dma("sync", i2t_c[:, :], rowbc[0:1, 3:4].partition_broadcast(128),
                        [td1], SY[1])
            wp1_done = [td2, td3, td4, td5]

        P.cur_phase = 2.2
        pF = ExitStack()
        with pF:
            y1 = sb("y1", [128, 32 * 1024], BF16, pF)
            psA = [ps("psFA%d" % i, [128, 512], F32, pF) for i in range(2)]
            evh = {}
            jwF = join()
            import os as _os
            _ffa = int(_os.environ.get("FFA_C", "32"))
            for c in range(_ffa):
                for th in range(2):
                    k = (c * 2 + th) % 2
                    w = [xqfm_done] + (jwF if (c == 0 and th == 0) else [])
                    if (c * 2 + th) >= 2:
                        w.append(evh[c * 2 + th - 2])
                    tm = P.emit("tensor", lambda e, c=c, th=th, k=k: e.matmul(
                        psA[k][:, :], c_h128[:, :],
                        xq_fm[:, c * 1024 + th * 512:c * 1024 + th * 512 + 512],
                        start=True, stop=True), w, T)
                    evh[c * 2 + th] = P.emit(
                        "vector", lambda e, c=c, th=th, k=k: e.tensor_copy(
                            out=y1[:, c * 1024 + th * 512:c * 1024 + th * 512 + 512],
                            in_=psA[k][:, :]), [tm], V)
            y1_done = evh[_ffa * 2 - 1]
            P.cur_phase = 2.22
            tb1 = P.dma("sync", bounce[:, :], y1[:, :], [y1_done], SY[2])
        sXQ.close()

        P.cur_phase = 2.3
        pF2 = ExitStack()
        with pF2:
            xh = sb("xh", [128, 32 * 1024], BF16, pF2)
            pY1T = ExitStack()
            y1t = sb("y1t", [128, 32 * 1024], BF16, pY1T)
            tb2 = P.dma("sync", y1t[:, :].rearrange("p (g t) -> p g t", g=32),
                        bounce.rearrange("(g j) (c t) -> (j c) g t", j=4, c=32),
                        [tb1], SY[2])
            psB = [ps("psFB%d" % i, [128, 512], F32, pY1T) for i in range(2)]
            evh = {}
            for g in range(32):
                for tsp in range(2):
                    gi = g * 2 + tsp
                    k = gi % 2
                    tm = None
                    for j in range(4):
                        w = [tb2]
                        if gi >= 2:
                            w.append(evh[gi - 2])
                        tm = P.emit("tensor", lambda e, g=g, tsp=tsp, j=j, k=k: e.matmul(
                            psB[k][32 * j:32 * j + 32, 0:512],
                            c_h32s[32 * j:32 * j + 32, :],
                            y1t[32 * j:32 * j + 32,
                                g * 1024 + tsp * 512:g * 1024 + tsp * 512 + 512],
                            start=True, stop=True, tile_position=(32 * j, 32 * j)),
                            w, T if j == 3 else None)
                    evh[gi] = P.emit("vector", lambda e, g=g, tsp=tsp, k=k: e.tensor_copy(
                        out=xh[:, g * 1024 + tsp * 512:g * 1024 + tsp * 512 + 512],
                        in_=psB[k][:, :]), [tm], V)
            xh_done = evh[63]
            tap("xh", xh[:, :], [xh_done])
            pY1T.close()

            P.cur_phase = 2.4
            pOPF = ExitStack()
            opfm = sb("opfm", [128, 32 * 1024], BF16, pOPF, side="right")
            pMAIN = ExitStack()
            outsp = sb("outsp", [128, 8 * 512], BF16, pMAIN)
            wfr = [sb("wfr%d" % i, [128, 512], F32, pMAIN) for i in range(3)]
            wqg = [sb("wqg%d" % i, [128, 512], BF16, pMAIN) for i in range(2)]
            qi32v = sb("qi32v", [128, 512], I32, pMAIN)
            qi32g = sb("qi32g", [128, 512], I32, pMAIN)
            psM = [ps("psM%d" % i, [128, 512], F32, pMAIN) for i in range(8)]
            qt = {}
            mm_last = {}
            tr_done = [None] * 8
            op_done = [None] * 8
            for osp in range(8):
                for g in range(32):
                    idx = osp * 32 + g
                    rb = idx % 3
                    qb = idx % 2
                    w = list(wp1_done)
                    if idx < 3:
                        w.append(xh_done)
                    if idx >= 3:
                        w.append(qt[idx - 3])
                    tl = P.dma("sync", wfr[rb][:, :], wf_t[osp, g, :, :], w, SY[2])
                    qe, qc = (("vector", V) if idx % 2 == 0 else ("gpsimd", G))
                    qa = qi32v if idx % 2 == 0 else qi32g
                    wq_war = [mm_last[idx - 2]] if idx >= 2 else []
                    g1 = P.emit(qe, lambda e, rb=rb, qa=qa: e.tensor_scalar(
                        out=qa[:, :], in0=wfr[rb][:, :], scalar1=i2t_c[:, :],
                        scalar2=None, op0=ALU.mult), [tl] + wq_war, qc)
                    qt[idx] = P.emit(qe, lambda e, qb=qb, qa=qa: e.tensor_scalar(
                        out=wqg[qb][:, :], in0=qa[:, :], scalar1=1.0, scalar2=-1.0,
                        op0=ALU.min, op1=ALU.max), [g1], qc)
                    tm = None
                    for tt in range(8):
                        w = [qt[idx], xh_done]
                        if g == 0:
                            if tr_done[tt] is not None:
                                w.append(tr_done[tt])
                        tm = P.emit("tensor", lambda e, g=g, tt=tt, qb=qb: e.matmul(
                            psM[tt][:, :],
                            xh[:, g * 1024 + tt * 128:g * 1024 + tt * 128 + 128],
                            wqg[qb][:, :], start=(g == 0), stop=(g == 31)),
                            w, T if (g == 31 or tt == 7) else None)
                    mm_last[idx] = tm
                for tt in range(8):
                    ev = P.emit("vector", lambda e, tt=tt: e.tensor_scalar(
                        out=outsp[:, tt * 512:(tt + 1) * 512],
                        in0=psM[tt][:, :], scalar1=wsc_c[:, :],
                        scalar2=None, op0=ALU.mult),
                        [mm_last[osp * 32 + 31]] + ([op_done[tt]] if op_done[tt] else []),
                        V)
                    tm2 = None
                    for cj in range(4):
                        tm2 = P.emit("tensor", lambda e, tt=tt, cj=cj: e.matmul(
                            psM[tt][:, cj * 128:(cj + 1) * 128],
                            outsp[:, tt * 512 + cj * 128:tt * 512 + cj * 128 + 128],
                            c_idbf[:, :], start=True, stop=True),
                            [ev], T if cj == 3 else None)
                    tr_done[tt] = tm2
                    def fop(e, tt=tt, osp=osp):
                        dst = opfm[:, :].rearrange("p (c t) -> p c t", c=32)
                        return e.tensor_copy(
                            out=dst[:, osp * 4:osp * 4 + 4, tt * 128:(tt + 1) * 128],
                            in_=psM[tt][:, :])
                    op_done[tt] = P.emit("vector", fop, [tm2], V)
            opfm_done = [t for t in op_done]
            tap("outpre", opfm[:, :], opfm_done)
            pMAIN.close()

        P.cur_phase = 2.5
        pF3 = ExitStack()
        with pF3:
            y2 = sb("y2", [128, 32 * 1024], BF16, pF3)
            psA2 = [ps("psA2%d" % i, [128, 512], F32, pF3) for i in range(2)]
            evh = {}
            for c in range(32):
                for th in range(2):
                    gi = c * 2 + th
                    k = gi % 2
                    w = list(opfm_done)
                    if gi >= 2:
                        w.append(evh[gi - 2])
                    tm = P.emit("tensor", lambda e, c=c, th=th, k=k: e.matmul(
                        psA2[k][:, :], c_h128[:, :],
                        opfm[:, c * 1024 + th * 512:c * 1024 + th * 512 + 512],
                        start=True, stop=True), w, T)
                    evh[gi] = P.emit("vector", lambda e, c=c, th=th, k=k: e.tensor_copy(
                        out=y2[:, c * 1024 + th * 512:c * 1024 + th * 512 + 512],
                        in_=psA2[k][:, :]), [tm], V)
            tb3 = P.dma("sync", bounce[:, :], y2[:, :], [evh[63]], SY[2])
        pOPF.close()
        P.cur_phase = 2.6
        pF4 = ExitStack()
        with pF4:
            y2t = sb("y2t", [128, 32 * 1024], BF16, pF4)
            tb4 = P.dma("sync", y2t[:, :].rearrange("p (g t) -> p g t", g=32),
                        bounce.rearrange("(g j) (c t) -> (j c) g t", j=4, c=32),
                        [tb3], SY[2])
            psB2 = [ps("psB2%d" % i, [128, 512], F32, pF4) for i in range(2)]
            sbf = [sb("sbf%d" % i, [128, 512], F32, pF4) for i in range(2)]
            dstw = ffn_scr.rearrange("(gg j) (cc t) -> (j cc) gg t", j=4, cc=32)
            dvh = {}
            for g in range(32):
                for tsp in range(2):
                    gi = g * 2 + tsp
                    k = gi % 2
                    tm = None
                    for j in range(4):
                        w = [tb4]
                        if gi >= 2:
                            w.append(dvh[gi - 2])
                        tm = P.emit("tensor", lambda e, g=g, tsp=tsp, j=j, k=k: e.matmul(
                            psB2[k][32 * j:32 * j + 32, 0:512],
                            c_h32s[32 * j:32 * j + 32, :],
                            y2t[32 * j:32 * j + 32,
                                g * 1024 + tsp * 512:g * 1024 + tsp * 512 + 512],
                            start=True, stop=True, tile_position=(32 * j, 32 * j)),
                            w, T if j == 3 else None)
                    ev = P.emit("vector", lambda e, k=k: e.tensor_copy(
                        out=sbf[k][:, :], in_=psB2[k][:, :]), [tm], V)
                    dvh[gi] = P.dma("sync", dstw[:, g, tsp * 512:tsp * 512 + 512],
                                    sbf[k][:, :], [ev], SY[2])
            ffn_done = list(dvh.values())
            tap("ffn", ffn_scr, ffn_done)

        P.cur_phase = 3
        pM1 = ExitStack()
        qlat = sb("qlat", [128, 12 * TOWN], BF16, pM1)
        pQ = ExitStack()
        with pQ:
            murln_row = sb("murln_row", [1, TOWN], F32, pQ)
            rln_row = sb("rln_row", [1, TOWN], F32, pQ)
            xfm = sb("xfm", [128, 32 * 1024], BF16, pQ)
            xt2 = [sb("xtB%d" % i, [128, D], F32, pQ) for i in range(2)]
            bn6b = sb("bn6b", [128, 48], F32, pQ)
            mvb = sb("mvb", [128, 2], F32, pQ)
            rlnb_c = sb("rlnb_c", [128, 1], F32, pQ)
            nmub_c = sb("nmub_c", [128, 1], F32, pQ)
            mnrb_c = sb("mnrb_c", [128, 1], F32, pQ)
            prowB = ps("prowB", [1, 512], F32, pQ)
            prowB2 = ps("prowB2", [1, 512], F32, pQ)
            ptrB = ps("ptrB", [128, 512], F32, pQ)
            jw = join()
            ev = None
            rtok = None
            he1 = he2 = None
            t6hist = {}
            for i in range(8):
                b = i % 2
                tl = P.dma("sync", xt2[b][:, :], x_own[i * 128:(i + 1) * 128, :],
                           (jw if i < 2 else [ev, t6hist[i - 2]]), SY[3])
                t1 = None
                for kk in range(8):
                    t1 = P.emit("vector", lambda e, b=b, kk=kk: e.bn_stats(
                        bn6b[:, kk * 6:(kk + 1) * 6],
                        xt2[b][:, kk * 512:(kk + 1) * 512]), [tl], V)
                t2 = P.emit("vector", lambda e: e.bn_aggr(mvb[:, :], bn6b[:, :]),
                            [t1], V)
                t3 = P.emit("scalar", lambda e: e.activation(
                    rlnb_c[:, :], mvb[:, 1:2], AF.Ln, bias=cst[:, 0:1], scale=1.0), [t2], SC)
                t4 = P.emit("scalar", lambda e: e.activation(
                    rlnb_c[:, :], rlnb_c[:, :], AF.Exp, bias=0.0, scale=-0.5), [t3], SC)
                t5 = P.emit("vector", lambda e: e.tensor_scalar(
                    out=nmub_c[:, :], in0=mvb[:, 0:1], scalar1=-1.0, scalar2=None,
                    op0=ALU.mult), [t2], V)
                t6 = P.emit("vector", lambda e: e.tensor_scalar(
                    out=mnrb_c[:, :], in0=nmub_c[:, :], scalar1=rlnb_c[:, :],
                    scalar2=None, op0=ALU.mult), [t4, t5], V)
                t6hist[i] = t6
                half, col = divmod(i, 4)
                hw = [t6] + ([rtok] if rtok else [])
                if col == 0 and half == 1:
                    hw += [he1, he2]
                r1 = P.emit("tensor", lambda e, col=col: e.matmul(
                    prowB[0:1, col * 128:(col + 1) * 128], mnrb_c[:, :], c_idf[:, :],
                    start=True, stop=True), hw, T)
                rtok = P.emit("tensor", lambda e, col=col: e.matmul(
                    prowB2[0:1, col * 128:(col + 1) * 128], rlnb_c[:, :], c_idf[:, :],
                    start=True, stop=True), [t4, r1], T)
                if col == 3:
                    he1 = P.emit("vector", lambda e, half=half: e.tensor_copy(
                        out=murln_row[0:1, half * 512:(half + 1) * 512],
                        in_=prowB[0:1, :]), [rtok], V)
                    he2 = P.emit("vector", lambda e, half=half: e.tensor_copy(
                        out=rln_row[0:1, half * 512:(half + 1) * 512],
                        in_=prowB2[0:1, :]), [rtok], V)

                def wrB(c0, nb, pbank, tokT, i=i):
                    def f(e, c0=c0, nb=nb, i=i):
                        dst = xfm[:, :].rearrange("p (c t) -> p c t", c=32)
                        return e.activation(
                            out=dst[:, c0:c0 + nb, i * 128:(i + 1) * 128],
                            in_=pbank[:, 0:nb * 128], func=AF.Copy)
                    return P.emit("scalar", f, [tokT], SC)
                ev = transposes(xt2[b], 32, wrB, tl, ptrB, c_idf)
            xfm_done = ev
            tap("xfm", xfm[:, :], [xfm_done])
            d1 = P.dma("sync", rowbc[1:2, 0:1024], murln_row[0:1, :], [he1], SY[3])
            d2 = P.dma("sync", rowbc[2:3, 0:1024], rln_row[0:1, :], [he2], SY[3])
            B1q = sb("B1q", [128, 1024], F32, pQ)
            rlnB = sb("rlnB", [128, 1024], F32, pQ)
            d3 = P.dma("sync", B1q[:, :], rowbc[1:2, 0:1024].partition_broadcast(128),
                       [d1], SY[3])
            d4 = P.dma("sync", rlnB[:, :], rowbc[2:3, 0:1024].partition_broadcast(128),
                       [d2], SY[3])

            wqr = [sb("wqr0", [128, 32 * 512], BF16, pQ)]
            psQ = [ps("psQ%d" % i, [128, 512], F32, pQ) for i in range(2)]
            psR = ps("psRq", [1, 512], F32, pQ)
            psC = ps("psCq", [128, 1], F32, pQ)
            qraw = sb("qraw", [128, 512], F32, pQ)
            qc_last = None
            psq_free = [None, None]
            wq_free = None
            ws_done = {}
            for MB in range(3):
                wb = 0
                tl = P.dma("gpsimd", wqr[wb][:, :], wqa_t[MB, :, :],
                           [wq_free] if wq_free else join(), gq)
                for Mi in range(4):
                    M = MB * 4 + Mi
                    tm = None
                    for g in range(32):
                        w = [tl, m2]
                        if M > 0:
                            w.append(ws_done[M - 1])
                        tm = P.emit("tensor", lambda e, Mi=Mi, g=g, wb=wb: e.matmul(
                            psR[0:1, 0:128], ones_c[:, :],
                            wqr[wb][:, g * 512 + Mi * 128:g * 512 + Mi * 128 + 128],
                            start=(g == 0), stop=(g == 31)), w,
                            T if g == 31 else None)
                    er = P.emit("vector", lambda e: e.tensor_copy(
                        out=rowtmp[0:1, 0:128], in_=psR[0:1, 0:128]), [tm], V)
                    ws_done[M] = row_to_col(rowtmp[0:1, 0:128], psC[:, 0:1],
                                            wsq_col[:, M:M + 1], [er, m4])
                for Mi in range(4):
                    M = MB * 4 + Mi
                    for th in range(2):
                        k = (M * 2 + th) % 2
                        tm = None
                        for g in range(32):
                            w = [tl, xfm_done]
                            if g == 0 and psq_free[k] is not None:
                                w.append(psq_free[k])
                            if g == 0:
                                w.append(ws_done[11] if M == 11 else ws_done[M])
                            tm = P.emit("tensor",
                                        lambda e, Mi=Mi, g=g, th=th, k=k, wb=wb: e.matmul(
                                            psQ[k][:, :],
                                            wqr[wb][:, g * 512 + Mi * 128:
                                                    g * 512 + Mi * 128 + 128],
                                            xfm[:, g * 1024 + th * 512:
                                                g * 1024 + th * 512 + 512],
                                            start=(g == 0), stop=(g == 31)),
                                        w, T if g == 31 else None)
                        c1 = P.emit("vector", lambda e, k=k, th=th: e.scalar_tensor_tensor(
                            out=qraw[:, :], in0=psQ[k][:, :], scalar=1.0,
                            in1=rlnB[:, th * 512:th * 512 + 512],
                            op0=ALU.bypass, op1=ALU.mult), [tm, d4, qc_last], V)
                        psq_free[k] = c1
                        qc_last = P.emit(
                            "vector", lambda e, M=M, th=th: e.scalar_tensor_tensor(
                                out=qlat[:, M * 1024 + th * 512:M * 1024 + th * 512 + 512],
                                in0=B1q[:, th * 512:th * 512 + 512],
                                scalar=wsq_col[:, M:M + 1], in1=qraw[:, :],
                                op0=ALU.mult, op1=ALU.add),
                            [c1, d3], V, fwaits=[ws_done[M]])
                wq_free = qc_last
            sqs = sb("sqs", [128, 512], BF16, pQ)
            psS2 = ps("psS2", [1, 512], F32, pQ)
            rms_ev = {}
            for th in range(2):
                tm = None
                sq_tok = None
                for M in range(12):
                    w = [qc_last, tm]
                    if M == 0 and th == 1:
                        w.append(rms_ev[0])
                    sq_tok = P.emit("vector", lambda e, M=M, th=th: e.tensor_tensor(
                        out=sqs[:, :],
                        in0=qlat[:, M * 1024 + th * 512:M * 1024 + th * 512 + 512],
                        in1=qlat[:, M * 1024 + th * 512:M * 1024 + th * 512 + 512],
                        op=ALU.mult), w, V)
                    tm = P.emit("tensor", lambda e, M=M, th=th: e.matmul(
                        psS2[0:1, :], ones_c[:, :], sqs[:, :],
                        start=(M == 0), stop=(M == 11)),
                        [sq_tok] + ([rms_ev[0]] if (M == 0 and th == 1) else []), T)
                rms_ev[th] = P.emit("vector", lambda e, th=th: e.tensor_copy(
                    out=rowtmp[0:1, th * 512:th * 512 + 512], in_=psS2[0:1, :]),
                    [tm], V)
            s3 = P.emit("scalar", lambda e: e.activation(
                rowtmp[0:1, 1024:2048], rowtmp[0:1, 0:1024], AF.Ln,
                bias=cst[0:1, 1:2], scale=1.0 / QL), [rms_ev[0], rms_ev[1]], SC)
            s4 = P.emit("scalar", lambda e: e.activation(
                rowtmp[0:1, 1024:2048], rowtmp[0:1, 1024:2048], AF.Exp,
                bias=0.0, scale=-0.5), [s3], SC)
            d5 = P.dma("sync", rowbc[3:4, 0:1024], rowtmp[0:1, 1024:2048], [s4], SY[3])
            rmsqB = sb("rmsqB", [128, 1024], F32, pQ)
            d6 = P.dma("sync", rmsqB[:, :], rowbc[3:4, 0:1024].partition_broadcast(128),
                       [d5], SY[3])
            ql_done = None
            for M in range(12):
                ql_done = P.emit("vector", lambda e, M=M: e.scalar_tensor_tensor(
                    out=qlat[:, M * 1024:(M + 1) * 1024],
                    in0=qlat[:, M * 1024:(M + 1) * 1024],
                    scalar=c_qnw[:, M:M + 1], in1=rmsqB[:, :],
                    op0=ALU.mult, op1=ALU.mult), [d6, qc_last], V)
            qlat_done = ql_done
            tap("qlat", qlat[:, :], [qlat_done])

        P.cur_phase = 4
        pM2 = ExitStack()
        kvlat = sb("kvlat", [128, 4 * TB], BF16, pM2)
        kpe = sb("kpe", [65, TB], BF16, pM2)
        pC = ExitStack()
        with pC:
            wkva = sb("wkva", [128, 32 * KVA], BF16, pC)
            xt3 = [sb("xtC0", [128, D], F32, pC)]
            xfsp = sb("xfsp", [128, 32 * 512], BF16, pC)
            bn6c = sb("bn6c", [128, 48], F32, pC)
            mvc = sb("mvc", [128, 2], F32, pC)
            rlnc_c = sb("rlnc_c", [128, 1], F32, pC)
            nmuc_c = sb("nmuc_c", [128, 1], F32, pC)
            mnrc_c = sb("mnrc_c", [128, 1], F32, pC)
            kvc = sb("kvc", [128, 6 * 512], F32, pC)
            kvr = sb("kvr", [128, 512], F32, pC)
            sqc = sb("sqc", [128, 512], F32, pC)
            mB = sb("mB_c", [128, 512], F32, pC)
            rB = sb("rB_c", [128, 512], F32, pC)
            rkB = sb("rkB_c", [128, 512], F32, pC)
            ropes = sb("ropes", [32, 512], F32, pC)
            ropes2 = sb("ropes2", [32, 512], F32, pC)
            cosc = sb("cosc", [32, 512], F32, pC)
            sinc = sb("sinc", [32, 512], F32, pC)
            prowC = ps("prowC", [1, 512], F32, pC)
            prowC2 = ps("prowC2", [1, 512], F32, pC)
            ptrC = ps("ptrC", [128, 512], F32, pC)
            psK = [ps("psK%d" % i, [128, 512], F32, pC) for i in range(2)]
            psRC = ps("psRC", [1, 512], F32, pC)
            psCC = ps("psCC", [128, 1], F32, pC)
            wskv_col = sb("wskv_col", [128, 6], F32, pC)

            jw = join()
            lt = P.dma("gpsimd", wkva[:, :], wkva_t, jw, gq)
            ws_kv_done = {}
            prev = None
            _kvoff = [0, 128, 256, 384, 512, 544]
            _kvwid = [128, 128, 128, 128, 32, 32]
            for M in range(6):
                wid = _kvwid[M]
                off = _kvoff[M]
                tm = None
                for g in range(32):
                    w = [lt, m2] + ([prev] if prev else [])
                    tm = P.emit("tensor", lambda e, off=off, g=g, wid=wid: e.matmul(
                        psRC[0:1, 0:wid], ones_c[:, :],
                        wkva[:, g * KVA + off:g * KVA + off + wid],
                        start=(g == 0), stop=(g == 31)), w, T if g == 31 else None)
                er = P.emit("vector", lambda e, wid=wid: e.tensor_copy(
                    out=rowtmp[0:1, 0:wid], in_=psRC[0:1, 0:wid]), [tm], V)
                ws_kv_done[M] = row_to_col(rowtmp[0:1, 0:wid], psCC[0:wid, 0:1],
                                           wskv_col[0:wid, M:M + 1], [er, m4])
                prev = ws_kv_done[M]

            kv_done = None
            t6c = {}
            db1_prev = db4_prev = None
            ev_sp = None
            for sp in range(8):
                rtokc = None
                lc1 = P.dma("sync", cosc[:, :], cos_cov[:, sp * 512:(sp + 1) * 512],
                            jw + ([kv_done] if kv_done else []), SY[4])
                lc2 = P.dma("sync", sinc[:, :], sin_cov[:, sp * 512:(sp + 1) * 512],
                            jw + ([kv_done] if kv_done else []), SY[4])
                for q in range(4):
                    i = sp * 4 + q
                    b = 0
                    w = [kv_done] if (sp > 0 and q < 2) else []
                    if i >= 1:
                        w = w + ([ev_sp] if ev_sp else []) + [t6c[i - 1]]
                    tl = P.dma("sync", xt3[b][:, :], x_cov[i * 128:(i + 1) * 128, :],
                               (jw + w) if i < 2 else w, SY[4])
                    t1 = None
                    for kk in range(8):
                        t1 = P.emit("vector", lambda e, b=b, kk=kk: e.bn_stats(
                            bn6c[:, kk * 6:(kk + 1) * 6],
                            xt3[b][:, kk * 512:(kk + 1) * 512]), [tl], V)
                    t2 = P.emit("vector", lambda e: e.bn_aggr(mvc[:, :], bn6c[:, :]),
                                [t1], V)
                    t3 = P.emit("scalar", lambda e: e.activation(
                        rlnc_c[:, :], mvc[:, 1:2], AF.Ln, bias=cst[:, 0:1], scale=1.0),
                        [t2], SC)
                    t4 = P.emit("scalar", lambda e: e.activation(
                        rlnc_c[:, :], rlnc_c[:, :], AF.Exp, bias=0.0, scale=-0.5),
                        [t3], SC)
                    t5 = P.emit("vector", lambda e: e.tensor_scalar(
                        out=nmuc_c[:, :], in0=mvc[:, 0:1], scalar1=-1.0, scalar2=None,
                        op0=ALU.mult), [t2], V)
                    t6 = P.emit("vector", lambda e: e.tensor_scalar(
                        out=mnrc_c[:, :], in0=nmuc_c[:, :], scalar1=rlnc_c[:, :],
                        scalar2=None, op0=ALU.mult), [t4, t5], V)
                    t6c[i] = t6
                    r1 = P.emit("tensor", lambda e, q=q: e.matmul(
                        prowC[0:1, q * 128:(q + 1) * 128], mnrc_c[:, :], c_idf[:, :],
                        start=True, stop=True), [t6] + ([rtokc] if rtokc else []), T)
                    rtokc = P.emit("tensor", lambda e, q=q: e.matmul(
                        prowC2[0:1, q * 128:(q + 1) * 128], rlnc_c[:, :], c_idf[:, :],
                        start=True, stop=True), [t4, r1], T)

                    def wrC(c0, nb, pbank, tokT, q=q):
                        def f(e, c0=c0, nb=nb, q=q):
                            dst = xfsp[:, :].rearrange("p (c t) -> p c t", c=32)
                            return e.activation(
                                out=dst[:, c0:c0 + nb, q * 128:(q + 1) * 128],
                                in_=pbank[:, 0:nb * 128], func=AF.Copy)
                        return P.emit("scalar", f, [tokT], SC)
                    ev_sp = transposes(xt3[b], 32, wrC, tl, ptrC, c_idf)
                e1 = P.emit("vector", lambda e: e.tensor_copy(
                    out=rowtmp[0:1, 1024:1536], in_=prowC[0:1, 0:512]),
                    [rtokc] + ([db1_prev] if db1_prev else []), V)
                e2 = P.emit("vector", lambda e: e.tensor_copy(
                    out=rowtmp[0:1, 1536:2048], in_=prowC2[0:1, 0:512]), [rtokc], V)
                db1 = P.dma("sync", rowbc[4:5, 0:1024], rowtmp[0:1, 1024:2048],
                            [e1, e2], SY[4])
                db1_prev = db1
                db2 = P.dma("sync", mB[:, :], rowbc[4:5, 0:512].partition_broadcast(128),
                            [db1], SY[4])
                db3 = P.dma("sync", rB[:, :], rowbc[4:5, 512:1024].partition_broadcast(128),
                            [db1], SY[4])
                cor = {}
                for M in range(6):
                    wid = _kvwid[M]
                    off = _kvoff[M]
                    k = M % 2
                    tm = None
                    for g in range(32):
                        w = [lt, ev_sp]
                        if g == 0 and M >= 2:
                            w.append(cor[M - 2])
                        tm = P.emit("tensor", lambda e, off=off, g=g, wid=wid, k=k: e.matmul(
                            psK[k][0:wid, :],
                            wkva[:, g * KVA + off:g * KVA + off + wid],
                            xfsp[:, g * 512:(g + 1) * 512],
                            start=(g == 0), stop=(g == 31)), w,
                            T if g == 31 else None)
                    c1 = P.emit("vector", lambda e, M=M, wid=wid, k=k: e.scalar_tensor_tensor(
                        out=kvr[0:wid, :], in0=psK[k][0:wid, :], scalar=1.0,
                        in1=rB[0:wid, :], op0=ALU.bypass, op1=ALU.mult),
                        [tm, db3] + ([cor[M - 1]] if M >= 1 else []), V)
                    cor[M] = P.emit("vector", lambda e, M=M, wid=wid: e.scalar_tensor_tensor(
                        out=kvc[0:wid, M * 512:(M + 1) * 512], in0=mB[0:wid, :],
                        scalar=wskv_col[0:wid, M:M + 1], in1=kvr[0:wid, :],
                        op0=ALU.mult, op1=ALU.add), [c1, db2], V,
                        fwaits=[ws_kv_done[M]])
                tm = None
                sq_tok = None
                for M in range(4):
                    sq_tok = P.emit("vector", lambda e, M=M: e.tensor_tensor(
                        out=sqc[:, :], in0=kvc[:, M * 512:(M + 1) * 512],
                        in1=kvc[:, M * 512:(M + 1) * 512], op=ALU.mult),
                        [cor[M], tm], V)
                    tm = P.emit("tensor", lambda e, M=M: e.matmul(
                        psRC[0:1, :], onesf[:, :], sqc[:, :],
                        start=(M == 0), stop=(M == 3)), [sq_tok, m3], T)
                er = P.emit("vector", lambda e: e.tensor_copy(
                    out=rowtmp[0:1, 2048:2560], in_=psRC[0:1, :]), [tm], V)
                s5 = P.emit("scalar", lambda e: e.activation(
                    rowtmp[0:1, 2560:3072], rowtmp[0:1, 2048:2560], AF.Ln,
                    bias=cst[0:1, 1:2], scale=1.0 / KVL),
                    [er] + ([db4_prev] if db4_prev else []), SC)
                s6 = P.emit("scalar", lambda e: e.activation(
                    rowtmp[0:1, 2560:3072], rowtmp[0:1, 2560:3072], AF.Exp,
                    bias=0.0, scale=-0.5), [s5], SC)
                db4 = P.dma("sync", rowbc[5:6, 0:512], rowtmp[0:1, 2560:3072],
                            [s6], SY[4])
                db4_prev = db4
                db5 = P.dma("sync", rkB[:, :], rowbc[5:6, 0:512].partition_broadcast(128),
                            [db4], SY[4])
                kv_ev = {}
                for M in range(4):
                    kv_ev[M] = P.emit("vector", lambda e, M=M, sp=sp: e.scalar_tensor_tensor(
                        out=kvlat[:, M * TB + sp * 512:M * TB + sp * 512 + 512],
                        in0=kvc[:, M * 512:(M + 1) * 512],
                        scalar=c_kvnw[:, M:M + 1], in1=rkB[:, :],
                        op0=ALU.mult, op1=ALU.mult), [cor[M], db5], V)
                ro1 = P.emit("vector", lambda e: e.tensor_tensor(
                    out=ropes[:, :], in0=kvc[0:32, 4 * 512:5 * 512],
                    in1=cosc[:, :], op=ALU.mult), [cor[4], cor[5], lc1], V)
                ro2 = P.emit("vector", lambda e: e.tensor_tensor(
                    out=ropes2[:, :], in0=kvc[0:32, 5 * 512:6 * 512],
                    in1=sinc[:, :], op=ALU.mult), [cor[5], lc2], V)
                ro3 = P.emit("vector", lambda e, sp=sp: e.tensor_tensor(
                    out=kpe[0:32, sp * 512:(sp + 1) * 512], in0=ropes[:, :],
                    in1=ropes2[:, :], op=ALU.subtract), [ro1, ro2], V)
                ro4 = P.emit("vector", lambda e: e.tensor_tensor(
                    out=ropes[:, :], in0=kvc[0:32, 4 * 512:5 * 512],
                    in1=sinc[:, :], op=ALU.mult), [ro3, cor[4]], V)
                ro5 = P.emit("vector", lambda e: e.tensor_tensor(
                    out=ropes2[:, :], in0=kvc[0:32, 5 * 512:6 * 512],
                    in1=cosc[:, :], op=ALU.mult), [ro3, cor[5]], V)
                ro6 = P.emit("vector", lambda e, sp=sp: e.tensor_tensor(
                    out=kpe[32:64, sp * 512:(sp + 1) * 512], in0=ropes[:, :],
                    in1=ropes2[:, :], op=ALU.add), [ro4, ro5], V)
                kv_done = ro6
                kv_sp_done = [kv_ev[M] for M in range(4)] + [ro6]
            kvlat_done = kv_sp_done
            tap("kvlat", kvlat[:, :], kvlat_done)
            tap("kpe", kpe[0:64, :], kvlat_done)

        P.cur_phase = 5
        pT = ExitStack()
        with pT:
            coso = sb("coso", [32, TOWN], F32, pT)
            sino = sb("sino", [32, TOWN], F32, pT)
            wqbr = [sb("wqbr%d" % i, [128, 12 * 192], BF16, pT) for i in range(2)]
            wkbr = [sb("wkbr%d" % i, [128, 4 * 256], BF16, pT) for i in range(2)]
            qn = sb("qn", [128, TOWN], BF16, pT)
            qpe = sb("qpe", [65, TOWN], BF16, pT)
            kn = sb("kn", [128, TB], BF16, pT)
            vfm = sb("vfm", [128, TB], BF16, pT)
            vtm = sb("vtm", [128, TB], BF16, pT)
            ptb = sb("ptb", [128, 32 * 512], BF16, pT)
            kpe2 = sb("kpe2", [64, TB], BF16, pT)
            attnh = sb("attnh", [128, TOWN], BF16, pT)
            q2s = sb("q2s", [128, TOWN], BF16, pT)
            k2s = sb("k2s", [128, TB], BF16, pT)
            negm = sb("negm", [1, TOWN], BF16, pT)
            linvB = sb("linvB", [128, TOWN], BF16, pT)
            rop1 = sb("rop1", [32, 512], F32, pT)
            rop2 = sb("rop2", [32, 512], F32, pT)
            rop3 = sb("rop3", [32, 512], F32, pT)
            psq = [ps("psq%d" % i, [128, 512], F32, pT) for i in range(2)]
            psq2 = ps("psq2", [32, 512], F32, pT)
            pss = [ps("pss%d" % i, [128, 512], F32, pT) for i in range(2)]
            psr = ps("psr", [1, 512], F32, pT)
            pstr = ps("pstr", [128, 512], BF16, pT)
            psav = ps("psavb", [128, 512], F32, pT)

            jw = join()
            lo1 = P.dma("sync", coso[:, :], cos_own, jw, SY[4])
            lo2 = P.dma("sync", sino[:, :], sin_own, jw, SY[4])
            mko = P.emit("vector", lambda e: e.memset(kpe[64:65, :], 1.0), jw, V)
            prev_head = {}
            for h in range(NH):
                hb = h % 2
                wl1 = P.dma("gpsimd", wqbr[hb][:, :], wqb_t[h, :, :],
                            jw if h < 2 else [prev_head[h - 2]["qdone"]], gq)
                wl2 = P.dma("gpsimd", wkbr[hb][:, :], wkvb_t[h, :, :],
                            jw if h < 2 else [prev_head[h - 2]["kvdone"]], gq)

                qev = {}
                for th in range(2):
                    k = th
                    tm = None
                    for g in range(12):
                        w = [wl1, qlat_done]
                        if h > 0 and g == 0:
                            w.append(prev_head[h - 1]["s_done"])
                        tm = P.emit("tensor", lambda e, g=g, th=th, k=k, hb=hb: e.matmul(
                            psq[k][:, :],
                            wqbr[hb][:, g * 192:g * 192 + 128],
                            qlat[:, g * 1024 + th * 512:g * 1024 + th * 512 + 512],
                            start=(g == 0), stop=(g == 11)), w,
                            T if g == 11 else None)
                    qev[th] = P.emit("vector", lambda e, th=th, k=k: e.tensor_scalar(
                        out=qn[:, th * 512:th * 512 + 512], in0=psq[k][:, :],
                        scalar1=SCALE, scalar2=None, op0=ALU.mult), [tm], V)
                pe_ev = {}
                for th in range(2):
                    k = th
                    tm1 = None
                    for g in range(12):
                        w = [wl1, qlat_done, qev[th]]
                        tm1 = P.emit("tensor", lambda e, g=g, th=th, k=k, hb=hb: e.matmul(
                            psq[k][0:32, 0:512],
                            wqbr[hb][:, g * 192 + 128:g * 192 + 160],
                            qlat[:, g * 1024 + th * 512:g * 1024 + th * 512 + 512],
                            start=(g == 0), stop=(g == 11)), w,
                            T if g == 11 else None)
                    tm2 = None
                    for g in range(12):
                        w2_ = [tm1]
                        if g == 0 and th == 1:
                            w2_.append(pe_ev[0])
                        tm2 = P.emit("tensor", lambda e, g=g, th=th, k=k, hb=hb: e.matmul(
                            psq2[0:32, 0:512],
                            wqbr[hb][:, g * 192 + 160:g * 192 + 192],
                            qlat[:, g * 1024 + th * 512:g * 1024 + th * 512 + 512],
                            start=(g == 0), stop=(g == 11)), w2_,
                            T if g == 11 else None)
                    p1 = P.emit("vector", lambda e, th=th, k=k: e.tensor_tensor(
                        out=rop1[:, :], in0=psq[k][0:32, 0:512],
                        in1=coso[:, th * 512:th * 512 + 512], op=ALU.mult),
                        [tm1, tm2, lo1], V)
                    p2 = P.emit("vector", lambda e, th=th, k=k: e.tensor_tensor(
                        out=rop2[:, :], in0=psq2[0:32, 0:512],
                        in1=sino[:, th * 512:th * 512 + 512], op=ALU.mult),
                        [tm2, lo2], V)
                    p3 = P.emit("vector", lambda e: e.tensor_tensor(
                        out=rop3[:, :], in0=rop1[:, :], in1=rop2[:, :],
                        op=ALU.subtract), [p1, p2], V)
                    p4 = P.emit("vector", lambda e, th=th: e.tensor_scalar(
                        out=qpe[0:32, th * 512:th * 512 + 512], in0=rop3[:, :],
                        scalar1=SCALE, scalar2=None, op0=ALU.mult), [p3], V)
                    p5 = P.emit("vector", lambda e, th=th, k=k: e.tensor_tensor(
                        out=rop1[:, :], in0=psq[k][0:32, 0:512],
                        in1=sino[:, th * 512:th * 512 + 512], op=ALU.mult), [p3], V)
                    p6 = P.emit("vector", lambda e, th=th, k=k: e.tensor_tensor(
                        out=rop2[:, :], in0=psq2[0:32, 0:512],
                        in1=coso[:, th * 512:th * 512 + 512], op=ALU.mult), [p3], V)
                    p7 = P.emit("vector", lambda e: e.tensor_tensor(
                        out=rop3[:, :], in0=rop1[:, :], in1=rop2[:, :],
                        op=ALU.add), [p5, p6], V)
                    pe_ev[th] = P.emit("vector", lambda e, th=th: e.tensor_scalar(
                        out=qpe[32:64, th * 512:th * 512 + 512], in0=rop3[:, :],
                        scalar1=SCALE, scalar2=None, op0=ALU.mult), [p4, p7], V)
                q_done = [qev[0], qev[1], pe_ev[0], pe_ev[1]]

                kv_ev = []
                for which in range(2):
                    for sp2 in range(8):
                        k = sp2 % 2
                        tm = None
                        for g in range(4):
                            w = [wl2] + list(kvlat_done)
                            if which == 0 and sp2 == 0 and g == 0:
                                w += [pe_ev[0], pe_ev[1]]
                            if h > 0 and sp2 == 0 and g == 0:
                                w.append(prev_head[h - 1]["s_done"])
                            if len(kv_ev) >= 2 and g == 0:
                                w.append(kv_ev[-2])
                            tm = P.emit("tensor",
                                        lambda e, g=g, sp2=sp2, which=which, k=k,
                                        hb=hb: e.matmul(
                                            psq[k][:, :],
                                            wkbr[hb][:, g * 256 + which * 128:
                                                     g * 256 + which * 128 + 128],
                                            kvlat[:, g * TB + sp2 * 512:
                                                  g * TB + sp2 * 512 + 512],
                                            start=(g == 0), stop=(g == 3)), w,
                                        T if g == 3 else None)
                        dst = kn if which == 0 else vfm
                        kv_ev.append(P.emit(
                            "vector", lambda e, sp2=sp2, k=k, dst=dst: e.tensor_copy(
                                out=dst[:, sp2 * 512:sp2 * 512 + 512],
                                in_=psq[k][:, :]), [tm], V))
                kv_done_h = kv_ev[-1]
                def wrV(c0, nb, pbank, tokT):
                    def f(e, c0=c0, nb=nb):
                        return e.tensor_copy(
                            out=vtm[:, c0 * 128:c0 * 128 + nb * 128],
                            in_=pbank[:, 0:nb * 128])
                    return P.emit("vector", f, [tokT], V)
                vt_done = transposes(vfm, 32, wrV, kv_done_h, pstr, c_idbf,
                                     extra_src_toks=([prev_head[h - 1]["vt"]]
                                                     if h > 0 else []))

                prev_nr1 = prev_head[h - 1]["nr1"] if h > 0 else None
                q2a = P.emit("vector", lambda e: e.tensor_tensor(
                    out=q2s[:, :], in0=qn[:, :], in1=qn[:, :], op=ALU.mult),
                    q_done + ([prev_nr1] if prev_nr1 else []), V)
                qpe2 = P.emit("vector", lambda e: e.tensor_tensor(
                    out=k2s[0:64, 0:TOWN], in0=qpe[0:64, :], in1=qpe[0:64, :],
                    op=ALU.mult), q_done, V)
                q2r = None
                for th in range(2):
                    tmq = P.emit("tensor", lambda e, th=th: e.matmul(
                        psr[0:1, 0:512], ones_c[:, :],
                        q2s[:, th * 512:th * 512 + 512],
                        start=True, stop=False), [q2a, m2, q2r], None)
                    tmq = P.emit("tensor", lambda e, th=th: e.matmul(
                        psr[0:1, 0:512], ones_c[0:64, :],
                        k2s[0:64, th * 512:th * 512 + 512],
                        start=False, stop=True), [qpe2], T)
                    q2r = P.emit("vector", lambda e, th=th: e.tensor_copy(
                        out=rowtmp[0:1, th * 512:th * 512 + 512], in_=psr[0:1, 0:512]),
                        [tmq] + ([prev_nr1] if prev_nr1 else []), V)
                k2a = P.emit("vector", lambda e: e.tensor_tensor(
                    out=k2s[:, :], in0=kn[:, :], in1=kn[:, :], op=ALU.mult),
                    [kv_ev[7], q2r, qpe2], V)
                kp2 = None
                for sp2 in range(8):
                    kp2 = P.emit("vector", lambda e, sp2=sp2: e.tensor_tensor(
                        out=kpe2[:, sp2 * 512:(sp2 + 1) * 512],
                        in0=kpe[0:64, sp2 * 512:(sp2 + 1) * 512],
                        in1=kpe[0:64, sp2 * 512:(sp2 + 1) * 512], op=ALU.mult),
                        [kp2] if kp2 else list(kvlat_done), V)
                tmk = None
                for sp2 in range(8):
                    tmk = P.emit("tensor", lambda e, sp2=sp2: e.matmul(
                        psr[0:1, 0:512], ones_c[:, :],
                        k2s[:, sp2 * 512:sp2 * 512 + 512],
                        start=True, stop=False), [k2a, q2r, tmk], None)
                    tmk = P.emit("tensor", lambda e, sp2=sp2: e.matmul(
                        psr[0:1, 0:512], ones_c[0:64, :],
                        kpe2[:, sp2 * 512:sp2 * 512 + 512],
                        start=False, stop=True), [kp2], T)
                    tmk = P.emit("vector", lambda e, sp2=sp2: e.reduce_max(
                        rowtmp[0:1, 2048 + sp2:2048 + sp2 + 1], psr[0:1, 0:512],
                        axis=AX), [tmk], V)
                mk1 = P.emit("vector", lambda e: e.reduce_max(
                    rowtmp[0:1, 2056:2057], rowtmp[0:1, 2048:2056], axis=AX),
                    [tmk], V)
                mk2 = P.emit("scalar", lambda e: e.activation(
                    rowtmp[0:1, 2057:2058], rowtmp[0:1, 2056:2057], AF.Ln,
                    bias=cst[0:1, 2:3], scale=1.0), [mk1], SC)
                mk3 = P.emit("scalar", lambda e: e.activation(
                    rowtmp[0:1, 2057:2058], rowtmp[0:1, 2057:2058], AF.Exp,
                    bias=0.0, scale=0.5), [mk2], SC)
                mk4 = P.emit("vector", lambda e: e.tensor_scalar(
                    out=rowtmp[0:1, 2058:2059], in0=rowtmp[0:1, 2057:2058],
                    scalar1=-1.02, scalar2=None, op0=ALU.mult), [mk3], V)
                nr1 = P.emit("scalar", lambda e: e.activation(
                    rowtmp[0:1, 1024:2048], rowtmp[0:1, 0:1024], AF.Ln,
                    bias=cst[0:1, 2:3], scale=1.0), [q2r, mk4], SC)
                nr2 = P.emit("scalar", lambda e: e.activation(
                    rowtmp[0:1, 1024:2048], rowtmp[0:1, 1024:2048], AF.Exp,
                    bias=0.0, scale=0.5), [nr1], SC)
                nr3 = P.emit("vector", lambda e: e.tensor_scalar(
                    out=negm[0:1, :], in0=rowtmp[0:1, 1024:2048],
                    scalar1=rowtmp[0:1, 2058:2059], scalar2=None, op0=ALU.mult),
                    [nr2], V, fwaits=[mk4])
                nb1 = P.dma("sync", rowbc[7:8, 0:1024].bitcast(BF16)[0:1, 0:1024],
                            negm[0:1, :], [nr3], SY[4])
                nb2 = P.dma("sync", qpe[64:65, :],
                            rowbc[7:8, 0:1024].bitcast(BF16)[0:1, 0:1024],
                            [nb1] + [pe_ev[0], pe_ev[1]], SY[4])
                if h == 0:
                    tap("negm_h0", negm[0:1, :], [nr3])

                s_done = None
                avev = {}
                lrow_ev = {}
                prev_out = prev_head[h - 1]["out"] if h > 0 else None
                prev_dl1 = prev_head[h - 1]["dl1"] if h > 0 else None
                for g5 in range(2):
                    KC = 16 * (g5 + 1)
                    av_tok = None
                    l_tok = None
                    exp_done = {}
                    ex_tok = {}
                    for step in range(KC + 2):
                        kc = step
                        if kc < KC:
                            kb = kc % 2
                            w = q_done + [kv_ev[kc // 4], nb2, kv_done_h]
                            if kc >= 2:
                                w.append(ex_tok[kc - 2])
                            P.emit("tensor", lambda e, kc=kc, g5=g5, kb=kb: e.matmul(
                                pss[kb][:, :], kn[:, kc * 128:(kc + 1) * 128],
                                qn[:, g5 * 512:g5 * 512 + 512],
                                start=True, stop=False), w, None)
                            tm = P.emit("tensor", lambda e, kc=kc, g5=g5, kb=kb: e.matmul(
                                pss[kb][:, :], kpe[:, kc * 128:(kc + 1) * 128],
                                qpe[:, g5 * 512:g5 * 512 + 512],
                                start=False, stop=True), [mko], T)
                            ex = P.emit("scalar", lambda e, kc=kc, kb=kb: e.activation(
                                out=ptb[:, kc * 512:(kc + 1) * 512], in_=pss[kb][:, :],
                                func=AF.Exp, bias=0.0, scale=1.0), [tm], SC)
                            ex_tok[kc] = ex
                            mtoks = []
                            for dq in range(4):
                                i_t = 4 * g5 + dq
                                if 4 * i_t <= kc < 4 * i_t + 4:
                                    O = 128 * kc - 512 * i_t
                                    mtoks.append(P.emit(
                                        "vector",
                                        lambda e, kc=kc, dq=dq, O=O: e.scalar_tensor_tensor(
                                            out=ptb[:, kc * 512 + dq * 128:
                                                    kc * 512 + dq * 128 + 128],
                                            in0=ptb[:, kc * 512 + dq * 128:
                                                    kc * 512 + dq * 128 + 128],
                                            scalar=1.0,
                                            in1=c_strip[:, 384 - O:384 - O + 128],
                                            op0=ALU.bypass, op1=ALU.mult), [ex], V))
                                elif kc >= 4 * i_t + 4:
                                    mtoks.append(P.emit(
                                        "vector", lambda e, kc=kc, dq=dq: e.memset(
                                            ptb[:, kc * 512 + dq * 128:
                                                kc * 512 + dq * 128 + 128], 0.0),
                                        [ex], V))
                            exp_done[kc] = mtoks[-1] if mtoks else ex
                        lc = step - 2
                        if lc < 0:
                            continue
                        lw = [exp_done[lc], m2]
                        if lc == 0:
                            if g5 > 0:
                                lw.append(lrow_ev[g5 - 1])
                            elif prev_dl1 is not None:
                                lw.append(prev_dl1)
                        l_tok = P.emit("tensor", lambda e, lc=lc, KC=KC: e.matmul(
                            psr[0:1, 0:512], ones_c[:, :],
                            ptb[:, lc * 512:(lc + 1) * 512],
                            start=(lc == 0), stop=(lc == KC - 1)), lw,
                            T if lc == KC - 1 else None)
                        aw = [exp_done[lc], vt_done]
                        if lc == 0:
                            if g5 > 0:
                                aw.append(avev[g5 - 1])
                            elif prev_out is not None:
                                aw.append(prev_out)
                        av_tok = P.emit("tensor", lambda e, lc=lc, KC=KC: e.matmul(
                            psav[:, :], vtm[:, lc * 128:(lc + 1) * 128],
                            ptb[:, lc * 512:(lc + 1) * 512],
                            start=(lc == 0), stop=(lc == KC - 1)), aw,
                            T if lc == KC - 1 else None)
                    avw = [av_tok]
                    if g5 == 0 and prev_out is not None:
                        avw.append(prev_out)
                    avev[g5] = P.emit("vector", lambda e, g5=g5: e.tensor_copy(
                        out=attnh[:, g5 * 512:(g5 + 1) * 512], in_=psav[:, :]),
                        avw, V)
                    lvw = [l_tok]
                    if g5 == 0 and prev_dl1 is not None:
                        lvw.append(prev_dl1)
                    lrow_ev[g5] = P.emit("vector", lambda e, g5=g5: e.tensor_copy(
                        out=rowtmp[0:1, 3072 + g5 * 512:3072 + (g5 + 1) * 512],
                        in_=psr[0:1, 0:512]), lvw, V)
                    s_done = avev[g5]
                rc = P.emit("vector", lambda e: e.reciprocal(
                    rowtmp[0:1, 3072:3072 + 1024],
                    rowtmp[0:1, 3072:3072 + 1024]),
                    [lrow_ev[g] for g in range(2)], V)
                dl1 = P.dma("sync", rowbc[6:7, 0:1024], rowtmp[0:1, 3072:4096],
                            [rc], SY[4])
                dl2 = P.dma("gpsimd", linvB[:, :],
                            rowbc[6:7, 0:1024].partition_broadcast(128), [dl1], gq)
                nm = P.emit("vector", lambda e: e.tensor_tensor(
                    out=attnh[:, :], in0=attnh[:, :], in1=linvB[:, :],
                    op=ALU.mult), [dl2] + [avev[g] for g in range(2)], V)
                da = P.dma("sync", attn_scr[h * 128:(h + 1) * 128, :], attnh[:, :],
                           [nm], SY[4])
                if h == 0:
                    tap("qn_h0", qn[:, :], q_done)
                    tap("qpe_h0", qpe[0:64, :], q_done)
                    tap("kn_h0", kn[:, :], [kv_ev[7]])
                    tap("vtm_h0", vtm[:, :], [vt_done])
                    tap("pt_h0", ptb[:, 0:32 * 256], [s_done])
                    tap("lrow_h0", rowtmp[0:1, 3072:4096],
                        [lrow_ev[g] for g in range(2)])
                    tap("attnh_h0", attnh[:, :], [nm])
                prev_head[h] = {"qdone": qev[1], "kvdone": kv_done_h,
                                "s_done": s_done, "out": da, "dl1": dl1,
                                "nr1": nr1, "vt": vt_done}
            attn_done = [prev_head[hh]["out"] for hh in range(NH)]
            tap("attn_all", attn_scr, attn_done)

        P.cur_phase = 6
        pZ = ExitStack()
        with pZ:
            attn_all = sb("attn_all", [128, 32 * 1024], BF16, pZ)
            wor = [sb("wor%d" % i, [128, 512], F32, pZ) for i in range(3)]
            wob = [sb("wob%d" % i, [128, 512], BF16, pZ) for i in range(2)]
            xsl = [sb("xsl%d" % i, [128, 512], F32, pZ) for i in range(2)]
            fsl = [sb("fsl%d" % i, [128, 512], F32, pZ) for i in range(2)]
            oub = [sb("oub%d" % i, [128, 512], F32, pZ) for i in range(2)]
            psO = [ps("psO%d" % i, [128, 512], F32, pZ) for i in range(8)]
            jw = join()
            srcv = attn_scr.rearrange("(k p) t -> p k t", p=128)
            dstv = attn_all[:, :].rearrange("p (k t) -> p k t", k=32)
            la_toks = []
            for kq in range(4):
                la_toks.append(P.dma("sync", dstv[:, kq * 8:(kq + 1) * 8, :],
                                     srcv[:, kq * 8:(kq + 1) * 8, :],
                                     attn_done + jw, SY[0]))
            cast_prev = None
            mm_done2 = {}
            ev_done = {}
            out_toks = []
            for osp in range(8):
                tmz = {}
                for K in range(32):
                    idx = osp * 32 + K
                    rb = idx % 3
                    qb = idx % 2
                    w = jw if idx == 0 else []
                    if idx >= 3:
                        w = [mm_done2[idx - 3]]
                    tl = P.dma("sync", wor[rb][:, :], wo_t[osp, K, :, :], w, SY[0])
                    cw = [tl]
                    if idx >= 2:
                        cw.append(mm_done2[idx - 2])
                    cast_prev = P.emit("scalar", lambda e, rb=rb, qb=qb: e.activation(
                        out=wob[qb][:, :], in_=wor[rb][:, :], func=AF.Copy),
                        cw, SC)
                    tm = None
                    for tt in range(8):
                        w2 = [cast_prev] + la_toks
                        if K == 0 and ev_done.get((osp - 1, tt)) is not None:
                            w2.append(ev_done[(osp - 1, tt)])
                        tm = P.emit("tensor", lambda e, K=K, tt=tt, qb=qb: e.matmul(
                            psO[tt][:, :],
                            attn_all[:, K * 1024 + tt * 128:K * 1024 + tt * 128 + 128],
                            wob[qb][:, :], start=(K == 0), stop=(K == 31)),
                            w2, T if (K == 31 or tt == 7) else None)
                        if K == 31:
                            tmz[tt] = tm
                    mm_done2[idx] = tm
                for tt in range(8):
                    b = tt % 2
                    xw = [out_toks[-2]] if len(out_toks) >= 2 else []
                    tx = P.dma("sync", xsl[b][:, :],
                               x_own[tt * 128:(tt + 1) * 128,
                                     osp * 512:(osp + 1) * 512], xw, SY[0])
                    e1 = P.emit("vector", lambda e, tt=tt, b=b: e.scalar_tensor_tensor(
                        out=oub[b][:, :], in0=psO[tt][:, :], scalar=1.0,
                        in1=xsl[b][:, :], op0=ALU.bypass, op1=ALU.add),
                        [tmz[tt], tx], V)
                    ev_done[(osp, tt)] = e1
                    tf = P.dma("sync",
                               fsl[b][:, :].rearrange("p (c t) -> p c t", c=4),
                               ffn_scr[:, :].rearrange("p (c t) -> p c t", c=32)
                               [:, 4 * osp:4 * osp + 4, tt * 128:(tt + 1) * 128],
                               ffn_done + xw, SY[0])
                    tm2 = None
                    for cj in range(4):
                        tm2 = P.emit("tensor", lambda e, cj=cj, tt=tt, b=b: e.transpose(
                            psO[tt][:, cj * 128:(cj + 1) * 128],
                            fsl[b][:, cj * 128:(cj + 1) * 128], c_idf[:, :]),
                            [tf, e1], T if cj == 3 else None)
                    e2 = P.emit("vector", lambda e, tt=tt, b=b: e.scalar_tensor_tensor(
                        out=oub[b][:, :], in0=oub[b][:, :], scalar=1.0,
                        in1=psO[tt][:, :], op0=ALU.bypass, op1=ALU.add),
                        [tm2], V)
                    ot = P.dma("sync", out[tt * 128:(tt + 1) * 128,
                                          osp * 512:(osp + 1) * 512],
                               oub[b][:, :], [e2], SY[0])
                    out_toks.append(ot)

        pM2.close()
        pM1.close()
        final_waits = [(c, c.n) for c in ALL]
        with nc.Block() as block:
            P.finalize(block, final_waits)
    return nc


def _prep_inputs(inputs):
    x = np.asarray(inputs["L0_attn_x"])
    wq_a = np.asarray(inputs["L0_attn_wq_a"])
    q_norm_w = np.asarray(inputs["L0_attn_q_norm_w"])
    wq_b = np.asarray(inputs["L0_attn_wq_b"])
    wkv_a = np.asarray(inputs["L0_attn_wkv_a"])
    kv_norm_w = np.asarray(inputs["L0_attn_kv_norm_w"])
    wkv_b = np.asarray(inputs["L0_attn_wkv_b"])
    wo = np.asarray(inputs["L0_attn_wo"])
    gamma = np.asarray(inputs["L0_ffn_gamma"])
    beta = np.asarray(inputs["L0_ffn_beta"])
    wf = np.asarray(inputs["L0_ffn_weight"])

    pe = np.concatenate([np.arange(0, DPE, 2), np.arange(1, DPE, 2)])

    wqaT = np.ascontiguousarray(wq_a.T)
    wqa_t = np.empty((3, 128, 32 * 512), np.float32)
    for MB in range(3):
        blk = wqaT[:, MB * 512:(MB + 1) * 512].reshape(32, 128, 512)
        wqa_t[MB] = blk.transpose(1, 0, 2).reshape(128, 32 * 512)

    wqbT = wq_b.T.reshape(QL, NH, DN + DPE)
    perm = np.concatenate([np.arange(DN), DN + pe])
    wqbT = wqbT[:, :, perm]
    wqb_t = np.empty((NH, 128, 12 * 192), np.float32)
    for h in range(NH):
        blk = wqbT[:, h, :].reshape(12, 128, 192)
        wqb_t[h] = blk.transpose(1, 0, 2).reshape(128, 12 * 192)

    wkvaT = wkv_a.T
    perm2 = np.concatenate([np.arange(KVL), KVL + pe])
    wkvaT = wkvaT[:, perm2]
    wkva_t = np.ascontiguousarray(
        wkvaT.reshape(32, 128, KVA).transpose(1, 0, 2).reshape(128, 32 * KVA))

    wkvbT = wkv_b.T.reshape(KVL, NH, DN + DV)
    wkvb_t = np.empty((NH, 128, 4 * 256), np.float32)
    for h in range(NH):
        blk = wkvbT[:, h, :].reshape(4, 128, 256)
        wkvb_t[h] = blk.transpose(1, 0, 2).reshape(128, 4 * 256)

    woT = np.ascontiguousarray(wo.T)
    wo_t = np.ascontiguousarray(
        woT.reshape(32, 128, 8, 512).transpose(2, 0, 1, 3))

    Pv = np.arange(128)
    pi = np.zeros(4096, np.int64)
    for g in range(32):
        pi[g * 128:(g + 1) * 128] = (Pv % 32) * 128 + 4 * g + Pv // 32
    wfT = np.ascontiguousarray(wf.T[pi, :])
    wf_t = np.ascontiguousarray(
        wfT.reshape(32, 128, 8, 512).transpose(2, 0, 1, 3))

    qnw_t = np.ascontiguousarray(q_norm_w.reshape(12, 128).T)
    kvnw_t = np.ascontiguousarray(kv_norm_w.reshape(4, 128).T)

    h128c = _hadamard(128).astype(BF)
    h32 = _hadamard(32).astype(BF)
    h32s = np.ascontiguousarray(np.concatenate([h32] * 4, axis=0))
    idbf = np.eye(128, dtype=np.float32).astype(BF)
    idf = np.eye(128, dtype=np.float32)

    freqs = 1.0 / 10000.0 ** (np.arange(0, DPE, 2, dtype=np.float32) / DPE)
    ang = np.outer(freqs, np.arange(TB, dtype=np.float32))
    cosc = np.ascontiguousarray(np.cos(ang).astype(np.float32))
    sinc = np.ascontiguousarray(np.sin(ang).astype(np.float32))

    in_maps = []
    own_rows_all = []
    for core in range(NCORE):
        b, a = divmod(core, 4)
        own = (128 * (4 * np.arange(8)[:, None] + a) +
               np.arange(128)[None, :]).reshape(-1)
        own_rows_all.append((b, own))
        u = np.arange(512)[None, :]
        p = np.arange(128)[:, None]
        stripc = (u >= p - 128 * a + 384).astype(np.float32).astype(BF)
        m = {
            "x_own": np.ascontiguousarray(x[b][own, :]),
            "x_cov": np.ascontiguousarray(x[b]),
            "wqa_t": wqa_t, "wqb_t": wqb_t, "wkva_t": wkva_t,
            "wkvb_t": wkvb_t, "wo_t": wo_t, "wf_t": wf_t,
            "qnw": qnw_t, "kvnw": kvnw_t,
            "gamma": np.ascontiguousarray(gamma.reshape(1, D)),
            "beta": np.ascontiguousarray(beta.reshape(1, D)),
            "h128": h128c, "h32s": h32s, "ident_bf": idbf, "ident_f32": idf,
            "strip": stripc,
            "cos_own": np.ascontiguousarray(cosc[:, own]),
            "sin_own": np.ascontiguousarray(sinc[:, own]),
            "cos_cov": cosc, "sin_cov": sinc,
        }
        in_maps.append(m)
    return in_maps, own_rows_all


def kernel(**inputs):
    in_maps, own_rows_all = _prep_inputs(inputs)
    nc = build(DEV_TAPS, MAX_PHASE)
    res = run_bass_kernel_spmd(nc, in_maps, core_ids=list(range(NCORE)))
    outp = np.zeros((2, TB, D), np.float32)
    for core in range(NCORE):
        b, own = own_rows_all[core]
        outp[b][own, :] = np.asarray(res.results[core]["out"])
    if DEV_TAPS:
        kernel.last_results = res.results
    return outp


# version 04
# speedup vs baseline: 1.8757x; 1.0838x over previous
import math
import numpy as np
import ml_dtypes
import concourse.bass as bass
import concourse.mybir as mybir
from concourse.bass_utils import run_bass_kernel_spmd

F32 = mybir.dt.float32
BF16 = mybir.dt.bfloat16
I32 = mybir.dt.int32
AF = mybir.ActivationFunctionType
ALU = mybir.AluOpType
AX = mybir.AxisListType.X

D = 4096
NH = 32
DN = 128
DPE = 64
DV = 128
QL = 1536
KVL = 512
KVA = KVL + DPE
TOWN = 1024
TB = 4096
SCALE = 1.0 / math.sqrt(DN + DPE)
EPS_RMS = 1e-6
EPS_LN = 1e-5
NCORE = 8

BF = ml_dtypes.bfloat16

DEV_TAPS = ()
MAX_PHASE = 99
SERIALIZE = False
_TAPS = {
    "xq_fm": ([128, 32 * 1024], BF16),
    "dbg_xn": ([128, 4096], F32),
    "dbg_mv": ([128, 2], F32),
    "dbg_rln": ([128, 1], F32),
    "dbg_gammaB": ([128, 4096], F32),
    "dbg_xt": ([128, 4096], F32),
    "dbg_xln": ([128, 4096], F32),
    "dbg_sf": ([128, 1], F32),
    "dbg_xqb": ([128, 4096], BF16),
    "xh": ([128, 32 * 1024], BF16),
    "outpre": ([128, 8 * 4096], BF16),
    "ffn": ([128, 32 * 1024], F32),
    "xfm": ([128, 32 * 1024], BF16),
    "qlat": ([128, 12 * 1024], BF16),
    "kvlat": ([128, 4 * 4096], BF16),
    "kpe": ([64, 4096], BF16),
    "qn_h0": ([128, 1024], BF16),
    "qpe_h0": ([64, 1024], BF16),
    "kn_h0": ([128, 4096], BF16),
    "vtm_h0": ([128, 4096], BF16),
    "negm_h0": ([1, 1024], BF16),
    "pt_h0": ([128, 32 * 256], BF16),
    "lrow_h0": ([1, 1024], F32),
    "attnh_h0": ([128, 1024], BF16),
    "attn_all": ([4096, 1024], BF16),
}


def _hadamard(n):
    h = np.array([[1.0]], dtype=np.float32)
    while h.shape[0] < n:
        h = np.block([[h, h], [h, -h]])
    return h


class Ctr:
    __slots__ = ("h", "n", "name", "is_lane")

    def __init__(self, h, name):
        self.h = h
        self.n = 0
        self.name = name
        self.is_lane = False


class DmaQ:

    def __init__(self, lanes):
        self.lanes = lanes
        self.idx = 0

    def next(self):
        K = len(self.lanes)
        i = self.idx
        self.idx += 1
        lane = self.lanes[i % K]
        pre = (lane, 16 * (i // K)) if i >= K else None
        lane.n = 16 * (i // K + 1)
        return lane, pre, (lane, lane.n)

    def final_tokens(self):
        return [(ln, ln.n) for ln in self.lanes]


class Prog:
    ENGS = ("sync", "vector", "scalar", "tensor", "gpsimd")

    def __init__(self, nc):
        self.nc = nc
        self.q = {e: [] for e in self.ENGS}
        self.state = {e: {} for e in self.ENGS}
        self.owner = {}
        self.cur_phase = 0
        self.max_phase = 99
        self.serialize = False
        self.all_ctrs = []
        self.tap_gate = []

    def ctr(self, stack, name, owner):
        h = stack.enter_context(self.nc.semaphore(name))
        c = Ctr(h, name)
        self.owner[id(c)] = owner
        return c

    def emit(self, eng, fn, waits=(), inc=None, amt=1, fwaits=()):
        if self.cur_phase > self.max_phase:
            return None
        if self.tap_gate:
            waits = list(waits) + self.tap_gate
        if self.serialize:
            waits = list(waits) + [(c, c.n) for c in self.all_ctrs]
        wl = []
        st = self.state[eng]
        for tok in fwaits:
            if tok is None:
                continue
            c, v = tok
            if v > 0:
                wl.append((c.h, v))
        for tok in waits:
            if tok is None:
                continue
            c, v = tok
            if v <= 0:
                continue
            if (self.owner.get(id(c)) == eng and not c.is_lane
                    and (c.n - v) > 12):
                continue
            if st.get(id(c), 0) >= v:
                continue
            st[id(c)] = v
            wl.append((c.h, v))
        tok = None
        if inc is not None:
            inc.n += amt
            tok = (inc, inc.n)
        self.q[eng].append((wl, fn, ((inc.h if inc is not None else None), amt)))
        return tok

    def dma(self, eng, out, in_, waits=(), inc=None):
        def fn(e, out=out, in_=in_):
            return e.dma_start(out=out, in_=in_)
        if isinstance(inc, DmaQ):
            if self.cur_phase > self.max_phase:
                return None
            lane, pre, tok = inc.next()
            w = list(waits)
            if pre is not None:
                w.append(pre)
            self.emit(eng, fn, w, lane, 16, fwaits=())
            lane.n = tok[1]
            return tok
        return self.emit(eng, fn, waits, inc, 16)

    def finalize(self, block, final_waits):
        def runner(e, name):
            for wl, fn, (semh, amt) in self.q[name]:
                for h, v in wl:
                    e.wait_ge(h, v)
                ins = fn(e)
                if semh is not None and ins is not None:
                    ins.then_inc(semh, amt)
            if name == "sync":
                for c, v in final_waits:
                    e.wait_ge(c.h, v)
        block.sync(lambda e: runner(e, "sync"))
        block.vector(lambda e: runner(e, "vector"))
        block.scalar(lambda e: runner(e, "scalar"))
        block.tensor(lambda e: runner(e, "tensor"))
        block.gpsimd(lambda e: runner(e, "gpsimd"))


def build(taps=(), max_phase=99):
    from contextlib import ExitStack
    nc = bass.Bass("TRN2", target_bir_lowering=False, debug=False)

    def din(name, shape, dt=F32):
        return nc.dram_tensor(name, list(shape), dt, kind="ExternalInput").ap()

    x_own = din("x_own", [TOWN, D])
    wqa_t = din("wqa_t", [3, 128, 32 * 512])
    wqb_t = din("wqb_t", [NH, 128, 12 * 192])
    wkva_t = din("wkva_t", [128, 32 * KVA])
    wkvb_t = din("wkvb_t", [NH, 128, 4 * 256])
    wo_t = din("wo_t", [8, 32, 128, 512])
    wf_t = din("wf_t", [8, 32, 128, 512])
    x_cov = din("x_cov", [TB, D])
    qnw = din("qnw", [128, 12])
    kvnw = din("kvnw", [128, 4])
    gamma = din("gamma", [1, D])
    beta = din("beta", [1, D])
    h128 = din("h128", [128, 128], BF16)
    h32s = din("h32s", [128, 32], BF16)
    ident_bf = din("ident_bf", [128, 128], BF16)
    ident_f32 = din("ident_f32", [128, 128], F32)
    strip = din("strip", [128, 512], BF16)
    cos_own = din("cos_own", [32, TOWN])
    sin_own = din("sin_own", [32, TOWN])
    cos_cov = din("cos_cov", [32, TB])
    sin_cov = din("sin_cov", [32, TB])

    out = nc.dram_tensor("out", [TOWN, D], F32, kind="ExternalOutput").ap()

    bounce = nc.dram_tensor("bounce", [128, 32 * 1024], BF16, kind="Internal").ap()
    ffn_scr = nc.dram_tensor("ffn_scr", [128, 32 * 1024], F32, kind="Internal").ap()
    attn_scr = nc.dram_tensor("attn_scr", [NH * 128, TOWN], BF16, kind="Internal").ap()
    rowbc = nc.dram_tensor("rowbc", [8, 4096], F32, kind="Internal").ap()

    tap_t = {}
    for name in taps:
        shape, dt = _TAPS[name]
        tap_t[name] = nc.dram_tensor("tap_" + name, list(shape), dt,
                                     kind="ExternalOutput").ap()

    st = ExitStack()
    with st:
        P = Prog(nc)
        P.max_phase = max_phase
        import sys as _sys
        P.serialize = bool(globals().get("SERIALIZE")) or SERIALIZE
        T = P.ctr(st, "T", "tensor")
        V = P.ctr(st, "V", "vector")
        SC = P.ctr(st, "SC", "scalar")
        G = P.ctr(st, "G", "gpsimd")
        sy_lanes = [P.ctr(st, "SYL%d" % i, "sync") for i in range(8)]
        gq_lanes = [P.ctr(st, "GQL%d" % i, "gpsimd") for i in range(4)]
        for _ln in sy_lanes + gq_lanes:
            _ln.is_lane = True
        syq = DmaQ(sy_lanes)
        gq = DmaQ(gq_lanes)
        SY = [syq] * 6
        ALL = [T, V, SC, G] + sy_lanes + gq_lanes
        P.all_ctrs = ALL

        def sb(name, shape, dt, stack=None, side=None):
            return (stack or st).enter_context(
                nc.sbuf_tensor(name, list(shape), dt, side=side))

        def ps(name, shape, dt, stack=None):
            return (stack or st).enter_context(nc.psum_tensor(name, list(shape), dt))

        def join():
            return [(c, c.n) for c in ALL]

        def tap(name, src_ap, waits):
            if name in tap_t:
                tok = P.dma("sync", tap_t[name], src_ap, waits, SY[5])
                if tok is not None:
                    P.tap_gate.append(tok)

        c_h128 = sb("c_h128", [128, 128], BF16)
        c_h32s = sb("c_h32s", [128, 32], BF16)
        c_idbf = sb("c_idbf", [128, 128], BF16)
        c_idf = sb("c_idf", [128, 128], F32)
        c_strip = sb("c_strip", [128, 512], BF16)
        c_qnw = sb("c_qnw", [128, 12], F32)
        c_kvnw = sb("c_kvnw", [128, 4], F32)
        ones_r = sb("ones_r", [1, 128], BF16)
        ones_c = sb("ones_c", [128, 1], BF16)
        onesf = sb("onesf", [128, 1], F32)
        one1 = sb("one1", [1, 1], F32)
        wsq_col = sb("wsq_col", [128, 12], F32)
        wabs = sb("wabs", [128, 258], F32)
        thr_c = sb("thr_c", [128, 1], F32)
        nthr_c = sb("nthr_c", [128, 1], F32)
        wsc_c = sb("wsc_c", [128, 1], F32)
        i2t_c = sb("i2t_c", [128, 1], F32)
        rowtmp = sb("rowtmp", [1, 4096], F32)
        coltmp = sb("coltmp", [128, 2], F32)
        cst = sb("cst", [128, 3], F32)

        ld0 = []
        for dst, src in ((c_h128, h128), (c_h32s, h32s), (c_idbf, ident_bf),
                         (c_idf, ident_f32), (c_strip, strip), (c_qnw, qnw),
                         (c_kvnw, kvnw)):
            ld0.append(P.dma("sync", dst[:, :], src, (), SY[0]))
        m1 = P.emit("vector", lambda e: e.memset(ones_r[:, :], 1.0), (), V)
        m2 = P.emit("vector", lambda e: e.memset(ones_c[:, :], 1.0), (), V)
        m3 = P.emit("vector", lambda e: e.memset(onesf[:, :], 1.0), (), V)
        m4 = P.emit("vector", lambda e: e.memset(one1[:, :], 1.0), (), V)
        m5a = P.emit("vector", lambda e: e.memset(cst[:, 0:1], EPS_LN), (), V)
        m5b = P.emit("vector", lambda e: e.memset(cst[:, 1:2], EPS_RMS), (), V)
        m5c = P.emit("vector", lambda e: e.memset(cst[:, 2:3], 1e-30), (), V)
        consts_tok0 = [m5a, m5b, m5c]
        consts_tok = ld0 + [m1, m2, m3, m4]

        def transposes(src_tile, n_c, dst_write, tok_src, pbank, ident,
                       extra_src_toks=()):
            last = None
            for c0 in range(0, n_c, 4):
                nb = min(4, n_c - c0)
                tm = None
                for k in range(nb):
                    c = c0 + k
                    w = [tok_src] + list(extra_src_toks)
                    if last is not None:
                        w.append(last)
                    tm = P.emit("tensor", lambda e, c=c, k=k: e.transpose(
                        pbank[:, k * 128:(k + 1) * 128],
                        src_tile[:, c * 128:(c + 1) * 128], ident[:, :]),
                        w, T if k == nb - 1 else None)
                last = dst_write(c0, nb, pbank, tm)
            return last

        def row_to_col(row_ap, psc_out, dst_col, waits):
            tm = P.emit("tensor", lambda e: e.matmul(
                psc_out, row_ap, one1[:, :], start=True, stop=True),
                list(waits), T)
            return P.emit("vector", lambda e: e.tensor_copy(out=dst_col, in_=psc_out),
                          [tm], V)

        P.cur_phase = 1
        sXQ = ExitStack()
        xq_fm = sb("xq_fm", [128, 32 * 1024], BF16, sXQ, side="right")
        pA = ExitStack()
        with pA:
            xt = [sb("xtA%d" % i, [128, D], F32, pA) for i in range(2)]
            gammaB = sb("gammaB", [128, D], F32, pA)
            betaB = sb("betaB", [128, D], F32, pA)
            bn6 = sb("bn6", [128, 48], F32, pA)
            mv = sb("mv", [128, 2], F32, pA)
            rln_c = sb("rln_c", [128, 1], F32, pA)
            nmu_c = sb("nmu_c", [128, 1], F32, pA)
            sf_c = sb("sf_c", [128, 1], F32, pA)
            s1 = sb("s1A", [128, D], F32, pA)
            xqi = sb("xqi", [128, D // 2], I32, pA)
            xqb = sb("xqb", [128, D], BF16, pA)
            ptrA = ps("ptrA", [128, 512], BF16, pA)

            gb = P.dma("sync", gammaB[:, :], gamma.partition_broadcast(128), (), SY[0])
            bb = P.dma("sync", betaB[:, :], beta.partition_broadcast(128), (), SY[0])

            ev = None
            tg_prev = None
            for i in range(8):
                b = i % 2
                tl = P.dma("sync", xt[b][:, :], x_own[i * 128:(i + 1) * 128, :],
                           [ev] if i >= 2 else [], SY[0])
                t1 = None
                for kk in range(8):
                    t1 = P.emit("vector", lambda e, b=b, kk=kk: e.bn_stats(
                        bn6[:, kk * 6:(kk + 1) * 6],
                        xt[b][:, kk * 512:(kk + 1) * 512]), [tl, tg_prev], V)
                t2 = P.emit("vector", lambda e: e.bn_aggr(mv[:, :], bn6[:, :]), [t1], V)
                t3 = P.emit("scalar", lambda e: e.activation(
                    rln_c[:, :], mv[:, 1:2], AF.Ln, bias=cst[:, 0:1], scale=1.0), [t2], SC)
                t4 = P.emit("scalar", lambda e: e.activation(
                    rln_c[:, :], rln_c[:, :], AF.Exp, bias=0.0, scale=-0.5), [t3], SC)
                t5 = P.emit("vector", lambda e: e.tensor_scalar(
                    out=nmu_c[:, :], in0=mv[:, 0:1], scalar1=-1.0, scalar2=None,
                    op0=ALU.mult), [t2], V)
                t7 = P.emit("vector", lambda e, b=b: e.tensor_scalar(
                    out=s1[:, :], in0=xt[b][:, :], scalar1=nmu_c[:, :],
                    scalar2=rln_c[:, :], op0=ALU.add, op1=ALU.mult), [t4], V,
                    fwaits=[t5])
                if i == 0:
                    tap("dbg_xn", s1[:, :], [t7])
                    tap("dbg_mv", mv[:, :], [t2])
                    tap("dbg_rln", rln_c[:, :], [t4])
                    tap("dbg_gammaB", gammaB[:, :], [gb])
                    tap("dbg_xt", xt[0][:, :], [tl])
                t8 = P.emit("gpsimd", lambda e: e.tensor_tensor(
                    out=s1[:, :], in0=s1[:, :], in1=gammaB[:, :], op=ALU.mult),
                    [t7, gb], G)
                t9 = P.emit("gpsimd", lambda e: e.tensor_tensor(
                    out=s1[:, :], in0=s1[:, :], in1=betaB[:, :], op=ALU.add),
                    [t8, bb], G)
                ta = P.emit("vector", lambda e: e.reduce_max(
                    sf_c[:, :], s1[:, :], axis=AX, apply_absolute_value=True), [t9], V)
                tb = P.emit("vector", lambda e: e.tensor_scalar(
                    out=coltmp[:, 0:1], in0=sf_c[:, :], scalar1=1e-6, scalar2=1e6,
                    op0=ALU.max, op1=ALU.min), [ta], V)
                tc = P.emit("vector", lambda e: e.reciprocal(
                    coltmp[:, 1:2], coltmp[:, 0:1]), [tb], V)
                td = P.emit("vector", lambda e: e.tensor_scalar(
                    out=coltmp[:, 0:1], in0=coltmp[:, 1:2], scalar1=7.0, scalar2=1e6,
                    op0=ALU.mult, op1=ALU.min), [tc], V)
                te = P.emit("vector", lambda e: e.tensor_scalar(
                    out=sf_c[:, :], in0=coltmp[:, 0:1], scalar1=1e-6, scalar2=None,
                    op0=ALU.max), [td], V)
                tg = None
                for hh in range(2):
                    tf = P.emit("vector", lambda e, hh=hh: e.tensor_scalar(
                        out=xqi[:, :], in0=s1[:, hh * 2048:(hh + 1) * 2048],
                        scalar1=sf_c[:, :], scalar2=None,
                        op0=ALU.mult), [tg], V, fwaits=[te])
                    tg = P.emit("vector", lambda e, hh=hh: e.tensor_copy(
                        out=xqb[:, hh * 2048:(hh + 1) * 2048], in_=xqi[:, :]),
                        [tf], V)
                tg_prev = tg
                if i == 0:
                    tap("dbg_xln", s1[:, :], [t9])
                    tap("dbg_sf", sf_c[:, :], [te])
                    tap("dbg_xqb", xqb[:, :], [tg])

                def wrA(c0, nb, pbank, tokT, i=i):
                    def f(e, c0=c0, nb=nb, i=i):
                        dst = xq_fm[:, :].rearrange("p (c t) -> p c t", c=32)
                        return e.tensor_copy(
                            out=dst[:, c0:c0 + nb, i * 128:(i + 1) * 128],
                            in_=pbank[:, 0:nb * 128])
                    return P.emit("vector", f, [tokT], V)
                ev = transposes(xqb, 32, wrA, tg, ptrA, c_idbf,
                                extra_src_toks=consts_tok)
            xqfm_done = ev
            tap("xq_fm", xq_fm[:, :], [xqfm_done])

        P.cur_phase = 2
        pW = ExitStack()
        with pW:
            wt = [sb("wp1_%d" % i, [128, 2048], F32, pW) for i in range(2)]
            wp1dump = sb("wp1dump", [128, 512], F32, pW)
            vts = {}
            jwW = join()
            for idx in range(64):
                b = idx % 2
                osp, g4 = divmod(idx, 8)
                srcv = wf_t[osp, g4 * 4:(g4 + 1) * 4, :, :].rearrange("g p o -> p g o")
                w = [vts[idx - 2]] if idx >= 2 else list(jwW)
                tl = P.dma("sync", wt[b][:, :].rearrange("p (g o) -> p g o", g=4),
                           srcv, w, SY[1])
                vt2 = vts.get(idx - 1)
                for kq in range(4):
                    vt2 = P.emit("scalar", lambda e, b=b, idx=idx, kq=kq: e.activation(
                        out=wp1dump[:, :], in_=wt[b][:, kq * 512:(kq + 1) * 512],
                        func=AF.Abs,
                        accum_out=wabs[:, idx * 4 + kq:idx * 4 + kq + 1]),
                        [tl, vt2], SC)
                vts[idx] = vt2
            tr = P.emit("vector", lambda e: e.reduce_sum(
                wabs[:, 256:257], wabs[:, 0:256], axis=AX),
                [vts[i] for i in range(64)], V)
            psW = ps("psW", [1, 8], F32, pW)
            tm = P.emit("tensor", lambda e: e.matmul(
                psW[0:1, 0:1], wabs[:, 256:257], onesf[:, :], start=True, stop=True),
                [tr, m3], T)
            tv = P.emit("vector", lambda e: e.tensor_scalar(
                out=rowtmp[0:1, 0:1], in0=psW[0:1, 0:1],
                scalar1=0.5 / float(D * D), scalar2=None, op0=ALU.mult), [tm], V)
            tv2 = P.emit("vector", lambda e: e.tensor_scalar(
                out=rowtmp[0:1, 1:2], in0=rowtmp[0:1, 0:1], scalar1=-1.0,
                scalar2=None, op0=ALU.mult), [tv], V)
            tv3 = P.emit("vector", lambda e: e.tensor_scalar(
                out=rowtmp[0:1, 2:3], in0=rowtmp[0:1, 0:1], scalar1=2.0,
                scalar2=None, op0=ALU.mult), [tv], V)
            tv4 = P.emit("vector", lambda e: e.reciprocal(
                rowtmp[0:1, 3:4], rowtmp[0:1, 2:3]), [tv3], V)
            td1 = P.dma("sync", rowbc[0:1, 0:4], rowtmp[0:1, 0:4], [tv2, tv3, tv4],
                        SY[1])
            td2 = P.dma("sync", thr_c[:, :], rowbc[0:1, 0:1].partition_broadcast(128),
                        [td1], SY[1])
            td3 = P.dma("sync", nthr_c[:, :], rowbc[0:1, 1:2].partition_broadcast(128),
                        [td1], SY[1])
            td4 = P.dma("sync", wsc_c[:, :], rowbc[0:1, 2:3].partition_broadcast(128),
                        [td1], SY[1])
            td5 = P.dma("sync", i2t_c[:, :], rowbc[0:1, 3:4].partition_broadcast(128),
                        [td1], SY[1])
            wp1_done = [td2, td3, td4, td5]

        P.cur_phase = 2.2
        pF = ExitStack()
        with pF:
            y1 = sb("y1", [128, 32 * 1024], BF16, pF)
            psA = [ps("psFA%d" % i, [128, 512], F32, pF) for i in range(2)]
            evh = {}
            jwF = join()
            import os as _os
            _ffa = int(_os.environ.get("FFA_C", "32"))
            for c in range(_ffa):
                for th in range(2):
                    k = (c * 2 + th) % 2
                    w = [xqfm_done] + (jwF if (c == 0 and th == 0) else [])
                    if (c * 2 + th) >= 2:
                        w.append(evh[c * 2 + th - 2])
                    tm = P.emit("tensor", lambda e, c=c, th=th, k=k: e.matmul(
                        psA[k][:, :], c_h128[:, :],
                        xq_fm[:, c * 1024 + th * 512:c * 1024 + th * 512 + 512],
                        start=True, stop=True), w, T)
                    evh[c * 2 + th] = P.emit(
                        "vector", lambda e, c=c, th=th, k=k: e.tensor_copy(
                            out=y1[:, c * 1024 + th * 512:c * 1024 + th * 512 + 512],
                            in_=psA[k][:, :]), [tm], V)
            y1_done = evh[_ffa * 2 - 1]
            P.cur_phase = 2.22
            tb1 = P.dma("sync", bounce[:, :], y1[:, :], [y1_done], SY[2])
        sXQ.close()

        P.cur_phase = 2.3
        pF2 = ExitStack()
        with pF2:
            xh = sb("xh", [128, 32 * 1024], BF16, pF2)
            pY1T = ExitStack()
            y1t = sb("y1t", [128, 32 * 1024], BF16, pY1T)
            tb2 = P.dma("sync", y1t[:, :].rearrange("p (g t) -> p g t", g=32),
                        bounce.rearrange("(g j) (c t) -> (j c) g t", j=4, c=32),
                        [tb1], SY[2])
            psB = [ps("psFB%d" % i, [128, 512], F32, pY1T) for i in range(2)]
            evh = {}
            for g in range(32):
                for tsp in range(2):
                    gi = g * 2 + tsp
                    k = gi % 2
                    tm = None
                    for j in range(4):
                        w = [tb2]
                        if gi >= 2:
                            w.append(evh[gi - 2])
                        tm = P.emit("tensor", lambda e, g=g, tsp=tsp, j=j, k=k: e.matmul(
                            psB[k][32 * j:32 * j + 32, 0:512],
                            c_h32s[32 * j:32 * j + 32, :],
                            y1t[32 * j:32 * j + 32,
                                g * 1024 + tsp * 512:g * 1024 + tsp * 512 + 512],
                            start=True, stop=True, tile_position=(32 * j, 32 * j)),
                            w, T if j == 3 else None)
                    evh[gi] = P.emit("vector", lambda e, g=g, tsp=tsp, k=k: e.tensor_copy(
                        out=xh[:, g * 1024 + tsp * 512:g * 1024 + tsp * 512 + 512],
                        in_=psB[k][:, :]), [tm], V)
            xh_done = evh[63]
            tap("xh", xh[:, :], [xh_done])
            pY1T.close()

            P.cur_phase = 2.4
            pOPF = ExitStack()
            opfm = sb("opfm", [128, 32 * 1024], BF16, pOPF, side="right")
            pMAIN = ExitStack()
            outsp = sb("outsp", [128, 8 * 512], BF16, pMAIN)
            wfr = [sb("wfr%d" % i, [128, 512], F32, pMAIN) for i in range(3)]
            wqg = [sb("wqg%d" % i, [128, 512], BF16, pMAIN) for i in range(2)]
            qi32v = sb("qi32v", [128, 512], I32, pMAIN)
            qi32g = sb("qi32g", [128, 512], I32, pMAIN)
            psM = [ps("psM%d" % i, [128, 512], F32, pMAIN) for i in range(8)]
            qt = {}
            mm_last = {}
            tr_done = [None] * 8
            op_done = [None] * 8
            for osp in range(8):
                for g in range(32):
                    idx = osp * 32 + g
                    rb = idx % 3
                    qb = idx % 2
                    w = list(wp1_done)
                    if idx < 3:
                        w.append(xh_done)
                    if idx >= 3:
                        w.append(qt[idx - 3])
                    tl = P.dma("sync", wfr[rb][:, :], wf_t[osp, g, :, :], w, SY[2])
                    qe, qc = (("vector", V) if idx % 2 == 0 else ("gpsimd", G))
                    qa = qi32v if idx % 2 == 0 else qi32g
                    wq_war = [mm_last[idx - 2]] if idx >= 2 else []
                    g1 = P.emit(qe, lambda e, rb=rb, qa=qa: e.tensor_scalar(
                        out=qa[:, :], in0=wfr[rb][:, :], scalar1=i2t_c[:, :],
                        scalar2=None, op0=ALU.mult), [tl] + wq_war, qc)
                    qt[idx] = P.emit(qe, lambda e, qb=qb, qa=qa: e.tensor_scalar(
                        out=wqg[qb][:, :], in0=qa[:, :], scalar1=1.0, scalar2=-1.0,
                        op0=ALU.min, op1=ALU.max), [g1], qc)
                    tm = None
                    for tt in range(8):
                        w = [qt[idx], xh_done]
                        if g == 0:
                            if tr_done[tt] is not None:
                                w.append(tr_done[tt])
                        tm = P.emit("tensor", lambda e, g=g, tt=tt, qb=qb: e.matmul(
                            psM[tt][:, :],
                            xh[:, g * 1024 + tt * 128:g * 1024 + tt * 128 + 128],
                            wqg[qb][:, :], start=(g == 0), stop=(g == 31)),
                            w, T if (g == 31 or tt == 7) else None)
                    mm_last[idx] = tm
                for tt in range(8):
                    ev = P.emit("vector", lambda e, tt=tt: e.tensor_scalar(
                        out=outsp[:, tt * 512:(tt + 1) * 512],
                        in0=psM[tt][:, :], scalar1=wsc_c[:, :],
                        scalar2=None, op0=ALU.mult),
                        [mm_last[osp * 32 + 31]] + ([op_done[tt]] if op_done[tt] else []),
                        V)
                    tm2 = None
                    for cj in range(4):
                        tm2 = P.emit("tensor", lambda e, tt=tt, cj=cj: e.matmul(
                            psM[tt][:, cj * 128:(cj + 1) * 128],
                            outsp[:, tt * 512 + cj * 128:tt * 512 + cj * 128 + 128],
                            c_idbf[:, :], start=True, stop=True),
                            [ev], T if cj == 3 else None)
                    tr_done[tt] = tm2
                    def fop(e, tt=tt, osp=osp):
                        dst = opfm[:, :].rearrange("p (c t) -> p c t", c=32)
                        return e.tensor_copy(
                            out=dst[:, osp * 4:osp * 4 + 4, tt * 128:(tt + 1) * 128],
                            in_=psM[tt][:, :])
                    op_done[tt] = P.emit("vector", fop, [tm2], V)
            opfm_done = [t for t in op_done]
            tap("outpre", opfm[:, :], opfm_done)
            pMAIN.close()

        P.cur_phase = 2.5
        pF3 = ExitStack()
        with pF3:
            y2 = sb("y2", [128, 32 * 1024], BF16, pF3)
            psA2 = [ps("psA2%d" % i, [128, 512], F32, pF3) for i in range(2)]
            evh = {}
            for c in range(32):
                for th in range(2):
                    gi = c * 2 + th
                    k = gi % 2
                    w = list(opfm_done)
                    if gi >= 2:
                        w.append(evh[gi - 2])
                    tm = P.emit("tensor", lambda e, c=c, th=th, k=k: e.matmul(
                        psA2[k][:, :], c_h128[:, :],
                        opfm[:, c * 1024 + th * 512:c * 1024 + th * 512 + 512],
                        start=True, stop=True), w, T)
                    evh[gi] = P.emit("vector", lambda e, c=c, th=th, k=k: e.tensor_copy(
                        out=y2[:, c * 1024 + th * 512:c * 1024 + th * 512 + 512],
                        in_=psA2[k][:, :]), [tm], V)
            tb3 = P.dma("sync", bounce[:, :], y2[:, :], [evh[63]], SY[2])
        pOPF.close()
        P.cur_phase = 2.6
        pF4 = ExitStack()
        with pF4:
            y2t = sb("y2t", [128, 32 * 1024], BF16, pF4)
            tb4 = P.dma("sync", y2t[:, :].rearrange("p (g t) -> p g t", g=32),
                        bounce.rearrange("(g j) (c t) -> (j c) g t", j=4, c=32),
                        [tb3], SY[2])
            psB2 = [ps("psB2%d" % i, [128, 512], F32, pF4) for i in range(2)]
            sbf = [sb("sbf%d" % i, [128, 512], F32, pF4) for i in range(2)]
            dstw = ffn_scr.rearrange("(gg j) (cc t) -> (j cc) gg t", j=4, cc=32)
            dvh = {}
            for g in range(32):
                for tsp in range(2):
                    gi = g * 2 + tsp
                    k = gi % 2
                    tm = None
                    for j in range(4):
                        w = [tb4]
                        if gi >= 2:
                            w.append(dvh[gi - 2])
                        tm = P.emit("tensor", lambda e, g=g, tsp=tsp, j=j, k=k: e.matmul(
                            psB2[k][32 * j:32 * j + 32, 0:512],
                            c_h32s[32 * j:32 * j + 32, :],
                            y2t[32 * j:32 * j + 32,
                                g * 1024 + tsp * 512:g * 1024 + tsp * 512 + 512],
                            start=True, stop=True, tile_position=(32 * j, 32 * j)),
                            w, T if j == 3 else None)
                    ev = P.emit("vector", lambda e, k=k: e.tensor_copy(
                        out=sbf[k][:, :], in_=psB2[k][:, :]), [tm], V)
                    dvh[gi] = P.dma("sync", dstw[:, g, tsp * 512:tsp * 512 + 512],
                                    sbf[k][:, :], [ev], SY[2])
            ffn_done = list(dvh.values())
            tap("ffn", ffn_scr, ffn_done)

        P.cur_phase = 3
        pM1 = ExitStack()
        qlat = sb("qlat", [128, 12 * TOWN], BF16, pM1)
        pQ = ExitStack()
        with pQ:
            murln_row = sb("murln_row", [1, TOWN], F32, pQ)
            rln_row = sb("rln_row", [1, TOWN], F32, pQ)
            xfm = sb("xfm", [128, 32 * 1024], BF16, pQ)
            xt2 = [sb("xtB%d" % i, [128, D], F32, pQ) for i in range(2)]
            bn6b = sb("bn6b", [128, 48], F32, pQ)
            mvb = sb("mvb", [128, 2], F32, pQ)
            rlnb_c = sb("rlnb_c", [128, 1], F32, pQ)
            nmub_c = sb("nmub_c", [128, 1], F32, pQ)
            mnrb_c = sb("mnrb_c", [128, 1], F32, pQ)
            prowB = ps("prowB", [1, 512], F32, pQ)
            prowB2 = ps("prowB2", [1, 512], F32, pQ)
            ptrB = ps("ptrB", [128, 512], F32, pQ)
            jw = join()
            ev = None
            rtok = None
            he1 = he2 = None
            t6hist = {}
            for i in range(8):
                b = i % 2
                tl = P.dma("sync", xt2[b][:, :], x_own[i * 128:(i + 1) * 128, :],
                           (jw if i < 2 else [ev, t6hist[i - 2]]), SY[3])
                t1 = None
                for kk in range(8):
                    t1 = P.emit("vector", lambda e, b=b, kk=kk: e.bn_stats(
                        bn6b[:, kk * 6:(kk + 1) * 6],
                        xt2[b][:, kk * 512:(kk + 1) * 512]), [tl], V)
                t2 = P.emit("vector", lambda e: e.bn_aggr(mvb[:, :], bn6b[:, :]),
                            [t1], V)
                t3 = P.emit("scalar", lambda e: e.activation(
                    rlnb_c[:, :], mvb[:, 1:2], AF.Ln, bias=cst[:, 0:1], scale=1.0), [t2], SC)
                t4 = P.emit("scalar", lambda e: e.activation(
                    rlnb_c[:, :], rlnb_c[:, :], AF.Exp, bias=0.0, scale=-0.5), [t3], SC)
                t5 = P.emit("vector", lambda e: e.tensor_scalar(
                    out=nmub_c[:, :], in0=mvb[:, 0:1], scalar1=-1.0, scalar2=None,
                    op0=ALU.mult), [t2], V)
                t6 = P.emit("vector", lambda e: e.tensor_scalar(
                    out=mnrb_c[:, :], in0=nmub_c[:, :], scalar1=rlnb_c[:, :],
                    scalar2=None, op0=ALU.mult), [t4, t5], V)
                t6hist[i] = t6
                half, col = divmod(i, 4)
                hw = [t6] + ([rtok] if rtok else [])
                if col == 0 and half == 1:
                    hw += [he1, he2]
                r1 = P.emit("tensor", lambda e, col=col: e.matmul(
                    prowB[0:1, col * 128:(col + 1) * 128], mnrb_c[:, :], c_idf[:, :],
                    start=True, stop=True), hw, T)
                rtok = P.emit("tensor", lambda e, col=col: e.matmul(
                    prowB2[0:1, col * 128:(col + 1) * 128], rlnb_c[:, :], c_idf[:, :],
                    start=True, stop=True), [t4, r1], T)
                if col == 3:
                    he1 = P.emit("vector", lambda e, half=half: e.tensor_copy(
                        out=murln_row[0:1, half * 512:(half + 1) * 512],
                        in_=prowB[0:1, :]), [rtok], V)
                    he2 = P.emit("vector", lambda e, half=half: e.tensor_copy(
                        out=rln_row[0:1, half * 512:(half + 1) * 512],
                        in_=prowB2[0:1, :]), [rtok], V)

                def wrB(c0, nb, pbank, tokT, i=i):
                    def f(e, c0=c0, nb=nb, i=i):
                        dst = xfm[:, :].rearrange("p (c t) -> p c t", c=32)
                        return e.activation(
                            out=dst[:, c0:c0 + nb, i * 128:(i + 1) * 128],
                            in_=pbank[:, 0:nb * 128], func=AF.Copy)
                    return P.emit("scalar", f, [tokT], SC)
                ev = transposes(xt2[b], 32, wrB, tl, ptrB, c_idf)
            xfm_done = ev
            tap("xfm", xfm[:, :], [xfm_done])
            d1 = P.dma("sync", rowbc[1:2, 0:1024], murln_row[0:1, :], [he1], SY[3])
            d2 = P.dma("sync", rowbc[2:3, 0:1024], rln_row[0:1, :], [he2], SY[3])
            B1q = sb("B1q", [128, 1024], F32, pQ)
            rlnB = sb("rlnB", [128, 1024], F32, pQ)
            d3 = P.dma("sync", B1q[:, :], rowbc[1:2, 0:1024].partition_broadcast(128),
                       [d1], SY[3])
            d4 = P.dma("sync", rlnB[:, :], rowbc[2:3, 0:1024].partition_broadcast(128),
                       [d2], SY[3])

            wqr = [sb("wqr0", [128, 32 * 512], BF16, pQ)]
            psQ = [ps("psQ%d" % i, [128, 512], F32, pQ) for i in range(2)]
            psR = ps("psRq", [1, 512], F32, pQ)
            psC = ps("psCq", [128, 1], F32, pQ)
            qraw = sb("qraw", [128, 512], F32, pQ)
            qc_last = None
            psq_free = [None, None]
            wq_free = None
            ws_done = {}
            for MB in range(3):
                wb = 0
                tl = P.dma("gpsimd", wqr[wb][:, :], wqa_t[MB, :, :],
                           [wq_free] if wq_free else join(), gq)
                for Mi in range(4):
                    M = MB * 4 + Mi
                    tm = None
                    for g in range(32):
                        w = [tl, m2]
                        if M > 0:
                            w.append(ws_done[M - 1])
                        tm = P.emit("tensor", lambda e, Mi=Mi, g=g, wb=wb: e.matmul(
                            psR[0:1, 0:128], ones_c[:, :],
                            wqr[wb][:, g * 512 + Mi * 128:g * 512 + Mi * 128 + 128],
                            start=(g == 0), stop=(g == 31)), w,
                            T if g == 31 else None)
                    er = P.emit("vector", lambda e: e.tensor_copy(
                        out=rowtmp[0:1, 0:128], in_=psR[0:1, 0:128]), [tm], V)
                    ws_done[M] = row_to_col(rowtmp[0:1, 0:128], psC[:, 0:1],
                                            wsq_col[:, M:M + 1], [er, m4])
                for Mi in range(4):
                    M = MB * 4 + Mi
                    for th in range(2):
                        k = (M * 2 + th) % 2
                        tm = None
                        for g in range(32):
                            w = [tl, xfm_done]
                            if g == 0 and psq_free[k] is not None:
                                w.append(psq_free[k])
                            if g == 0:
                                w.append(ws_done[11] if M == 11 else ws_done[M])
                            tm = P.emit("tensor",
                                        lambda e, Mi=Mi, g=g, th=th, k=k, wb=wb: e.matmul(
                                            psQ[k][:, :],
                                            wqr[wb][:, g * 512 + Mi * 128:
                                                    g * 512 + Mi * 128 + 128],
                                            xfm[:, g * 1024 + th * 512:
                                                g * 1024 + th * 512 + 512],
                                            start=(g == 0), stop=(g == 31)),
                                        w, T if g == 31 else None)
                        c1 = P.emit("vector", lambda e, k=k, th=th: e.scalar_tensor_tensor(
                            out=qraw[:, :], in0=psQ[k][:, :], scalar=1.0,
                            in1=rlnB[:, th * 512:th * 512 + 512],
                            op0=ALU.bypass, op1=ALU.mult), [tm, d4, qc_last], V)
                        psq_free[k] = c1
                        qc_last = P.emit(
                            "vector", lambda e, M=M, th=th: e.scalar_tensor_tensor(
                                out=qlat[:, M * 1024 + th * 512:M * 1024 + th * 512 + 512],
                                in0=B1q[:, th * 512:th * 512 + 512],
                                scalar=wsq_col[:, M:M + 1], in1=qraw[:, :],
                                op0=ALU.mult, op1=ALU.add),
                            [c1, d3], V, fwaits=[ws_done[M]])
                wq_free = qc_last
            sqs = sb("sqs", [128, 512], BF16, pQ)
            psS2 = ps("psS2", [1, 512], F32, pQ)
            rms_ev = {}
            for th in range(2):
                tm = None
                sq_tok = None
                for M in range(12):
                    w = [qc_last, tm]
                    if M == 0 and th == 1:
                        w.append(rms_ev[0])
                    sq_tok = P.emit("vector", lambda e, M=M, th=th: e.tensor_tensor(
                        out=sqs[:, :],
                        in0=qlat[:, M * 1024 + th * 512:M * 1024 + th * 512 + 512],
                        in1=qlat[:, M * 1024 + th * 512:M * 1024 + th * 512 + 512],
                        op=ALU.mult), w, V)
                    tm = P.emit("tensor", lambda e, M=M, th=th: e.matmul(
                        psS2[0:1, :], ones_c[:, :], sqs[:, :],
                        start=(M == 0), stop=(M == 11)),
                        [sq_tok] + ([rms_ev[0]] if (M == 0 and th == 1) else []), T)
                rms_ev[th] = P.emit("vector", lambda e, th=th: e.tensor_copy(
                    out=rowtmp[0:1, th * 512:th * 512 + 512], in_=psS2[0:1, :]),
                    [tm], V)
            s3 = P.emit("scalar", lambda e: e.activation(
                rowtmp[0:1, 1024:2048], rowtmp[0:1, 0:1024], AF.Ln,
                bias=cst[0:1, 1:2], scale=1.0 / QL), [rms_ev[0], rms_ev[1]], SC)
            s4 = P.emit("scalar", lambda e: e.activation(
                rowtmp[0:1, 1024:2048], rowtmp[0:1, 1024:2048], AF.Exp,
                bias=0.0, scale=-0.5), [s3], SC)
            d5 = P.dma("sync", rowbc[3:4, 0:1024], rowtmp[0:1, 1024:2048], [s4], SY[3])
            rmsqB = sb("rmsqB", [128, 1024], F32, pQ)
            d6 = P.dma("sync", rmsqB[:, :], rowbc[3:4, 0:1024].partition_broadcast(128),
                       [d5], SY[3])
            ql_done = None
            for M in range(12):
                ql_done = P.emit("vector", lambda e, M=M: e.scalar_tensor_tensor(
                    out=qlat[:, M * 1024:(M + 1) * 1024],
                    in0=qlat[:, M * 1024:(M + 1) * 1024],
                    scalar=c_qnw[:, M:M + 1], in1=rmsqB[:, :],
                    op0=ALU.mult, op1=ALU.mult), [d6, qc_last], V)
            qlat_done = ql_done
            tap("qlat", qlat[:, :], [qlat_done])

        P.cur_phase = 4
        pM2 = ExitStack()
        kvlat = sb("kvlat", [128, 4 * TB], BF16, pM2)
        kpe = sb("kpe", [65, TB], BF16, pM2)
        pC = ExitStack()
        with pC:
            wkva = sb("wkva", [128, 32 * KVA], BF16, pC)
            xt3 = [sb("xtC0", [128, D], F32, pC)]
            xfsp = sb("xfsp", [128, 32 * 512], BF16, pC)
            bn6c = sb("bn6c", [128, 48], F32, pC)
            mvc = sb("mvc", [128, 2], F32, pC)
            rlnc_c = sb("rlnc_c", [128, 1], F32, pC)
            nmuc_c = sb("nmuc_c", [128, 1], F32, pC)
            mnrc_c = sb("mnrc_c", [128, 1], F32, pC)
            kvc = sb("kvc", [128, 6 * 512], F32, pC)
            kvr = sb("kvr", [128, 512], F32, pC)
            sqc = sb("sqc", [128, 512], F32, pC)
            mB = sb("mB_c", [128, 512], F32, pC)
            rB = sb("rB_c", [128, 512], F32, pC)
            rkB = sb("rkB_c", [128, 512], F32, pC)
            ropes = sb("ropes", [32, 512], F32, pC)
            ropes2 = sb("ropes2", [32, 512], F32, pC)
            cosc = sb("cosc", [32, 512], F32, pC)
            sinc = sb("sinc", [32, 512], F32, pC)
            prowC = ps("prowC", [1, 512], F32, pC)
            prowC2 = ps("prowC2", [1, 512], F32, pC)
            ptrC = ps("ptrC", [128, 512], F32, pC)
            psK = [ps("psK%d" % i, [128, 512], F32, pC) for i in range(2)]
            psRC = ps("psRC", [1, 512], F32, pC)
            psCC = ps("psCC", [128, 1], F32, pC)
            wskv_col = sb("wskv_col", [128, 6], F32, pC)

            jw = join()
            lt = P.dma("gpsimd", wkva[:, :], wkva_t, jw, gq)
            ws_kv_done = {}
            prev = None
            _kvoff = [0, 128, 256, 384, 512, 544]
            _kvwid = [128, 128, 128, 128, 32, 32]
            for M in range(6):
                wid = _kvwid[M]
                off = _kvoff[M]
                tm = None
                for g in range(32):
                    w = [lt, m2] + ([prev] if prev else [])
                    tm = P.emit("tensor", lambda e, off=off, g=g, wid=wid: e.matmul(
                        psRC[0:1, 0:wid], ones_c[:, :],
                        wkva[:, g * KVA + off:g * KVA + off + wid],
                        start=(g == 0), stop=(g == 31)), w, T if g == 31 else None)
                er = P.emit("vector", lambda e, wid=wid: e.tensor_copy(
                    out=rowtmp[0:1, 0:wid], in_=psRC[0:1, 0:wid]), [tm], V)
                ws_kv_done[M] = row_to_col(rowtmp[0:1, 0:wid], psCC[0:wid, 0:1],
                                           wskv_col[0:wid, M:M + 1], [er, m4])
                prev = ws_kv_done[M]

            kv_done = None
            t6c = {}
            db1_prev = db4_prev = None
            ev_sp = None
            for sp in range(8):
                rtokc = None
                lc1 = P.dma("sync", cosc[:, :], cos_cov[:, sp * 512:(sp + 1) * 512],
                            jw + ([kv_done] if kv_done else []), SY[4])
                lc2 = P.dma("sync", sinc[:, :], sin_cov[:, sp * 512:(sp + 1) * 512],
                            jw + ([kv_done] if kv_done else []), SY[4])
                for q in range(4):
                    i = sp * 4 + q
                    b = 0
                    w = [kv_done] if (sp > 0 and q < 2) else []
                    if i >= 1:
                        w = w + ([ev_sp] if ev_sp else []) + [t6c[i - 1]]
                    tl = P.dma("sync", xt3[b][:, :], x_cov[i * 128:(i + 1) * 128, :],
                               (jw + w) if i < 2 else w, SY[4])
                    t1 = None
                    for kk in range(8):
                        t1 = P.emit("vector", lambda e, b=b, kk=kk: e.bn_stats(
                            bn6c[:, kk * 6:(kk + 1) * 6],
                            xt3[b][:, kk * 512:(kk + 1) * 512]), [tl], V)
                    t2 = P.emit("vector", lambda e: e.bn_aggr(mvc[:, :], bn6c[:, :]),
                                [t1], V)
                    t3 = P.emit("scalar", lambda e: e.activation(
                        rlnc_c[:, :], mvc[:, 1:2], AF.Ln, bias=cst[:, 0:1], scale=1.0),
                        [t2], SC)
                    t4 = P.emit("scalar", lambda e: e.activation(
                        rlnc_c[:, :], rlnc_c[:, :], AF.Exp, bias=0.0, scale=-0.5),
                        [t3], SC)
                    t5 = P.emit("vector", lambda e: e.tensor_scalar(
                        out=nmuc_c[:, :], in0=mvc[:, 0:1], scalar1=-1.0, scalar2=None,
                        op0=ALU.mult), [t2], V)
                    t6 = P.emit("vector", lambda e: e.tensor_scalar(
                        out=mnrc_c[:, :], in0=nmuc_c[:, :], scalar1=rlnc_c[:, :],
                        scalar2=None, op0=ALU.mult), [t4, t5], V)
                    t6c[i] = t6
                    r1 = P.emit("tensor", lambda e, q=q: e.matmul(
                        prowC[0:1, q * 128:(q + 1) * 128], mnrc_c[:, :], c_idf[:, :],
                        start=True, stop=True), [t6] + ([rtokc] if rtokc else []), T)
                    rtokc = P.emit("tensor", lambda e, q=q: e.matmul(
                        prowC2[0:1, q * 128:(q + 1) * 128], rlnc_c[:, :], c_idf[:, :],
                        start=True, stop=True), [t4, r1], T)

                    def wrC(c0, nb, pbank, tokT, q=q):
                        def f(e, c0=c0, nb=nb, q=q):
                            dst = xfsp[:, :].rearrange("p (c t) -> p c t", c=32)
                            return e.activation(
                                out=dst[:, c0:c0 + nb, q * 128:(q + 1) * 128],
                                in_=pbank[:, 0:nb * 128], func=AF.Copy)
                        return P.emit("scalar", f, [tokT], SC)
                    ev_sp = transposes(xt3[b], 32, wrC, tl, ptrC, c_idf)
                e1 = P.emit("vector", lambda e: e.tensor_copy(
                    out=rowtmp[0:1, 1024:1536], in_=prowC[0:1, 0:512]),
                    [rtokc] + ([db1_prev] if db1_prev else []), V)
                e2 = P.emit("vector", lambda e: e.tensor_copy(
                    out=rowtmp[0:1, 1536:2048], in_=prowC2[0:1, 0:512]), [rtokc], V)
                db1 = P.dma("sync", rowbc[4:5, 0:1024], rowtmp[0:1, 1024:2048],
                            [e1, e2], SY[4])
                db1_prev = db1
                db2 = P.dma("sync", mB[:, :], rowbc[4:5, 0:512].partition_broadcast(128),
                            [db1], SY[4])
                db3 = P.dma("sync", rB[:, :], rowbc[4:5, 512:1024].partition_broadcast(128),
                            [db1], SY[4])
                cor = {}
                for M in range(6):
                    wid = _kvwid[M]
                    off = _kvoff[M]
                    k = M % 2
                    tm = None
                    for g in range(32):
                        w = [lt, ev_sp]
                        if g == 0 and M >= 2:
                            w.append(cor[M - 2])
                        tm = P.emit("tensor", lambda e, off=off, g=g, wid=wid, k=k: e.matmul(
                            psK[k][0:wid, :],
                            wkva[:, g * KVA + off:g * KVA + off + wid],
                            xfsp[:, g * 512:(g + 1) * 512],
                            start=(g == 0), stop=(g == 31)), w,
                            T if g == 31 else None)
                    c1 = P.emit("vector", lambda e, M=M, wid=wid, k=k: e.scalar_tensor_tensor(
                        out=kvr[0:wid, :], in0=psK[k][0:wid, :], scalar=1.0,
                        in1=rB[0:wid, :], op0=ALU.bypass, op1=ALU.mult),
                        [tm, db3] + ([cor[M - 1]] if M >= 1 else []), V)
                    cor[M] = P.emit("vector", lambda e, M=M, wid=wid: e.scalar_tensor_tensor(
                        out=kvc[0:wid, M * 512:(M + 1) * 512], in0=mB[0:wid, :],
                        scalar=wskv_col[0:wid, M:M + 1], in1=kvr[0:wid, :],
                        op0=ALU.mult, op1=ALU.add), [c1, db2], V,
                        fwaits=[ws_kv_done[M]])
                tm = None
                sq_tok = None
                for M in range(4):
                    sq_tok = P.emit("vector", lambda e, M=M: e.tensor_tensor(
                        out=sqc[:, :], in0=kvc[:, M * 512:(M + 1) * 512],
                        in1=kvc[:, M * 512:(M + 1) * 512], op=ALU.mult),
                        [cor[M], tm], V)
                    tm = P.emit("tensor", lambda e, M=M: e.matmul(
                        psRC[0:1, :], onesf[:, :], sqc[:, :],
                        start=(M == 0), stop=(M == 3)), [sq_tok, m3], T)
                er = P.emit("vector", lambda e: e.tensor_copy(
                    out=rowtmp[0:1, 2048:2560], in_=psRC[0:1, :]), [tm], V)
                s5 = P.emit("scalar", lambda e: e.activation(
                    rowtmp[0:1, 2560:3072], rowtmp[0:1, 2048:2560], AF.Ln,
                    bias=cst[0:1, 1:2], scale=1.0 / KVL),
                    [er] + ([db4_prev] if db4_prev else []), SC)
                s6 = P.emit("scalar", lambda e: e.activation(
                    rowtmp[0:1, 2560:3072], rowtmp[0:1, 2560:3072], AF.Exp,
                    bias=0.0, scale=-0.5), [s5], SC)
                db4 = P.dma("sync", rowbc[5:6, 0:512], rowtmp[0:1, 2560:3072],
                            [s6], SY[4])
                db4_prev = db4
                db5 = P.dma("sync", rkB[:, :], rowbc[5:6, 0:512].partition_broadcast(128),
                            [db4], SY[4])
                kv_ev = {}
                for M in range(4):
                    kv_ev[M] = P.emit("vector", lambda e, M=M, sp=sp: e.scalar_tensor_tensor(
                        out=kvlat[:, M * TB + sp * 512:M * TB + sp * 512 + 512],
                        in0=kvc[:, M * 512:(M + 1) * 512],
                        scalar=c_kvnw[:, M:M + 1], in1=rkB[:, :],
                        op0=ALU.mult, op1=ALU.mult), [cor[M], db5], V)
                ro1 = P.emit("vector", lambda e: e.tensor_tensor(
                    out=ropes[:, :], in0=kvc[0:32, 4 * 512:5 * 512],
                    in1=cosc[:, :], op=ALU.mult), [cor[4], cor[5], lc1], V)
                ro2 = P.emit("vector", lambda e: e.tensor_tensor(
                    out=ropes2[:, :], in0=kvc[0:32, 5 * 512:6 * 512],
                    in1=sinc[:, :], op=ALU.mult), [cor[5], lc2], V)
                ro3 = P.emit("vector", lambda e, sp=sp: e.tensor_tensor(
                    out=kpe[0:32, sp * 512:(sp + 1) * 512], in0=ropes[:, :],
                    in1=ropes2[:, :], op=ALU.subtract), [ro1, ro2], V)
                ro4 = P.emit("vector", lambda e: e.tensor_tensor(
                    out=ropes[:, :], in0=kvc[0:32, 4 * 512:5 * 512],
                    in1=sinc[:, :], op=ALU.mult), [ro3, cor[4]], V)
                ro5 = P.emit("vector", lambda e: e.tensor_tensor(
                    out=ropes2[:, :], in0=kvc[0:32, 5 * 512:6 * 512],
                    in1=cosc[:, :], op=ALU.mult), [ro3, cor[5]], V)
                ro6 = P.emit("vector", lambda e, sp=sp: e.tensor_tensor(
                    out=kpe[32:64, sp * 512:(sp + 1) * 512], in0=ropes[:, :],
                    in1=ropes2[:, :], op=ALU.add), [ro4, ro5], V)
                kv_done = ro6
                kv_sp_done = [kv_ev[M] for M in range(4)] + [ro6]
            kvlat_done = kv_sp_done
            tap("kvlat", kvlat[:, :], kvlat_done)
            tap("kpe", kpe[0:64, :], kvlat_done)

        P.cur_phase = 5
        pT = ExitStack()
        with pT:
            coso = sb("coso", [32, TOWN], F32, pT)
            sino = sb("sino", [32, TOWN], F32, pT)
            wqbr = [sb("wqbr%d" % i, [128, 12 * 192], BF16, pT) for i in range(2)]
            wkbr = [sb("wkbr%d" % i, [128, 4 * 256], BF16, pT) for i in range(2)]
            qn = [sb("qn%d" % i, [128, TOWN], BF16, pT) for i in range(2)]
            qpe = [sb("qpe%d" % i, [65, TOWN], BF16, pT) for i in range(2)]
            kn = [sb("kn%d" % i, [128, TB], BF16, pT) for i in range(2)]
            vfm = sb("vfm", [128, TB], BF16, pT)
            vtm = [sb("vtm%d" % i, [128, TB], BF16, pT) for i in range(2)]
            ptb = sb("ptb", [128, 16 * 512], BF16, pT)
            kpe2s = sb("kpe2s", [1, TB], BF16, pT)
            attnh = sb("attnh", [128, TOWN], BF16, pT)
            q2s = sb("q2s", [128, TOWN], BF16, pT)
            k2s = vfm
            negm = sb("negm", [1, TOWN], BF16, pT)
            linvB = sb("linvB", [128, TOWN], BF16, pT)
            rop1 = sb("rop1", [32, 512], F32, pT)
            rop2 = sb("rop2", [32, 512], F32, pT)
            rop3 = sb("rop3", [32, 512], F32, pT)
            psq = [ps("psq%d" % i, [128, 512], F32, pT) for i in range(2)]
            psq2 = ps("psq2", [32, 512], F32, pT)
            pss = [ps("pss%d" % i, [128, 512], F32, pT) for i in range(2)]
            psr = ps("psr", [1, 512], F32, pT)
            pstr = ps("pstr", [128, 512], BF16, pT)
            psav = ps("psavb", [128, 512], F32, pT)

            jw = join()
            lo1 = P.dma("sync", coso[:, :], cos_own, jw, SY[4])
            lo2 = P.dma("sync", sino[:, :], sin_own, jw, SY[4])
            mko = P.emit("vector", lambda e: e.memset(kpe[64:65, :], 1.0), jw, V)
            kp_mm = None
            kp_ev = None
            for sp2 in range(8):
                kpv = P.emit("vector", lambda e, sp2=sp2: e.tensor_tensor(
                    out=ptb[0:64, 0:512],
                    in0=kpe[0:64, sp2 * 512:(sp2 + 1) * 512],
                    in1=kpe[0:64, sp2 * 512:(sp2 + 1) * 512], op=ALU.mult),
                    jw + ([kp_mm] if kp_mm else []), V)
                kp_mm = P.emit("tensor", lambda e: e.matmul(
                    psq[0][0:1, 0:512], ones_c[0:64, :], ptb[0:64, 0:512],
                    start=True, stop=True),
                    [kpv, m2] + ([kp_ev] if kp_ev else []), T)
                kp_ev = P.emit("vector", lambda e, sp2=sp2: e.tensor_copy(
                    out=kpe2s[0:1, sp2 * 512:(sp2 + 1) * 512],
                    in_=psq[0][0:1, 0:512]), [kp_mm], V)
            kpe2s_done = kp_ev

            state = {}

            def build_head(h):
                st_h = state[h] = {}
                hb = h % 2
                pv = state.get(h - 1)
                pv2 = state.get(h - 2)
                sread = [pv2["s_tm_last"]] if pv2 else []
                psq_free = [pv["psq0_free"] if pv else kpe2s_done,
                            pv["psq1_free"] if pv else None]
                wl1 = P.dma("gpsimd", wqbr[hb][:, :], wqb_t[h, :, :],
                            jw if h < 2 else [pv2["q_tm_last"]], gq)
                wl2 = P.dma("gpsimd", wkbr[hb][:, :], wkvb_t[h, :, :],
                            jw if h < 2 else [pv2["kv_tm_last"]], gq)
                yield
                qev = {}
                for th in range(2):
                    k = th
                    tm = None
                    for g in range(12):
                        w = [wl1, qlat_done]
                        if g == 0 and psq_free[k] is not None:
                            w.append(psq_free[k])
                        tm = P.emit("tensor", lambda e, g=g, th=th, k=k, hb=hb: e.matmul(
                            psq[k][:, :],
                            wqbr[hb][:, g * 192:g * 192 + 128],
                            qlat[:, g * 1024 + th * 512:g * 1024 + th * 512 + 512],
                            start=(g == 0), stop=(g == 11)), w,
                            T if g == 11 else None)
                        if g % 4 == 3:
                            yield
                    qev[th] = P.emit("vector", lambda e, th=th, k=k, hb=hb: e.tensor_scalar(
                        out=qn[hb][:, th * 512:th * 512 + 512], in0=psq[k][:, :],
                        scalar1=SCALE, scalar2=None, op0=ALU.mult),
                        [tm] + sread, V)
                pe_ev = {}
                q_tm_last = None
                for th in range(2):
                    k = th
                    tm1 = None
                    for g in range(12):
                        w = [wl1, qlat_done, qev[th]]
                        tm1 = P.emit("tensor", lambda e, g=g, th=th, k=k, hb=hb: e.matmul(
                            psq[k][0:32, 0:512],
                            wqbr[hb][:, g * 192 + 128:g * 192 + 160],
                            qlat[:, g * 1024 + th * 512:g * 1024 + th * 512 + 512],
                            start=(g == 0), stop=(g == 11)), w,
                            T if g == 11 else None)
                        if g % 6 == 5:
                            yield
                    tm2 = None
                    for g in range(12):
                        w2_ = [tm1]
                        if g == 0:
                            if th == 1:
                                w2_.append(pe_ev[0])
                            elif pv:
                                w2_.append(pv["pe_ev1"])
                        tm2 = P.emit("tensor", lambda e, g=g, th=th, k=k, hb=hb: e.matmul(
                            psq2[0:32, 0:512],
                            wqbr[hb][:, g * 192 + 160:g * 192 + 192],
                            qlat[:, g * 1024 + th * 512:g * 1024 + th * 512 + 512],
                            start=(g == 0), stop=(g == 11)), w2_,
                            T if g == 11 else None)
                        if g % 6 == 5:
                            yield
                    q_tm_last = tm2
                    p1 = P.emit("vector", lambda e, th=th, k=k: e.tensor_tensor(
                        out=rop1[:, :], in0=psq[k][0:32, 0:512],
                        in1=coso[:, th * 512:th * 512 + 512], op=ALU.mult),
                        [tm1, tm2, lo1], V)
                    p2 = P.emit("vector", lambda e, th=th, k=k: e.tensor_tensor(
                        out=rop2[:, :], in0=psq2[0:32, 0:512],
                        in1=sino[:, th * 512:th * 512 + 512], op=ALU.mult),
                        [tm2, lo2], V)
                    p3 = P.emit("vector", lambda e: e.tensor_tensor(
                        out=rop3[:, :], in0=rop1[:, :], in1=rop2[:, :],
                        op=ALU.subtract), [p1, p2], V)
                    p4 = P.emit("vector", lambda e, th=th, hb=hb: e.tensor_scalar(
                        out=qpe[hb][0:32, th * 512:th * 512 + 512], in0=rop3[:, :],
                        scalar1=SCALE, scalar2=None, op0=ALU.mult),
                        [p3] + sread, V)
                    p5 = P.emit("vector", lambda e, th=th, k=k: e.tensor_tensor(
                        out=rop1[:, :], in0=psq[k][0:32, 0:512],
                        in1=sino[:, th * 512:th * 512 + 512], op=ALU.mult), [p3], V)
                    p6 = P.emit("vector", lambda e, th=th, k=k: e.tensor_tensor(
                        out=rop2[:, :], in0=psq2[0:32, 0:512],
                        in1=coso[:, th * 512:th * 512 + 512], op=ALU.mult), [p3], V)
                    p7 = P.emit("vector", lambda e: e.tensor_tensor(
                        out=rop3[:, :], in0=rop1[:, :], in1=rop2[:, :],
                        op=ALU.add), [p5, p6], V)
                    pe_ev[th] = P.emit("vector", lambda e, th=th, hb=hb: e.tensor_scalar(
                        out=qpe[hb][32:64, th * 512:th * 512 + 512], in0=rop3[:, :],
                        scalar1=SCALE, scalar2=None, op0=ALU.mult),
                        [p4, p7] + sread, V)
                    yield
                q_done = [qev[0], qev[1], pe_ev[0], pe_ev[1]]

                kv_ev = []
                kv_tm_last = None
                for which in range(2):
                    for sp2 in range(8):
                        k = sp2 % 2
                        tm = None
                        for g in range(4):
                            w = [wl2] + list(kvlat_done)
                            if which == 0 and sp2 == 0 and g == 0:
                                w += [pe_ev[0], pe_ev[1]]
                            if len(kv_ev) >= 2 and g == 0:
                                w.append(kv_ev[-2])
                            tm = P.emit("tensor",
                                        lambda e, g=g, sp2=sp2, which=which, k=k,
                                        hb=hb: e.matmul(
                                            psq[k][:, :],
                                            wkbr[hb][:, g * 256 + which * 128:
                                                     g * 256 + which * 128 + 128],
                                            kvlat[:, g * TB + sp2 * 512:
                                                  g * TB + sp2 * 512 + 512],
                                            start=(g == 0), stop=(g == 3)), w,
                                        T if g == 3 else None)
                        kv_tm_last = tm
                        dst = kn[hb] if which == 0 else vfm
                        evw = [tm]
                        if which == 0 and sp2 == 0:
                            evw += sread
                        if which == 1 and sp2 == 0 and pv:
                            evw += [pv["vt_done"], pv["tmk_tm"]]
                        kv_ev.append(P.emit(
                            "vector", lambda e, sp2=sp2, k=k, dst=dst: e.tensor_copy(
                                out=dst[:, sp2 * 512:sp2 * 512 + 512],
                                in_=psq[k][:, :]), evw, V))
                        if sp2 % 2 == 1:
                            yield
                kv_done_h = kv_ev[-1]

                lastv = None
                for c0 in range(0, 32, 4):
                    tmv = None
                    for kk in range(4):
                        c = c0 + kk
                        w = [kv_done_h]
                        if c0 == 0 and kk == 0 and pv:
                            w.append(pv["vt_done"])
                        if lastv is not None:
                            w.append(lastv)
                        tmv = P.emit("tensor", lambda e, c=c, kk=kk: e.transpose(
                            pstr[:, kk * 128:(kk + 1) * 128],
                            vfm[:, c * 128:(c + 1) * 128], c_idbf[:, :]),
                            w, T if kk == 3 else None)
                    wv = [tmv]
                    if c0 == 0 and pv2:
                        wv.append(pv2["av_tm_last"])
                    lastv = P.emit("vector", lambda e, c0=c0, hb=hb: e.tensor_copy(
                        out=vtm[hb][:, c0 * 128:c0 * 128 + 4 * 128],
                        in_=pstr[:, 0:4 * 128]), wv, V)
                    if c0 % 8 == 4:
                        yield
                vt_done = lastv

                q2a = P.emit("vector", lambda e, hb=hb: e.tensor_tensor(
                    out=q2s[:, :], in0=qn[hb][:, :], in1=qn[hb][:, :], op=ALU.mult),
                    q_done + ([pv["q2r_tm"]] if pv else []), V)
                qpe2 = P.emit("vector", lambda e, hb=hb: e.tensor_tensor(
                    out=k2s[0:64, 0:TOWN], in0=qpe[hb][0:64, :],
                    in1=qpe[hb][0:64, :], op=ALU.mult),
                    q_done + [vt_done] + ([pv["tmk_tm"]] if pv else []), V)
                yield
                q2r = None
                q2r_tm = None
                for th in range(2):
                    wq_ = [q2a, m2, q2r if q2r else kv_ev[14]]
                    tmq = P.emit("tensor", lambda e, th=th: e.matmul(
                        psq[0][0:1, 0:512], ones_c[:, :],
                        q2s[:, th * 512:th * 512 + 512],
                        start=True, stop=False), wq_, None)
                    tmq = P.emit("tensor", lambda e, th=th: e.matmul(
                        psq[0][0:1, 0:512], ones_c[0:64, :],
                        k2s[0:64, th * 512:th * 512 + 512],
                        start=False, stop=True), [qpe2], T)
                    q2r_tm = tmq
                    q2r = P.emit("vector", lambda e, th=th: e.tensor_copy(
                        out=rowtmp[0:1, th * 512:th * 512 + 512],
                        in_=psq[0][0:1, 0:512]),
                        [tmq] + ([pv["nr1"]] if pv else []), V)
                yield
                k2a = P.emit("vector", lambda e, hb=hb: e.tensor_tensor(
                    out=k2s[:, :], in0=kn[hb][:, :], in1=kn[hb][:, :], op=ALU.mult),
                    [kv_done_h, vt_done, q2r_tm, qpe2, q2r], V)
                tmk_tm = None
                tmax = None
                for sp2 in range(8):
                    w1 = [k2a, m2, tmax if tmax else q2r]
                    P.emit("tensor", lambda e, sp2=sp2: e.matmul(
                        psq[0][0:1, 0:512], ones_c[:, :],
                        k2s[:, sp2 * 512:sp2 * 512 + 512],
                        start=True, stop=False), w1, None)
                    tmk_tm = P.emit("tensor", lambda e, sp2=sp2: e.matmul(
                        psq[0][0:1, 0:512], ones_r[0:1, 0:1],
                        kpe2s[0:1, sp2 * 512:sp2 * 512 + 512],
                        start=False, stop=True), [kpe2s_done], T)
                    tmax = P.emit("vector", lambda e, sp2=sp2: e.reduce_max(
                        rowtmp[0:1, 2048 + sp2:2048 + sp2 + 1], psq[0][0:1, 0:512],
                        axis=AX), [tmk_tm], V)
                    if sp2 % 4 == 3:
                        yield
                mk1 = P.emit("vector", lambda e: e.reduce_max(
                    rowtmp[0:1, 2056:2057], rowtmp[0:1, 2048:2056], axis=AX),
                    [tmax], V)
                mk2 = P.emit("scalar", lambda e: e.activation(
                    rowtmp[0:1, 2057:2058], rowtmp[0:1, 2056:2057], AF.Ln,
                    bias=cst[0:1, 2:3], scale=1.0), [mk1], SC)
                mk3 = P.emit("scalar", lambda e: e.activation(
                    rowtmp[0:1, 2057:2058], rowtmp[0:1, 2057:2058], AF.Exp,
                    bias=0.0, scale=0.5), [mk2], SC)
                mk4 = P.emit("vector", lambda e: e.tensor_scalar(
                    out=rowtmp[0:1, 2058:2059], in0=rowtmp[0:1, 2057:2058],
                    scalar1=-1.02, scalar2=None, op0=ALU.mult), [mk3], V)
                nr1 = P.emit("scalar", lambda e: e.activation(
                    rowtmp[0:1, 1024:2048], rowtmp[0:1, 0:1024], AF.Ln,
                    bias=cst[0:1, 2:3], scale=1.0), [q2r, mk4], SC)
                nr2 = P.emit("scalar", lambda e: e.activation(
                    rowtmp[0:1, 1024:2048], rowtmp[0:1, 1024:2048], AF.Exp,
                    bias=0.0, scale=0.5), [nr1], SC)
                yield
                nr3 = P.emit("vector", lambda e: e.tensor_scalar(
                    out=negm[0:1, :], in0=rowtmp[0:1, 1024:2048],
                    scalar1=rowtmp[0:1, 2058:2059], scalar2=None, op0=ALU.mult),
                    [nr2] + ([pv["nb1"]] if pv else []), V, fwaits=[mk4])
                nb1 = P.dma("sync", rowbc[7:8, 0:1024].bitcast(BF16)[0:1, 0:1024],
                            negm[0:1, :],
                            [nr3] + ([pv["nb2"]] if pv else []), SY[4])
                nb2 = P.dma("sync", qpe[hb][64:65, :],
                            rowbc[7:8, 0:1024].bitcast(BF16)[0:1, 0:1024],
                            [nb1, pe_ev[0], pe_ev[1]] + sread, SY[4])
                if h == 0:
                    tap("negm_h0", negm[0:1, :], [nr3])
                    tap("qn_h0", qn[0][:, :], q_done)
                    tap("qpe_h0", qpe[0][0:64, :], q_done)
                    tap("kn_h0", kn[0][:, :], [kv_ev[7]])
                    tap("vtm_h0", vtm[0][:, :], [vt_done])
                st_h.update(q_done=q_done, kv_ev=kv_ev, kv_done=kv_done_h,
                            vt_done=vt_done, nb1=nb1, nb2=nb2, nr1=nr1,
                            q_tm_last=q_tm_last, kv_tm_last=kv_tm_last,
                            q2r_tm=q2r_tm, tmk_tm=tmk_tm, pe_ev1=pe_ev[1],
                            psq0_free=tmax, psq1_free=kv_done_h)

            def s_loop(h, gen):
                st_h = state[h]
                pv = state.get(h - 1)
                hb = h % 2
                qn_, qpe_, kn_, vtm_ = qn[hb], qpe[hb], kn[hb], vtm[hb]
                q_done = st_h["q_done"]
                kv_ev = st_h["kv_ev"]
                kv_done_h = st_h["kv_done"]
                vt_done = st_h["vt_done"]
                nb2 = st_h["nb2"]
                s_tm_last = None
                av_tm_last = None
                s_done = None
                avev = {}
                lrow_ev = {}
                prev_out = pv["out"] if pv else None
                prev_dl1 = pv["dl1"] if pv else None
                for g5 in range(2):
                    KC = 16 * (g5 + 1)
                    av_tok = None
                    l_tok = None
                    exp_done = {}
                    ex_tok = {}
                    for step in range(KC + 2):
                        kc = step
                        if kc < KC:
                            kb = kc % 2
                            sl = kc % 16
                            w = q_done + [kv_ev[kc // 4], nb2, kv_done_h]
                            if kc >= 2:
                                w.append(ex_tok[kc - 2])
                            P.emit("tensor", lambda e, kc=kc, g5=g5, kb=kb: e.matmul(
                                pss[kb][:, :], kn_[:, kc * 128:(kc + 1) * 128],
                                qn_[:, g5 * 512:g5 * 512 + 512],
                                start=True, stop=False), w, None)
                            tm = P.emit("tensor", lambda e, kc=kc, g5=g5, kb=kb: e.matmul(
                                pss[kb][:, :], kpe[:, kc * 128:(kc + 1) * 128],
                                qpe_[:, g5 * 512:g5 * 512 + 512],
                                start=False, stop=True), [mko], T)
                            s_tm_last = tm
                            ex = P.emit("scalar", lambda e, sl=sl, kb=kb: e.activation(
                                out=ptb[:, sl * 512:(sl + 1) * 512], in_=pss[kb][:, :],
                                func=AF.Exp, bias=0.0, scale=1.0), [tm], SC)
                            ex_tok[kc] = ex
                            mtoks = []
                            for dq in range(4):
                                i_t = 4 * g5 + dq
                                if 4 * i_t <= kc < 4 * i_t + 4:
                                    O = 128 * kc - 512 * i_t
                                    mtoks.append(P.emit(
                                        "vector",
                                        lambda e, sl=sl, dq=dq, O=O: e.scalar_tensor_tensor(
                                            out=ptb[:, sl * 512 + dq * 128:
                                                    sl * 512 + dq * 128 + 128],
                                            in0=ptb[:, sl * 512 + dq * 128:
                                                    sl * 512 + dq * 128 + 128],
                                            scalar=1.0,
                                            in1=c_strip[:, 384 - O:384 - O + 128],
                                            op0=ALU.bypass, op1=ALU.mult), [ex], V))
                                elif kc >= 4 * i_t + 4:
                                    mtoks.append(P.emit(
                                        "vector", lambda e, sl=sl, dq=dq: e.memset(
                                            ptb[:, sl * 512 + dq * 128:
                                                sl * 512 + dq * 128 + 128], 0.0),
                                        [ex], V))
                            exp_done[kc] = mtoks[-1] if mtoks else ex
                        lc = step - 2
                        if lc >= 0:
                            lsl = lc % 16
                            lw = [exp_done[lc], m2]
                            if lc == 0:
                                if g5 > 0:
                                    lw.append(lrow_ev[g5 - 1])
                                elif prev_dl1 is not None:
                                    lw.append(prev_dl1)
                            l_tok = P.emit("tensor", lambda e, lsl=lsl, lc=lc, KC=KC: e.matmul(
                                psr[0:1, 0:512], ones_c[:, :],
                                ptb[:, lsl * 512:(lsl + 1) * 512],
                                start=(lc == 0), stop=(lc == KC - 1)), lw,
                                T if lc == KC - 1 else None)
                            aw = [exp_done[lc], vt_done]
                            if lc == 0:
                                if g5 > 0:
                                    aw.append(avev[g5 - 1])
                                elif prev_out is not None:
                                    aw.append(prev_out)
                            av_tok = P.emit("tensor", lambda e, lsl=lsl, lc=lc, KC=KC: e.matmul(
                                psav[:, :], vtm_[:, lc * 128:(lc + 1) * 128],
                                ptb[:, lsl * 512:(lsl + 1) * 512],
                                start=(lc == 0), stop=(lc == KC - 1)), aw,
                                T if lc == KC - 1 else None)
                            av_tm_last = av_tok
                        if gen is not None:
                            next(gen, None)
                    avw = [av_tok]
                    if g5 == 0 and prev_out is not None:
                        avw.append(prev_out)
                    avev[g5] = P.emit("vector", lambda e, g5=g5: e.tensor_copy(
                        out=attnh[:, g5 * 512:(g5 + 1) * 512], in_=psav[:, :]),
                        avw, V)
                    lvw = [l_tok]
                    if g5 == 0 and prev_dl1 is not None:
                        lvw.append(prev_dl1)
                    lrow_ev[g5] = P.emit("vector", lambda e, g5=g5: e.tensor_copy(
                        out=rowtmp[0:1, 3072 + g5 * 512:3072 + (g5 + 1) * 512],
                        in_=psr[0:1, 0:512]), lvw, V)
                    s_done = avev[g5]
                rc = P.emit("vector", lambda e: e.reciprocal(
                    rowtmp[0:1, 3072:3072 + 1024],
                    rowtmp[0:1, 3072:3072 + 1024]),
                    [lrow_ev[g] for g in range(2)], V)
                dl1 = P.dma("sync", rowbc[6:7, 0:1024], rowtmp[0:1, 3072:4096],
                            [rc], SY[4])
                dl2 = P.dma("gpsimd", linvB[:, :],
                            rowbc[6:7, 0:1024].partition_broadcast(128), [dl1], gq)
                nm = P.emit("vector", lambda e: e.tensor_tensor(
                    out=attnh[:, :], in0=attnh[:, :], in1=linvB[:, :],
                    op=ALU.mult), [dl2] + [avev[g] for g in range(2)], V)
                da = P.dma("sync", attn_scr[h * 128:(h + 1) * 128, :], attnh[:, :],
                           [nm], SY[4])
                if h == 0:
                    tap("pt_h0", ptb[:, 0:32 * 256], [s_done])
                    tap("lrow_h0", rowtmp[0:1, 3072:4096],
                        [lrow_ev[g] for g in range(2)])
                    tap("attnh_h0", attnh[:, :], [nm])
                st_h.update(s_done=s_done, out=da, dl1=dl1, dl2=dl2,
                            s_tm_last=s_tm_last, av_tm_last=av_tm_last)

            g0 = build_head(0)
            for _ in g0:
                pass
            for h in range(NH):
                gnext = build_head(h + 1) if h + 1 < NH else None
                s_loop(h, gnext)
                if gnext is not None:
                    for _ in gnext:
                        pass
            attn_done = [state[hh]["out"] for hh in range(NH)]
            tap("attn_all", attn_scr, attn_done)

        P.cur_phase = 6
        pZ = ExitStack()
        with pZ:
            attn_all = sb("attn_all", [128, 32 * 1024], BF16, pZ)
            wor = [sb("wor%d" % i, [128, 512], F32, pZ) for i in range(3)]
            wob = [sb("wob%d" % i, [128, 512], BF16, pZ) for i in range(2)]
            xsl = [sb("xsl%d" % i, [128, 512], F32, pZ) for i in range(2)]
            fsl = [sb("fsl%d" % i, [128, 512], F32, pZ) for i in range(2)]
            oub = [sb("oub%d" % i, [128, 512], F32, pZ) for i in range(2)]
            psO = [ps("psO%d" % i, [128, 512], F32, pZ) for i in range(8)]
            jw = join()
            srcv = attn_scr.rearrange("(k p) t -> p k t", p=128)
            dstv = attn_all[:, :].rearrange("p (k t) -> p k t", k=32)
            la_toks = []
            for kq in range(4):
                la_toks.append(P.dma("sync", dstv[:, kq * 8:(kq + 1) * 8, :],
                                     srcv[:, kq * 8:(kq + 1) * 8, :],
                                     attn_done + jw, SY[0]))
            cast_prev = None
            mm_done2 = {}
            ev_done = {}
            out_toks = []
            for osp in range(8):
                tmz = {}
                for K in range(32):
                    idx = osp * 32 + K
                    rb = idx % 3
                    qb = idx % 2
                    w = jw if idx == 0 else []
                    if idx >= 3:
                        w = [mm_done2[idx - 3]]
                    tl = P.dma("sync", wor[rb][:, :], wo_t[osp, K, :, :], w, SY[0])
                    cw = [tl]
                    if idx >= 2:
                        cw.append(mm_done2[idx - 2])
                    cast_prev = P.emit("scalar", lambda e, rb=rb, qb=qb: e.activation(
                        out=wob[qb][:, :], in_=wor[rb][:, :], func=AF.Copy),
                        cw, SC)
                    tm = None
                    for tt in range(8):
                        w2 = [cast_prev] + la_toks
                        if K == 0 and ev_done.get((osp - 1, tt)) is not None:
                            w2.append(ev_done[(osp - 1, tt)])
                        tm = P.emit("tensor", lambda e, K=K, tt=tt, qb=qb: e.matmul(
                            psO[tt][:, :],
                            attn_all[:, K * 1024 + tt * 128:K * 1024 + tt * 128 + 128],
                            wob[qb][:, :], start=(K == 0), stop=(K == 31)),
                            w2, T if (K == 31 or tt == 7) else None)
                        if K == 31:
                            tmz[tt] = tm
                    mm_done2[idx] = tm
                for tt in range(8):
                    b = tt % 2
                    xw = [out_toks[-2]] if len(out_toks) >= 2 else []
                    tx = P.dma("sync", xsl[b][:, :],
                               x_own[tt * 128:(tt + 1) * 128,
                                     osp * 512:(osp + 1) * 512], xw, SY[0])
                    e1 = P.emit("vector", lambda e, tt=tt, b=b: e.scalar_tensor_tensor(
                        out=oub[b][:, :], in0=psO[tt][:, :], scalar=1.0,
                        in1=xsl[b][:, :], op0=ALU.bypass, op1=ALU.add),
                        [tmz[tt], tx], V)
                    ev_done[(osp, tt)] = e1
                    tf = P.dma("sync",
                               fsl[b][:, :].rearrange("p (c t) -> p c t", c=4),
                               ffn_scr[:, :].rearrange("p (c t) -> p c t", c=32)
                               [:, 4 * osp:4 * osp + 4, tt * 128:(tt + 1) * 128],
                               ffn_done + xw, SY[0])
                    tm2 = None
                    for cj in range(4):
                        tm2 = P.emit("tensor", lambda e, cj=cj, tt=tt, b=b: e.transpose(
                            psO[tt][:, cj * 128:(cj + 1) * 128],
                            fsl[b][:, cj * 128:(cj + 1) * 128], c_idf[:, :]),
                            [tf, e1], T if cj == 3 else None)
                    e2 = P.emit("vector", lambda e, tt=tt, b=b: e.scalar_tensor_tensor(
                        out=oub[b][:, :], in0=oub[b][:, :], scalar=1.0,
                        in1=psO[tt][:, :], op0=ALU.bypass, op1=ALU.add),
                        [tm2], V)
                    ot = P.dma("sync", out[tt * 128:(tt + 1) * 128,
                                          osp * 512:(osp + 1) * 512],
                               oub[b][:, :], [e2], SY[0])
                    out_toks.append(ot)

        pM2.close()
        pM1.close()
        final_waits = [(c, c.n) for c in ALL]
        with nc.Block() as block:
            P.finalize(block, final_waits)
    return nc


def _prep_inputs(inputs):
    x = np.asarray(inputs["L0_attn_x"])
    wq_a = np.asarray(inputs["L0_attn_wq_a"])
    q_norm_w = np.asarray(inputs["L0_attn_q_norm_w"])
    wq_b = np.asarray(inputs["L0_attn_wq_b"])
    wkv_a = np.asarray(inputs["L0_attn_wkv_a"])
    kv_norm_w = np.asarray(inputs["L0_attn_kv_norm_w"])
    wkv_b = np.asarray(inputs["L0_attn_wkv_b"])
    wo = np.asarray(inputs["L0_attn_wo"])
    gamma = np.asarray(inputs["L0_ffn_gamma"])
    beta = np.asarray(inputs["L0_ffn_beta"])
    wf = np.asarray(inputs["L0_ffn_weight"])

    pe = np.concatenate([np.arange(0, DPE, 2), np.arange(1, DPE, 2)])

    wqaT = np.ascontiguousarray(wq_a.T)
    wqa_t = np.empty((3, 128, 32 * 512), np.float32)
    for MB in range(3):
        blk = wqaT[:, MB * 512:(MB + 1) * 512].reshape(32, 128, 512)
        wqa_t[MB] = blk.transpose(1, 0, 2).reshape(128, 32 * 512)

    wqbT = wq_b.T.reshape(QL, NH, DN + DPE)
    perm = np.concatenate([np.arange(DN), DN + pe])
    wqbT = wqbT[:, :, perm]
    wqb_t = np.empty((NH, 128, 12 * 192), np.float32)
    for h in range(NH):
        blk = wqbT[:, h, :].reshape(12, 128, 192)
        wqb_t[h] = blk.transpose(1, 0, 2).reshape(128, 12 * 192)

    wkvaT = wkv_a.T
    perm2 = np.concatenate([np.arange(KVL), KVL + pe])
    wkvaT = wkvaT[:, perm2]
    wkva_t = np.ascontiguousarray(
        wkvaT.reshape(32, 128, KVA).transpose(1, 0, 2).reshape(128, 32 * KVA))

    wkvbT = wkv_b.T.reshape(KVL, NH, DN + DV)
    wkvb_t = np.empty((NH, 128, 4 * 256), np.float32)
    for h in range(NH):
        blk = wkvbT[:, h, :].reshape(4, 128, 256)
        wkvb_t[h] = blk.transpose(1, 0, 2).reshape(128, 4 * 256)

    woT = np.ascontiguousarray(wo.T)
    wo_t = np.ascontiguousarray(
        woT.reshape(32, 128, 8, 512).transpose(2, 0, 1, 3))

    Pv = np.arange(128)
    pi = np.zeros(4096, np.int64)
    for g in range(32):
        pi[g * 128:(g + 1) * 128] = (Pv % 32) * 128 + 4 * g + Pv // 32
    wfT = np.ascontiguousarray(wf.T[pi, :])
    wf_t = np.ascontiguousarray(
        wfT.reshape(32, 128, 8, 512).transpose(2, 0, 1, 3))

    qnw_t = np.ascontiguousarray(q_norm_w.reshape(12, 128).T)
    kvnw_t = np.ascontiguousarray(kv_norm_w.reshape(4, 128).T)

    h128c = _hadamard(128).astype(BF)
    h32 = _hadamard(32).astype(BF)
    h32s = np.ascontiguousarray(np.concatenate([h32] * 4, axis=0))
    idbf = np.eye(128, dtype=np.float32).astype(BF)
    idf = np.eye(128, dtype=np.float32)

    freqs = 1.0 / 10000.0 ** (np.arange(0, DPE, 2, dtype=np.float32) / DPE)
    ang = np.outer(freqs, np.arange(TB, dtype=np.float32))
    cosc = np.ascontiguousarray(np.cos(ang).astype(np.float32))
    sinc = np.ascontiguousarray(np.sin(ang).astype(np.float32))

    in_maps = []
    own_rows_all = []
    for core in range(NCORE):
        b, a = divmod(core, 4)
        own = (128 * (4 * np.arange(8)[:, None] + a) +
               np.arange(128)[None, :]).reshape(-1)
        own_rows_all.append((b, own))
        u = np.arange(512)[None, :]
        p = np.arange(128)[:, None]
        stripc = (u >= p - 128 * a + 384).astype(np.float32).astype(BF)
        m = {
            "x_own": np.ascontiguousarray(x[b][own, :]),
            "x_cov": np.ascontiguousarray(x[b]),
            "wqa_t": wqa_t, "wqb_t": wqb_t, "wkva_t": wkva_t,
            "wkvb_t": wkvb_t, "wo_t": wo_t, "wf_t": wf_t,
            "qnw": qnw_t, "kvnw": kvnw_t,
            "gamma": np.ascontiguousarray(gamma.reshape(1, D)),
            "beta": np.ascontiguousarray(beta.reshape(1, D)),
            "h128": h128c, "h32s": h32s, "ident_bf": idbf, "ident_f32": idf,
            "strip": stripc,
            "cos_own": np.ascontiguousarray(cosc[:, own]),
            "sin_own": np.ascontiguousarray(sinc[:, own]),
            "cos_cov": cosc, "sin_cov": sinc,
        }
        in_maps.append(m)
    return in_maps, own_rows_all


def kernel(**inputs):
    in_maps, own_rows_all = _prep_inputs(inputs)
    nc = build(DEV_TAPS, MAX_PHASE)
    res = run_bass_kernel_spmd(nc, in_maps, core_ids=list(range(NCORE)))
    outp = np.zeros((2, TB, D), np.float32)
    for core in range(NCORE):
        b, own = own_rows_all[core]
        outp[b][own, :] = np.asarray(res.results[core]["out"])
    if DEV_TAPS:
        kernel.last_results = res.results
    return outp
